# version 02
# baseline (speedup 1.0000x reference)
import numpy as np
import concourse.bass as bass
import concourse.mybir as mybir
from concourse.bass_utils import run_bass_kernel_spmd

F32 = mybir.dt.float32
U8 = mybir.dt.uint8
AF = mybir.ActivationFunctionType
ALU = mybir.AluOpType

D = 2048
T = 2048
NCH = 16
H, HKV, DH = 16, 4, 128
ACH = 64
NACH = T // ACH
EPS_LN = 1e-5
NCORES = 8
BPROJ_PARTS = ("kvq", "g", "gate", "beta")
PREP_N = 99
PREP_DMAS = None
BCORE_NCH = 64


class Sem:
    def __init__(self, handle):
        self.h = handle
        self.n = 0

    def inc(self, k=1):
        self.n += k
        return self.n


class Prog:
    ENGS = ("sync", "tensor", "vector", "scalar", "gpsimd")

    def __init__(self, nc):
        self.nc = nc
        self.q = {e: [] for e in self.ENGS}
        self._sems = {}

    def sem(self, name):
        if name not in self._sems:
            h = self.nc.semaphore(name).__enter__()
            self._sems[name] = Sem(h)
        return self._sems[name]

    def emit(self, eng, fn):
        self.q[eng].append(fn)

    def wait(self, eng, sem, val=None):
        v = sem.n if val is None else val
        if v <= 0:
            return
        self.emit(eng, lambda e, sh=sem.h, v=v: e.wait_ge(sh, v))

    def dma(self, eng, out, in_, sem, **kw):
        if sem.n > 0:
            self.wait(eng, sem, sem.n)
        tgt = sem.inc(16)
        self.emit(eng, lambda e, out=out, in_=in_, sh=sem.h, kw=kw:
                  e.dma_start(out=out, in_=in_, **kw).then_inc(sh, 16))
        return tgt

    def op(self, eng, fn, sem=None):
        if sem is None:
            self.emit(eng, fn)
            return None
        tgt = sem.inc(1)
        self.emit(eng, lambda e, fn=fn, sh=sem.h: fn(e).then_inc(sh, 1))
        return tgt

    def spc(self, eng, n=2):
        for _ in range(n):
            if eng == "scalar":
                self.emit(eng, lambda e: e.copy(self.spc_dst, self.spc_src))
            else:
                self.emit(eng, lambda e: e.tensor_copy(self.spc_dst, self.spc_src))

    def barrier(self):
        for name, s in list(self._sems.items()):
            if name.startswith("d_"):
                self.wait("sync", s)
        sems = {e: self.sem(f"bar_{e}") for e in self.ENGS}
        for e in self.ENGS:
            s = sems[e]
            s.inc(1)
            if e == "sync":
                self.emit(e, lambda en, sh=s.h: en.sem_inc(sh, 1))
            else:
                def f(en, sh=s.h):
                    en.drain()
                    en.sem_inc(sh, 1)
                self.emit(e, f)
        for e in self.ENGS:
            for o in self.ENGS:
                if o != e:
                    self.wait(e, sems[o])

    def build(self):
        block = self.nc.Block().__enter__()
        for eng in self.ENGS:
            thunks = self.q[eng]
            if not thunks:
                continue

            def body(e, thunks=thunks):
                for f in thunks:
                    f(e)
            getattr(block, eng)(body)
        block.__exit__(None, None, None)


def make_constants():
    c = {}
    c["ident"] = np.eye(128, dtype=np.float32)
    c["ones_col"] = np.ones((128, 1), np.float32)
    c["ones_row"] = np.ones((1, 128), np.float32)
    prot = np.zeros((128, 128), np.float32)
    for p in range(64):
        prot[p, p + 64] = -1.0
        prot[p + 64, p] = 1.0
    c["protT"] = np.ascontiguousarray(prot.T)
    inv = 1.0 / 10000.0 ** (np.arange(0, DH, 2, dtype=np.float32) / DH)
    fr = np.arange(T, dtype=np.float32)[:, None] * inv[None, :]
    c["rope_cos"] = np.ascontiguousarray(
        np.concatenate([np.cos(fr), np.cos(fr)], -1).T, np.float32)
    c["rope_sin"] = np.ascontiguousarray(
        np.concatenate([np.sin(fr), np.sin(fr)], -1).T, np.float32)
    lam = np.exp2(-8.0 * np.arange(1, H + 1, dtype=np.float32) / H)
    r = np.arange(ACH, dtype=np.float32)
    diff = r[:, None] - r[None, :]
    maskT = np.zeros((H, ACH, ACH), np.float32)
    for h in range(H):
        maskT[h] = np.where(diff >= 0, np.exp(-lam[h] * diff), 0.0).T
    c["a_maskT"] = maskT
    c["a_qdec"] = np.ascontiguousarray(
        np.broadcast_to(np.exp(-lam[:, None] * (r[None, :] + 1.0))[:, None, :],
                        (H, 128, ACH)), np.float32)
    c["a_kdec"] = np.ascontiguousarray(
        np.exp(-lam[:, None] * (ACH - 1.0 - r[None, :])).astype(np.float32).T)
    c["a_blk"] = [float(x) for x in np.exp(-lam * ACH)]
    ex = np.zeros((4, 32, 128), np.float32)
    for j in range(4):
        for t in range(32):
            ex[j, t, 4 * t + j] = 1.0
    c["expand"] = ex
    c["jrow"] = np.broadcast_to((np.arange(128) % 4).astype(np.float32)[None, :],
                                (128, 128)).copy()
    c["divsel"] = np.eye(8, dtype=np.float32)[np.arange(128) // 16]
    c["modsel"] = np.eye(16, dtype=np.float32)[np.arange(128) % 16]
    c["id32flat"] = np.broadcast_to(np.eye(32, dtype=np.float32).reshape(1, 1024),
                                    (128, 1024)).copy()
    c["j4"] = np.eye(4, dtype=np.float32)[np.arange(128) % 4].T.copy()
    c["expT"] = np.eye(32, dtype=np.float32)[np.arange(128) // 4].copy()
    m = np.arange(128)
    c["j3row"] = np.broadcast_to((3.0 * ((m // 4) % 4) + (m % 4)).astype(np.float32)[None, :],
                                 (128, 128)).copy()
    bd = np.zeros((128, 128), np.uint8)
    for b in range(8):
        for jl in range(16):
            bd[16*b+jl, 16*b+jl:16*b+16] = 1
    c["bdmask8"] = bd
    c["bval12"] = np.broadcast_to((12.0 * (np.arange(128) % 8)).astype(np.float32)[None, :],
                                  (128, 128)).copy()
    emT = np.zeros((8, 128, 16), np.float32)
    for B in range(8):
        for jl in range(16):
            emT[B, 16 * B + jl, jl] = 1.0
    c["emb16T"] = emT
    emb = np.zeros((8, 16, 128), np.float32)
    for B in range(8):
        for jl in range(16):
            emb[B, jl, 16 * B + jl] = 1.0
    c["emb16"] = emb
    return c


CONST_SPECS = [("ident", [128, 128]), ("ones_col", [128, 1]), ("ones_row", [1, 128]),
               ("protT", [128, 128]), ("rope_cos", [128, T]), ("rope_sin", [128, T]),
               ("a_maskT", [H, ACH, ACH]), ("a_qdec", [H, 128, ACH]),
               ("a_kdec", [ACH, H]),
               ("expand", [4, 32, 128]), ("jrow", [128, 128]),
               ("divsel", [128, 8]), ("modsel", [128, 16]),
               ("id32flat", [128, 1024]), ("emb16", [8, 16, 128]),
               ("j4", [4, 128]), ("expT", [128, 32]),
               ("j3row", [128, 128]), ("bval12", [128, 128]), ("bdmask8", [128, 128]),
               ("emb16T", [8, 128, 16])]

WEIGHT_SPECS = [
    ("L0_attn_a_wq", [D, D]), ("L0_attn_a_wk", [512, D]), ("L0_attn_a_wv", [512, D]),
    ("L0_attn_a_wo", [D, D]),
    ("L0_attn_b_q_proj_w", [D, D]), ("L0_attn_b_k_proj_w", [4 * D, D]),
    ("L0_attn_b_v_proj_w", [4 * D, D]), ("L0_attn_b_q_conv_w", [D, 4]),
    ("L0_attn_b_k_conv_w", [4 * D, 4]), ("L0_attn_b_v_conv_w", [4 * D, 4]),
    ("L0_attn_b_f1_w", [128, D]), ("L0_attn_b_f2_w", [D, 128]),
    ("L0_attn_b_b_proj_w", [64, D]), ("L0_attn_b_micro_logits", [16, 4]),
    ("L0_attn_b_A_log", [16]), ("L0_attn_b_dt_bias", [D]),
    ("L0_attn_b_g1_w", [128, D]), ("L0_attn_b_g2_w", [D, 128]),
    ("L0_attn_b_g2_b", [D]), ("L0_attn_b_o_norm_w", [128]),
    ("L0_attn_b_o_proj_w", [D, D]),
    ("L0_ffn_up_weight", [4 * D, D]), ("L0_ffn_down_weight", [D, 4 * D]),
    ("L0_ffn_up_scale", [4 * D]), ("L0_ffn_down_scale", [D]),
    ("L0_ffn_output_scale", [1]),
]


class MK:
    def __init__(self, dumps=()):
        nc = bass.Bass("TRN2", target_bir_lowering=False, dynamic_dma_scratch_size=4096)
        self.nc = nc
        self.P = Prog(nc)
        self.consts = make_constants()
        self._guards = []

        self.x_in = nc.dram_tensor("x", [T, D], F32, kind="ExternalInput").ap()
        self.out = nc.dram_tensor("out", [T, D], F32, kind="ExternalOutput").ap()
        self.win = {n: nc.dram_tensor(n, s, F32, kind="ExternalInput").ap()
                    for n, s in WEIGHT_SPECS}
        self.cin = {n: nc.dram_tensor("ci_" + n, s,
                                      mybir.dt.uint8 if n == "bdmask8" else F32,
                                      kind="ExternalInput").ap()
                    for n, s in CONST_SPECS}

        def scr(name, shape):
            return nc.dram_tensor(name, shape, F32, kind="Internal").ap()
        self.scr = scr
        self.r_spill = scr("r_spill", [NCH, 128, T])
        self.r_new = scr("r_new", [NCH, 128, T])
        self.qa = scr("qa", [H, 128, T])
        self.ka = scr("ka", [HKV, 128, T])
        self.va = scr("va", [HKV, 128, T])
        self.oa = scr("oa", [H, 128, T])
        self.hid = scr("hid", [64, 128, T])
        self.ffn_part = scr("ffn_part", [NCH, 128, T])
        self.kb = scr("kb", [H, 4, 128, T])
        self.vb = scr("vb", [H, 4, 128, T])
        self.qb = scr("qb", [H, 128, T])
        self.gb = scr("gb", [H, 128, T])
        self.gateb = scr("gateb", [NCH, 128, H * 128])
        self.betab = scr("betab", [T // 32, 16, 128])
        self.xdiag = scr("xdiag", [T // 32, H, 4, 1024])
        self.xinv = scr("xinv", [T // 32, H, 4, 1024])
        self.ob = scr("ob", [H, 128, T])
        self.dumps = {}
        self.sbuf_dumps = []
        for name in dumps:
            if name == "scl":
                self.dumps["scl"] = nc.dram_tensor("dbg_scl", [128, 256], F32,
                                                   kind="ExternalOutput").ap()
                self.sbuf_dumps.append("scl")
                continue
            if name == "scl_early":
                self.scl_early = nc.dram_tensor("dbg_scl_early", [128, 256], F32,
                                                kind="ExternalOutput").ap()
                self.ex_early = nc.dram_tensor("dbg_ex_early", [128, 512], F32,
                                               kind="ExternalOutput").ap()
                continue
            shp = {"r_spill": [NCH, 128, T], "qa": [H, 128, T], "ka": [HKV, 128, T],
                   "va": [HKV, 128, T], "oa": [H, 128, T], "hid": [64, 128, T],
                   "r_new": [NCH, 128, T], "ffn_part": [NCH, 128, T],
                   "kb": [H, 4, 128, T], "vb": [H, 4, 128, T], "qb": [H, 128, T],
                   "gb": [H, 128, T], "gateb": [NCH, 128, H * 128],
                   "betab": [T // 32, 16, 128], "ob": [H, 128, T],
                   }[name]
            self.dumps[name] = nc.dram_tensor("dbg_" + name, shp, F32,
                                              kind="ExternalOutput").ap()

        def sb(name, shape, dtype=F32):
            g = nc.sbuf_tensor(name, shape, dtype)
            t = g.__enter__()
            self._guards.append(g)
            return t
        self.sb = sb
        self.R = sb("Rres", [128, NCH * T])
        self.cb = {}
        for nm, shp in [("ident", [128, 128]), ("ones_col", [128, 1]),
                        ("ones_row", [1, 128]), ("protT", [128, 128])]:
            self.cb[nm] = sb("c_" + nm, shp)
        self.scl = sb("scl", [128, 1536])
        self.mask8 = sb("mask8", [128, 128], mybir.dt.uint8)
        self.P.spc_src = self.scl[:, 1408:1472]
        self.P.spc_dst = self.scl[:, 1472:1536]
        self.arena = sb("arena", [128, 20480])
        self.wnat = self.arena[:, 0:4096]
        self.wT = self.arena[:, 4096:8192]
        self.stage = self.arena[:, 8192:12288]
        self.extra = self.arena[:, 12288:20480]
        self.ps = []
        for i in range(8):
            g = nc.psum_tensor(f"ps{i}", [128, 512], F32)
            self.ps.append(g.__enter__())
            self._guards.append(g)

    def Rc(self, c):
        return self.R[:, c * T:(c + 1) * T]

    def A(self, off, width):
        return self.arena[:, off:off + width]

    def row(self, i):
        return self.arena[0:1, 8192 + i * T:8192 + (i + 1) * T]


def emit_load_const(K):
    P = K.P
    s = P.sem("d_const")
    for name in ("ident", "ones_col", "ones_row", "protT"):
        P.dma("sync", K.cb[name][:], K.cin[name], s)


def emit_load_x(K):
    P = K.P
    P.wait("tensor", P.sem("d_const"))
    P.wait("vector", P.sem("d_const"))
    s = P.sem("d_x")
    stp = P.sem("t_x")
    svv = P.sem("v_x")
    slots = [K.A(0, 2048), K.A(2048, 2048)]
    s0 = s.n
    for tt in range(16):
        buf = slots[tt % 2]
        if tt >= 2:
            P.wait("sync", stp, (tt - 1) * 16)
        P.dma("sync", buf, K.x_in[tt * 128:(tt + 1) * 128, :], s)
        P.wait("tensor", s, s0 + 16 * (tt + 1))
        for c in range(NCH):
            gi = tt * 16 + c
            bank = K.ps[gi % 2]
            if gi >= 2:
                P.wait("tensor", svv, svv.n - 1)
            P.op("tensor", lambda e, buf=buf, c=c, bank=bank: e.transpose(
                bank[:, 0:128], buf[:, c * 128:(c + 1) * 128], K.cb["ident"][:]), stp)
            P.wait("vector", stp)
            P.op("vector", lambda e, c=c, tt=tt, bank=bank: e.tensor_copy(
                K.Rc(c)[:, tt * 128:(tt + 1) * 128], bank[:, 0:128]), svv)


def emit_ln(K):
    P = K.P
    ssq = P.sem("ln_sq")
    stns = P.sem("ln_tns")
    ssp = P.sem("d_rspill")
    svec = P.sem("ln_vec")
    pb = K.ps[0]
    sp0 = ssp.n
    sq_slots = [K.A(0, 2048), K.A(2048, 2048)]

    for c in range(NCH):
        P.dma("sync", K.r_spill[c], K.Rc(c), ssp)

    stns0 = stns.n
    ssq0 = ssq.n
    for c in range(NCH):
        sq = sq_slots[c % 2]
        if c >= 2:
            P.wait("scalar", stns, stns0 + (c - 1))
        P.op("scalar", lambda e, c=c, sq=sq: e.activation(sq, K.Rc(c), AF.Square), ssq)
        P.wait("tensor", ssq, ssq0 + c + 1)
        for qq in range(4):
            sl = slice(qq * 512, (qq + 1) * 512)
            P.op("tensor", lambda e, c=c, sl=sl, qq=qq: e.matmul(
                K.ps[qq][0:1, :], K.cb["ones_col"][:], K.Rc(c)[:, sl],
                start=(c == 0), stop=(c == NCH - 1)))
            P.op("tensor", lambda e, c=c, sl=sl, sq=sq, qq=qq: e.matmul(
                K.ps[4 + qq][0:1, :], K.cb["ones_col"][:], sq[:, sl],
                start=(c == 0), stop=(c == NCH - 1)), stns if qq == 3 else None)
    smm_t = stns.n

    mean, varpe, rstd, tmp = K.row(0), K.row(1), K.row(2), K.row(3)
    P.wait("vector", stns, smm_t)
    for qq in range(4):
        sl = slice(qq * 512, (qq + 1) * 512)
        P.op("vector", lambda e, qq=qq, sl=sl: e.tensor_scalar_mul(
            mean[:, sl], K.ps[qq][0:1, :], 1.0 / D))
        P.op("vector", lambda e, qq=qq, sl=sl: e.tensor_scalar_mul(
            varpe[:, sl], K.ps[4 + qq][0:1, :], 1.0 / D), svec)
    P.wait("vector", svec)
    P.op("vector", lambda e: e.tensor_mul(tmp, mean, mean), svec)
    P.wait("vector", svec)
    P.op("vector", lambda e: e.tensor_sub(varpe, varpe, tmp), svec)
    P.wait("vector", svec)
    P.op("vector", lambda e: e.tensor_scalar_add(varpe, varpe, EPS_LN), svec)
    P.wait("vector", svec)
    P.op("vector", lambda e: e.reciprocal(rstd, varpe), svec)
    P.wait("scalar", svec)
    P.op("scalar", lambda e: e.activation(rstd, rstd, AF.Sqrt), ssq)
    P.wait("vector", ssq)
    P.op("vector", lambda e: e.tensor_mul(tmp, rstd, rstd), svec)
    P.wait("vector", svec)
    P.op("vector", lambda e: e.tensor_mul(tmp, tmp, varpe), svec)
    P.wait("vector", svec)
    P.op("vector", lambda e: e.tensor_scalar(tmp, tmp, -0.5, 1.5, ALU.mult, ALU.add), svec)
    P.wait("vector", svec)
    P.op("vector", lambda e: e.tensor_mul(rstd, rstd, tmp), svec)
    P.wait("vector", svec)
    P.op("vector", lambda e: e.tensor_mul(tmp, mean, rstd), svec)

    rb0, rb1 = K.A(4096, 2048), K.A(6144, 2048)
    P.wait("tensor", svec)
    for which, row in ((0, rstd), (1, tmp)):
        dst = rb0 if which == 0 else rb1
        for qq in range(4):
            sl = slice(qq * 512, (qq + 1) * 512)
            P.op("tensor", lambda e, row=row, sl=sl: e.matmul(
                pb[:, 0:512], K.cb["ones_row"][0:1, :], row[:, sl],
                start=True, stop=True), stns)
            P.wait("vector", stns)
            P.op("vector", lambda e, dst=dst, sl=sl: e.tensor_copy(
                dst[:, sl], pb[:, 0:512]), svec)
            P.wait("tensor", svec)

    for c in range(NCH):
        P.wait("vector", ssp, sp0 + 16 * (c + 1))
        P.op("vector", lambda e, c=c: e.tensor_mul(K.Rc(c), K.Rc(c), rb0), svec)
        P.wait("vector", svec)
        P.op("vector", lambda e, c=c: e.tensor_sub(K.Rc(c), K.Rc(c), rb1), svec)
    P.barrier()


def emit_proj(K, w_ap, n_oc, evac_fn, din=D, rhs_fn=None, sign=False):
    P = K.P
    ndin = din // 128
    dW, tTR, vWT = P.sem("d_w"), P.sem("t_tr"), P.sem("v_wT")
    tMM, eEV, ssg = P.sem("t_mm"), P.sem("ev_done"), P.sem("s_sign")
    ntt = T // 512
    ngr = (ndin + 3) // 4
    dW0, tTR0, vWT0, tMM0, eEV0 = dW.n, tTR.n, vWT.n, tMM.n, eEV.n
    if rhs_fn is None:
        def rhs_fn(kc, tsl):
            return K.Rc(kc)[:, tsl]

    for oc in range(n_oc):
        slot = oc % 2
        wn = K.wnat[:, slot * 2048: slot * 2048 + din]
        wt = K.wT[:, slot * 2048: slot * 2048 + din]
        if oc >= 2:
            P.wait("sync", tTR, tTR0 + (oc - 1) * ngr)
        P.dma("sync", wn, w_ap[oc * 128:(oc + 1) * 128, :], dW)
        if sign:
            P.wait("scalar", dW, dW0 + 16 * (oc + 1))
            if oc >= 2:
                P.wait("scalar", tTR, tTR0 + (oc - 1) * ngr)
            P.op("scalar", lambda e, wn=wn: e.activation(wn, wn, AF.Sign), ssg)
            P.wait("tensor", ssg)
        else:
            P.wait("tensor", dW, dW0 + 16 * (oc + 1))
        for g in range(ngr):
            bank = K.ps[2 + g % 2]
            gi = oc * ngr + g
            if gi >= 2:
                P.wait("tensor", vWT, vWT0 + gi - 1)
            n_in_g = min(4, ndin - g * 4)
            for j in range(n_in_g):
                kc = g * 4 + j
                P.op("tensor", lambda e, wn=wn, kc=kc, j=j, bank=bank: e.transpose(
                    bank[:, j * 128:(j + 1) * 128],
                    wn[:, kc * 128:(kc + 1) * 128], K.cb["ident"][:]),
                    tTR if j == n_in_g - 1 else None)
            P.wait("vector", tTR, tTR0 + gi + 1)
            if oc >= 2 and g == 0:
                P.wait("vector", tMM, tMM0 + (oc - 1) * ntt)
            P.op("vector", lambda e, wt=wt, g=g, bank=bank, n_in_g=n_in_g: e.tensor_copy(
                wt[:, g * 512:g * 512 + n_in_g * 128], bank[:, 0:n_in_g * 128]), vWT)
        for tt in range(ntt):
            gidx = oc * ntt + tt
            bank = K.ps[gidx % 2]
            tsl = slice(tt * 512, (tt + 1) * 512)
            P.wait("tensor", vWT, vWT0 + (oc + 1) * ngr)
            if gidx >= 2:
                P.wait("tensor", eEV, eEV0 + gidx - 1)
            for kc in range(ndin):
                P.op("tensor", lambda e, wt=wt, kc=kc, bank=bank, tsl=tsl: e.matmul(
                    bank[:], wt[:, kc * 128:(kc + 1) * 128], rhs_fn(kc, tsl),
                    start=(kc == 0), stop=(kc == ndin - 1)),
                    tMM if kc == ndin - 1 else None)
            evac_fn(oc, tt, bank, (tMM, tMM0 + gidx + 1), eEV)


def emit_attna_proj(K):
    P = K.P
    sst = P.sem("d_qkv_st")

    def mk_evac(dst):
        def evac(oc, tt, bank, mmw, done):
            slot = oc % 2
            st = K.stage[:, slot * T + tt * 512: slot * T + (tt + 1) * 512]
            if tt == 0:
                P.wait("vector", P.sem("d_qkv_st"))
            P.wait("vector", *mmw)
            P.op("vector", lambda e, st=st, bank=bank: e.tensor_copy(st, bank[:]), done)
            if tt == 3:
                P.wait("sync", done)
                P.dma("sync", dst[oc], K.stage[:, slot * T:(slot + 1) * T], sst)
        return evac
    emit_proj(K, K.win["L0_attn_a_wq"], 16, mk_evac(K.qa))
    emit_proj(K, K.win["L0_attn_a_wk"], 4, mk_evac(K.ka))
    emit_proj(K, K.win["L0_attn_a_wv"], 4, mk_evac(K.va))
    P.barrier()


def emit_rope(K):
    P = K.P
    sld = P.sem("d_rope")
    stp = P.sem("t_rope")
    svv = P.sem("v_rope")
    sst = P.sem("d_ropest")
    tiles = [(K.qa, oc) for oc in range(16)] + [(K.ka, oc) for oc in range(4)]
    buf_slots = [K.A(0, 2048), K.A(2048, 2048)]
    tmp = K.A(4096, 2048)
    cosb, sinb = K.A(6144, 2048), K.A(8192, 2048)
    P.dma("sync", cosb, K.cin["rope_cos"], sld)
    P.dma("sync", sinb, K.cin["rope_sin"], sld)
    ld0 = sld.n
    for i, (scr, oc) in enumerate(tiles):
        buf = buf_slots[i % 2]
        if i >= 2:
            P.wait("sync", sst, 16 * (i - 1))
        P.dma("sync", buf, scr[oc], sld)
        P.wait("tensor", sld, ld0 + 16 * (i + 1))
        for qq in range(4):
            bank = K.ps[qq % 2]
            sl = slice(qq * 512, (qq + 1) * 512)
            if i > 0 or qq >= 2:
                P.wait("tensor", svv, max(0, svv.n - 1))
            P.op("tensor", lambda e, bank=bank, buf=buf, sl=sl: e.matmul(
                bank[:], K.cb["protT"][:], buf[:, sl], start=True, stop=True), stp)
            P.wait("vector", stp)
            P.op("vector", lambda e, bank=bank, sl=sl: e.tensor_mul(
                tmp[:, sl], bank[:], sinb[:, sl]), svv)
        P.wait("vector", svv)
        P.op("vector", lambda e, buf=buf: e.tensor_mul(buf, buf, cosb), svv)
        P.wait("vector", svv)
        P.op("vector", lambda e, buf=buf: e.tensor_add(buf, buf, tmp), svv)
        P.wait("sync", svv)
        P.dma("sync", scr[oc], buf, sst)
    P.barrier()


def emit_attna_core(K):
    P = K.P
    kv = K.A(0, 2048)
    qf_slots = [K.A(2048, 1024), K.A(3072, 1024)]
    kvf_slots = [K.A(4096, 512), K.A(4608, 512)]
    ktok = K.A(5120, 512)
    vtok = K.A(5632, 512)
    ks = K.A(6144, 128)
    qdq = K.A(6272, 64)
    scT = K.A(6336, 64)
    ost_slots = [K.A(6656, 1024), K.A(7680, 1024)]
    amaskT = K.A(8704, 1024)
    aqdec = K.A(9728, 1024)
    akdec = K.A(10752, 16)

    sld = P.sem("d_acore")
    stp = P.sem("t_acore")
    svv = P.sem("v_acore")
    ssc = P.sem("s_acore")
    sst = P.sem("d_aost")
    P.op("vector", lambda e: e.memset(kv[:], 0.0), svv)
    P.dma("sync", amaskT[0:ACH, :], K.cin["a_maskT"].rearrange("h j i -> j h i"), sld)
    P.dma("sync", aqdec, K.cin["a_qdec"].rearrange("h p i -> p h i"), sld)
    P.dma("sync", akdec[0:ACH, :], K.cin["a_kdec"], sld)
    ld0 = sld.n
    P.wait("vector", sld, ld0)
    P.wait("gpsimd", sld, ld0)

    def load_chunk(c):
        tsl = slice(c * ACH, (c + 1) * ACH)
        if c >= 2:
            P.wait("sync", ssc, 16 * (c - 1))
        P.dma("sync", qf_slots[c % 2][:, 0:16 * ACH],
              K.qa.rearrange("h p t -> p h t")[:, :, tsl], sld)
        P.dma("sync", kvf_slots[c % 2][:, 0:4 * ACH],
              K.ka.rearrange("h p t -> p h t")[:, :, tsl], sld)
        P.dma("sync", kvf_slots[c % 2][:, 4 * ACH:8 * ACH],
              K.va.rearrange("h p t -> p h t")[:, :, tsl], sld)

    load_chunk(0)
    for c in range(NACH):
        if c + 1 < NACH:
            load_chunk(c + 1)
        qfc, kvc = qf_slots[c % 2], kvf_slots[c % 2]
        P.wait("tensor", sld, ld0 + 48 * (c + 1))
        if c >= 2:
            P.wait("scalar", sst, 16 * (c - 1))
        for kh in range(HKV):
            P.op("tensor", lambda e, kh=kh, kvc=kvc: e.transpose(
                K.ps[0][0:64, kh * 128:(kh + 1) * 128],
                kvc[:, kh * 64:(kh + 1) * 64], K.cb["ident"][:]))
            P.op("tensor", lambda e, kh=kh, kvc=kvc: e.transpose(
                K.ps[1][0:64, kh * 128:(kh + 1) * 128],
                kvc[:, 4 * 64 + kh * 64:4 * 64 + (kh + 1) * 64], K.cb["ident"][:]),
                stp if kh == HKV - 1 else None)
        P.wait("vector", stp)
        P.wait("vector", svv)
        P.op("vector", lambda e: e.tensor_copy(ktok[0:64, 0:512], K.ps[0][0:64, 0:512]), svv)
        P.wait("vector", svv)
        P.op("vector", lambda e: e.tensor_copy(vtok[0:64, 0:512], K.ps[1][0:64, 0:512]), svv)
        P.wait("tensor", svv)
        for h in range(H):
            kh = h // 4
            kfc = kvc[:, kh * 64:(kh + 1) * 64]
            qfh = qfc[:, h * 64:(h + 1) * 64]
            ktk = ktok[0:64, kh * 128:(kh + 1) * 128]
            vtk = vtok[0:64, kh * 128:(kh + 1) * 128]
            kvh = kv[:, h * 128:(h + 1) * 128]
            b_sc = K.ps[2 + h % 2]
            b_o = K.ps[4 + h % 2]
            b_kv = K.ps[6 + h % 2]
            P.wait("tensor", svv)
            P.op("tensor", lambda e, kfc=kfc, qfh=qfh, b_sc=b_sc: e.matmul(
                b_sc[0:64, 0:64], kfc, qfh, start=True, stop=True), stp)
            P.op("vector", lambda e, ktk=ktk, h=h: e.tensor_scalar(
                ks[0:64, :], ktk, akdec[0:64, h:h + 1], None, ALU.mult), svv)
            P.wait("gpsimd", stp)
            P.op("gpsimd", lambda e, qfh=qfh, h=h: e.tensor_mul(
                qdq, qfh, aqdec[:, h * 64:(h + 1) * 64]), P.sem("g_acore"))
            P.wait("vector", stp)
            P.op("vector", lambda e, b_sc=b_sc, h=h: e.tensor_mul(
                scT[0:64, :], b_sc[0:64, 0:64],
                amaskT[0:64, h * 64:(h + 1) * 64]), svv)
            P.wait("tensor", svv)
            P.wait("tensor", P.sem("g_acore"))
            P.wait("tensor", ssc)
            P.op("tensor", lambda e, vtk=vtk, b_o=b_o: e.matmul(
                b_o[:, 0:64], vtk, scT[0:64, :], start=True, stop=False))
            P.op("tensor", lambda e, kvh=kvh, b_o=b_o: e.matmul(
                b_o[:, 0:64], kvh, qdq, start=False, stop=True), stp)
            P.op("tensor", lambda e, vtk=vtk, b_kv=b_kv: e.matmul(
                b_kv[:, 0:128], ks[0:64, :], vtk, start=True, stop=True), stp)
            P.wait("vector", stp)
            blk = K.consts["a_blk"][h]
            P.op("vector", lambda e, kvh=kvh, b_kv=b_kv, blk=blk: e.scalar_tensor_tensor(
                kvh, kvh, blk, b_kv[:, 0:128], ALU.mult, ALU.add), svv)
            P.wait("scalar", stp)
            P.op("scalar", lambda e, b_o=b_o, h=h, c=c: e.copy(
                ost_slots[c % 2][:, h * 64:(h + 1) * 64], b_o[:, 0:64]), ssc)
        P.wait("sync", ssc)
        P.dma("sync", K.oa.rearrange("h p t -> p h t")[:, :, c * ACH:(c + 1) * ACH],
              ost_slots[c % 2], sst)
    P.barrier()


def emit_out_proj(K, w_name, src_scr, n_in_ch, scale_fn=None):
    P = K.P
    sld = P.sem("d_oproj")
    srn = P.sem("d_rnew")
    srold = P.sem("d_rold")
    svv = P.sem("v_oproj")
    ld0 = sld.n
    for c in range(n_in_ch):
        P.dma("sync", K.Rc(c), src_scr[c], sld)
    P.wait("tensor", sld, ld0 + 16 * n_in_ch)
    P.wait("vector", sld, ld0 + 16 * n_in_ch)
    rold_slots = [K.extra[:, 0:512], K.extra[:, 512:1024]]

    def evac(oc, tt, bank, mmw, done):
        slot = oc % 2
        st = K.stage[:, slot * T + tt * 512: slot * T + (tt + 1) * 512]
        gidx = oc * 4 + tt
        rb = rold_slots[gidx % 2][:, 0:512]
        P.wait("sync", done, max(0, done.n - 1))
        P.dma("sync", rb, K.r_spill[oc][:, tt * 512:(tt + 1) * 512], srold)
        if tt == 0:
            P.wait("vector", P.sem("d_rnew"))
        P.wait("vector", srold)
        P.wait("vector", *mmw)
        if scale_fn is None:
            P.op("vector", lambda e, st=st, bank=bank, rb=rb: e.tensor_add(
                st, bank[:], rb), done)
        else:
            sc = scale_fn(oc)
            P.op("vector", lambda e, st=st, bank=bank, sc=sc: e.tensor_scalar(
                st, bank[:], sc, None, ALU.mult))
            P.op("vector", lambda e, st=st, rb=rb: e.tensor_add(st, st, rb), done)
        if tt == 3:
            P.wait("sync", done)
            P.dma("sync", K.r_new[oc], K.stage[:, slot * T:(slot + 1) * T], srn)

    emit_proj(K, K.win[w_name], NCH, evac, din=n_in_ch * 128)
    P.barrier()
    s2 = P.sem("d_rback")
    for c in range(NCH):
        P.dma("sync", K.Rc(c), K.r_new[c], s2)
    P.barrier()


def emit_final_store(K):
    P = K.P
    stp = P.sem("t_fin")
    svv = P.sem("v_fin")
    sst = P.sem("d_fin")
    for c in range(NCH):
        for tt in range(NCH):
            bank = K.ps[(c * NCH + tt) % 2]
            gi = c * NCH + tt
            if gi >= 2:
                P.wait("tensor", svv, svv.n - 1)
            P.op("tensor", lambda e, c=c, tt=tt, bank=bank: e.transpose(
                bank[:, 0:128], K.Rc(c)[:, tt * 128:(tt + 1) * 128],
                K.cb["ident"][:]), stp)
            P.wait("vector", stp)
            st = K.stage[:, (gi % 8) * 128:(gi % 8 + 1) * 128]
            if gi >= 8:
                P.wait("vector", sst, 16 * (gi - 7))
            P.op("vector", lambda e, st=st, bank=bank: e.tensor_copy(
                st, bank[:, 0:128]), svv)
            P.wait("sync", svv)
            P.dma("sync", K.out[tt * 128:(tt + 1) * 128, c * 128:(c + 1) * 128],
                  st, sst)
    P.wait("sync", sst)


def emit_dumps(K):
    P = K.P
    s = P.sem("d_dump")
    for name, dst in K.dumps.items():
        if name in K.sbuf_dumps:
            P.dma("sync", dst, getattr(K, name)[:], s)
        else:
            P.dma("sync", dst, getattr(K, name), s)
    P.wait("sync", s)


def emit_ffn(K):
    P = K.P
    sld = P.sem("d_ffs")
    stp = P.sem("t_ffs")
    svv = P.sem("v_ffs")
    ld0 = sld.n
    ex = K.extra
    svv0 = P.sem("v_ffs")
    P.op("vector", lambda e: e.memset(ex[:, 0:256], 0.0), svv0)
    P.wait("sync", svv0)
    P.dma("sync", ex[0:64, 0:128], K.win["L0_ffn_up_scale"].rearrange("(a b) -> a b", b=128), sld)
    P.dma("sync", ex[0:16, 128:256], K.win["L0_ffn_down_scale"].rearrange("(a b) -> a b", b=128), sld)
    P.dma("sync", ex[0:1, 384:385], K.win["L0_ffn_output_scale"].unsqueeze(0), sld)
    P.wait("vector", sld, ld0 + 48)
    P.op("vector", lambda e: e.tensor_copy(ex[0:1, 256:257], ex[0:1, 384:385]), svv)
    P.wait("tensor", svv)
    P.op("tensor", lambda e: e.matmul(K.ps[6][:, 0:128], ex[:, 128:256],
                                      K.cb["ident"][:], start=True, stop=True))
    P.op("tensor", lambda e: e.matmul(K.ps[5][:, 0:128], ex[:, 0:128],
                                      K.cb["ident"][:], start=True, stop=True), stp)
    P.op("tensor", lambda e: e.matmul(K.ps[7][:, 0:16], K.cb["ones_row"][0:1, :],
                                      ex[0:1, 256:257].broadcast_to([1, 16]),
                                      start=True, stop=True), stp)
    P.wait("vector", stp)
    P.wait("vector", svv)
    P.op("vector", lambda e: e.tensor_copy(K.scl[:, 96:97], K.ps[7][:, 0:1]), svv)
    P.op("vector", lambda e: e.tensor_copy(K.scl[:, 0:64], K.ps[5][:, 0:64]), svv)
    P.wait("vector", svv)
    P.op("vector", lambda e: e.tensor_scalar(K.scl[:, 64:80], K.ps[6][:, 0:16],
                                             K.scl[:, 96:97], None, ALU.mult), svv)
    P.barrier()
    if hasattr(K, "scl_early"):
        P.dma("sync", K.scl_early, K.scl[:], P.sem("d_dump"))
        P.dma("sync", K.ex_early, ex[:, 0:512], P.sem("d_dump"))
        P.barrier()

    shst = P.sem("d_hidst")

    def up_evac(oc, tt, bank, mmw, done):
        slot = oc % 2
        st = K.stage[:, slot * T + tt * 512: slot * T + (tt + 1) * 512]
        if tt == 0:
            P.wait("vector", P.sem("d_hidst"))
        P.wait("vector", *mmw)
        P.op("vector", lambda e, st=st, bank=bank, oc=oc: e.tensor_scalar(
            st, bank[:], K.scl[:, oc:oc + 1], 0.0, ALU.mult, ALU.max), done)
        if tt == 3:
            P.wait("sync", done)
            P.dma("sync", K.hid[oc], K.stage[:, slot * T:(slot + 1) * T], shst)

    emit_proj(K, K.win["L0_ffn_up_weight"], 64, up_evac, sign=True)
    P.barrier()

    dW, tTR, vWT = P.sem("d_w"), P.sem("t_tr"), P.sem("v_wT")
    tMM, eEV, ssg = P.sem("t_mm"), P.sem("ev_done"), P.sem("s_sign")
    sprt = P.sem("d_part")
    shl = P.sem("d_hidld")
    srn = P.sem("d_rnew2")
    srold = P.sem("d_rold2")
    rold_slots = [K.extra[:, 0:1024], K.extra[:, 1024:2048]]
    part_slots = [K.extra[:, 2048:2560], K.extra[:, 2560:3072]]
    for q in range(4):
        hl0 = shl.n
        tMM0q = tMM.n
        P.barrier()
        for c in range(16):
            P.dma("sync", K.Rc(c), K.hid[q * 16 + c], shl)
        P.wait("tensor", shl, hl0 + 256)
        P.wait("vector", shl, hl0 + 256)
        for oc in range(NCH):
            slot = oc % 2
            wn = K.wnat[:, slot * 2048: slot * 2048 + 2048]
            wt = K.wT[:, slot * 2048: slot * 2048 + 2048]
            if oc >= 2:
                P.wait("sync", tTR, tTR.n - 4)
            P.dma("sync", wn, K.win["L0_ffn_down_weight"][oc * 128:(oc + 1) * 128,
                                                          q * 2048:(q + 1) * 2048], dW)
            P.wait("scalar", dW)
            if oc >= 2:
                P.wait("scalar", tTR, tTR.n - 4)
            P.op("scalar", lambda e, wn=wn: e.activation(wn, wn, AF.Sign), ssg)
            P.wait("tensor", ssg)
            for g in range(4):
                bank = K.ps[2 + g % 2]
                if oc * 4 + g >= 2:
                    P.wait("tensor", vWT, vWT.n - 1)
                for j in range(4):
                    kc = g * 4 + j
                    P.op("tensor", lambda e, wn=wn, kc=kc, j=j, bank=bank: e.transpose(
                        bank[:, j * 128:(j + 1) * 128],
                        wn[:, kc * 128:(kc + 1) * 128], K.cb["ident"][:]),
                        tTR if j == 3 else None)
                P.wait("vector", tTR)
                if g == 0 and oc >= 2:
                    P.wait("vector", tMM, tMM0q + (oc - 1) * 4)
                P.op("vector", lambda e, wt=wt, g=g, bank=bank: e.tensor_copy(
                    wt[:, g * 512:(g + 1) * 512], bank[:, 0:512]), vWT)
            for tt in range(4):
                gidx = oc * 4 + tt
                bank = K.ps[gidx % 2]
                tsl = slice(tt * 512, (tt + 1) * 512)
                P.wait("tensor", vWT)
                if gidx >= 2:
                    P.wait("tensor", eEV, eEV.n - 1)
                for kc in range(16):
                    P.op("tensor", lambda e, wt=wt, kc=kc, bank=bank, tsl=tsl: e.matmul(
                        bank[:], wt[:, kc * 128:(kc + 1) * 128], K.Rc(kc)[:, tsl],
                        start=(kc == 0), stop=(kc == 15)), tMM if kc == 15 else None)
                slot2 = gidx % 2
                st = K.stage[:, (oc % 2) * T + tt * 512: (oc % 2) * T + (tt + 1) * 512]
                if tt == 0:
                    P.wait("vector", sprt)
                    P.wait("vector", srn)
                if q == 0:
                    P.wait("vector", tMM)
                    P.op("vector", lambda e, st=st, bank=bank: e.tensor_copy(st, bank[:]), eEV)
                else:
                    pb = part_slots[slot2][:, 0:512]
                    P.wait("sync", eEV, max(0, eEV.n - 1))
                    P.dma("sync", pb, K.ffn_part[oc][:, tsl], sprt)
                    P.wait("vector", sprt)
                    P.wait("vector", tMM)
                    if q < 3:
                        P.op("vector", lambda e, st=st, bank=bank, pb=pb: e.tensor_add(
                            st, bank[:], pb), eEV)
                    else:
                        rb = rold_slots[slot2][:, 512:1024]
                        P.wait("sync", eEV, max(0, eEV.n - 1))
                        P.dma("sync", rb, K.r_spill[oc][:, tsl], srold)
                        P.wait("vector", srold)
                        P.op("vector", lambda e, st=st, bank=bank, pb=pb: e.tensor_add(
                            st, bank[:], pb), eEV)
                        P.wait("vector", eEV)
                        P.op("vector", lambda e, st=st, oc=oc: e.tensor_scalar(
                            st, st, K.scl[:, 64 + oc:64 + oc + 1], None, ALU.mult), eEV)
                        P.wait("vector", eEV)
                        P.op("vector", lambda e, st=st, rb=rb: e.tensor_add(st, st, rb), eEV)
                if tt == 3:
                    P.wait("sync", eEV)
                    dst = K.ffn_part[oc] if q < 3 else K.r_new[oc]
                    P.dma("sync", dst, K.stage[:, (oc % 2) * T:(oc % 2 + 1) * T],
                          srn if q == 3 else sprt)
    P.barrier()
    s2 = P.sem("d_rback2")
    for c in range(NCH):
        P.dma("sync", K.Rc(c), K.r_new[c], s2)
    P.barrier()




def emit_attnb_prep(K):
    P = K.P
    sld = P.sem("d_bprep")
    stp = P.sem("t_bprep")
    svv = P.sem("v_bprep")
    ssc = P.sem("s_bprep")
    sgp = P.sem("g_bprep")
    ex = K.extra
    P.op("vector", lambda e: e.memset(ex[:, 0:768], 0.0), svv)
    P.wait("sync", svv)
    _dmas = [
        lambda: P.dma("sync", ex[0:16, 0:128],
                      K.win["L0_attn_b_dt_bias"].rearrange("(a b) -> a b", b=128), sld),
        lambda: P.dma("sync", ex[0:1, 256:272], K.win["L0_attn_b_A_log"].unsqueeze(0), sld),
        lambda: P.dma("sync", ex[0:16, 384:388], K.win["L0_attn_b_micro_logits"], sld),
        lambda: P.dma("sync", ex[0:1, 388:516], K.win["L0_attn_b_o_norm_w"].unsqueeze(0), sld),
        lambda: P.dma("sync", K.scl[:, 208:272],
                      K.win["L0_attn_b_q_conv_w"].rearrange("(c p) k -> p c k", p=128), sld),
        lambda: P.dma("sync", K.scl[:, 272:528],
                      K.win["L0_attn_b_k_conv_w"].rearrange("(c p) k -> p c k", p=128), sld),
        lambda: P.dma("sync", K.scl[:, 528:784],
                      K.win["L0_attn_b_v_conv_w"].rearrange("(c p) k -> p c k", p=128), sld),
        lambda: P.dma("sync", ex[0:4, 640:768], K.cin["j4"], sld),
        lambda: P.dma("sync", ex[:, 768:800], K.cin["expT"], sld),
    ]
    for _i, _f in enumerate(_dmas):
        if PREP_DMAS is None or _i in PREP_DMAS:
            _f()
    ld_t = sld.n
    if PREP_N == 1:
        P.barrier(); return

    P.wait("tensor", sld, ld_t)
    P.op("tensor", lambda e: e.matmul(K.ps[0][:, 0:128], ex[:, 0:128], K.cb["ident"][:],
                                      start=True, stop=True), stp)
    P.wait("scalar", sld, ld_t)
    P.op("scalar", lambda e: e.activation(ex[0:1, 272:288], ex[0:1, 256:272], AF.Exp), ssc)
    P.wait("gpsimd", ssc)
    P.op("gpsimd", lambda e: e.tensor_scalar_mul(ex[0:1, 288:304], ex[0:1, 272:288], -1.0), sgp)
    P.wait("tensor", sgp)
    P.op("tensor", lambda e: e.matmul(K.ps[1][:, 0:16], K.cb["ones_row"][0:1, :],
                                      ex[0:1, 288:304], start=True, stop=True))
    P.op("tensor", lambda e: e.matmul(K.ps[2][:, 0:128], K.cb["ones_row"][0:1, :],
                                      ex[0:1, 388:516], start=True, stop=True), stp)
    P.wait("vector", stp)
    P.op("vector", lambda e: e.tensor_copy(K.scl[:, 128:144], K.ps[0][:, 0:16]))
    P.op("vector", lambda e: e.tensor_copy(K.scl[:, 160:176], K.ps[1][:, 0:16]))
    P.op("vector", lambda e: e.tensor_copy(K.scl[:, 784:912], K.ps[2][:, 0:128]), svv)
    if PREP_N == 2:
        P.barrier(); return

    t0 = ex[:, 544:560]
    t1 = ex[:, 560:576]
    P.wait("gpsimd", svv)
    P.op("gpsimd", lambda e: e.tensor_scalar_add(t0, K.scl[:, 128:144], -10000.0), sgp)
    P.wait("scalar", sgp)
    P.op("scalar", lambda e: e.activation(t1, t0, AF.Abs), ssc)
    P.wait("scalar", ssc)
    P.op("scalar", lambda e: e.activation(t1, t1, AF.Exp, scale=-1.0), ssc)
    P.wait("scalar", ssc)
    P.op("scalar", lambda e: e.activation(t1, t1, AF.Ln, bias=1.0), ssc)
    P.wait("gpsimd", ssc)
    P.op("gpsimd", lambda e: e.tensor_scalar_max(t0, t0, 0.0), sgp)
    P.wait("vector", sgp)
    P.op("vector", lambda e: e.tensor_add(t1, t1, t0), svv)
    P.wait("vector", svv)
    P.op("vector", lambda e: e.tensor_mul(K.scl[:, 144:160], t1, K.scl[:, 160:176]), svv)

    if PREP_N == 3:
        P.barrier(); return
    w = ex[0:16, 384:388]
    wm = ex[0:16, 576:580]
    rmax = ex[0:16, 580:581]
    rsum = ex[0:16, 581:582]
    P.op("vector", lambda e: e.reduce_max(rmax, w, mybir.AxisListType.X), svv)
    P.wait("gpsimd", svv)
    P.op("gpsimd", lambda e: e.tensor_scalar_mul(rmax, rmax, -1.0), sgp)
    P.wait("scalar", sgp)
    P.op("scalar", lambda e: e.activation(wm, w, AF.Exp, bias=rmax), ssc)
    if PREP_N == 4:
        P.barrier(); return
    P.wait("vector", ssc)
    P.op("vector", lambda e: e.reduce_sum(rsum, wm, mybir.AxisListType.X), svv)
    P.wait("vector", svv)
    P.op("vector", lambda e: e.reciprocal(rmax, rsum), svv)
    P.wait("vector", svv)
    P.op("vector", lambda e: e.tensor_scalar(wm, wm, rmax, None, ALU.mult), svv)
    P.wait("vector", svv)
    P.op("vector", lambda e: e.tensor_copy(ex[0:16, 592:596], wm), svv)
    if PREP_N == 5:
        P.barrier(); return
    P.wait("tensor", svv)
    P.op("tensor", lambda e: e.matmul(K.ps[3][0:4, 0:128], ex[:, 592:596], K.cb["ident"][:],
                                      start=True, stop=True), stp)
    P.wait("vector", stp)
    P.op("vector", lambda e: e.tensor_copy(ex[0:4, 608:624], K.ps[3][0:4, 0:16]), svv)
    P.wait("tensor", svv)
    P.op("tensor", lambda e: e.matmul(K.ps[4][:, 0:16], ex[0:4, 640:768], ex[0:4, 608:624],
                                      start=True, stop=True), stp)
    P.wait("vector", stp)
    P.op("vector", lambda e: e.tensor_copy(K.scl[:, 932:948], K.ps[4][:, 0:16]), svv)
    P.wait("vector", svv)
    for h in range(H):
        P.op("vector", lambda e, h=h: e.tensor_scalar(
            K.scl[:, 960 + h * 32:960 + (h + 1) * 32], ex[:, 768:800],
            K.scl[:, 932 + h:933 + h], None, ALU.mult), svv)
    P.barrier()


RS_, RV_, RAN_, RAQ_, RBB_ = 0, 2048, 4096, 6144, 8192
RMT_, RMQ_, RK_, RZ_ = 10240, 12288, 14336, 16384
RCSK_, RCSQ_, RK40_, RQ40_ = 18432, 20480, 22528, 24576
RPRK_, RPBK_, RE_ = 26624, 28672, 30720
ACUMG, AG_, AQT = 0, 512, 1024
AJLF, AID32, ALB, AXT = 2048, 4096, 5120, 6144
AXTA = 7168
AJ3, ABV12, ALFLR = 7680, 7808, 7936
ALF12, ALF96 = 8064, 8080
ABSX, ACND, ABER, AEBER, ALGL = 8096, 8224, 8352, 8480, 8608
AELGB, ANEGB, AELGL = 8736, 8864, 8992
ABB, ANBB, ABLD, ANBB2 = 9008, 9024, 9040, 9168
AGRAM, AMRAW, ABMAT = 9200, 9456, 9712
AW_, AO_, ABBAR, ARP = 9840, 9968, 10096, 10224
AOBST, ATMP = 10352, 10864


def emit_attnb_core(K):
    P = K.P
    H_ = 16
    sld = P.sem("d_bcl")
    sgt = P.sem("d_bgt")
    sxs = [P.sem(f"d_bxs{i}") for i in range(4)]
    sxb = P.sem("d_bxb")
    sxv = P.sem("d_bxv")
    sxl = P.sem("d_bxl")
    sob = P.sem("d_bob")
    stc = P.sem("t_bc")
    svc = P.sem("v_bc")
    ssc = P.sem("s_bc")
    sgc = P.sem("g_bc")
    QS = {"tensor": stc, "vector": svc, "scalar": ssc, "gpsimd": sgc}

    def OP(eng, fn, sem=None):
        P.wait(eng, QS[eng])
        P.op(eng, fn, sem if sem is not None else QS[eng])

    def RR(off, w, p0=0, p1=128):
        return K.R[p0:p1, off:off + w]

    RFULL = K.R[:, 0:NCH * T].rearrange("p (a b) -> p a b", b=2048)
    kmic = RR(RK_, 2048).rearrange("p (h r t) -> p h t r", r=4, t=32)
    vmicv = RR(RV_, 2048).rearrange("p (h r t) -> p h t r", r=4, t=32)
    qrep = K.A(AQT, 512).rearrange("p (h t) -> p h t", t=32) \
        .unsqueeze(3).broadcast_to([128, 16, 32, 4])
    cumg3 = K.A(ACUMG, 512).rearrange("p (h t) -> p h t", t=32)
    cumg_rep = cumg3.unsqueeze(3).broadcast_to([128, 16, 32, 4])
    cumg4 = K.A(ACUMG, 512).rearrange("p (h b f) -> p h b f", b=8, f=4)

    def rep8(off):
        return K.A(off, 128).rearrange("p (h b) -> p h b", b=8) \
            .unsqueeze(3).broadcast_to([128, 16, 8, 16])

    def rep16(off, n):
        return K.A(off, 16).unsqueeze(2).broadcast_to([128, 16, n])

    def whtj(off):
        return RR(off, 2048).rearrange("p (h t j) -> p h t j", t=32, j=4)

    def whbm(off):
        return RR(off, 2048).rearrange("p (h b m) -> p h b m", b=8, m=16)

    NBC = BCORE_NCH

    pl0 = sld.n
    P.dma("sync", K.A(AJ3, 128), K.cin["j3row"], sld)
    P.dma("sync", K.A(ABV12, 128), K.cin["bval12"], sld)
    P.dma("sync", K.A(AID32, 1024), K.cin["id32flat"], sld)
    P.dma("sync", K.mask8[:], K.cin["bdmask8"], sld)
    P.wait("gpsimd", sld, pl0 + 64)
    OP("gpsimd", lambda e: e.tensor_scalar_mul(K.A(ALF96, 16), K.scl[:, 144:160], 96.0), sgc)
    OP("gpsimd", lambda e: e.tensor_scalar_mul(K.A(ALF12, 16), K.scl[:, 144:160], 12.0), sgc)
    P.wait("vector", sld, pl0 + 64)
    lfrep128 = K.scl[:, 144:160].unsqueeze(2).broadcast_to([128, 16, 128])
    lfrep8 = K.scl[:, 144:160].unsqueeze(2).broadcast_to([128, 16, 8])
    j3rep = K.A(AJ3, 128).unsqueeze(1).broadcast_to([128, 16, 128])
    OP("vector", lambda e: e.tensor_mul(
        K.A(AJLF, 2048).rearrange("p (h m) -> p h m", m=128), j3rep, lfrep128), svc)
    OP("vector", lambda e: e.tensor_mul(
        K.A(ABV12, 128).rearrange("p (h b) -> p h b", b=8),
        K.A(ABV12, 128).rearrange("p (h b) -> p h b", b=8), lfrep8), svc)
    P.wait("vector", sgc)
    P.wait("vector", svc)
    OP("vector", lambda e: e.tensor_sub(
        K.A(ALFLR, 128).rearrange("p (h b) -> p h b", b=8),
        K.A(ALF96, 16).unsqueeze(2).broadcast_to([128, 16, 8]),
        K.A(ABV12, 128).rearrange("p (h b) -> p h b", b=8)), svc)
    OP("vector", lambda e: e.memset(K.A(ATMP + 144, 1), 40.0), svc)
    OP("vector", lambda e: e.memset(K.A(ATMP + 145, 1), -40.0), svc)
    OP("vector", lambda e: e.memset(RR(RS_, 2048), 0.0), svc)
    OP("vector", lambda e: e.memset(RR(RMQ_, 2048), 0.0), svc)
    OP("vector", lambda e: e.memset(K.A(AMRAW, 256), 0.0), svc)

    def load_chunk(c):
        mk = {}
        for hg in range(4):
            P.dma("sync", RR(RK_ + hg * 512, 512),
                  K.kb[4 * hg:4 * hg + 4, :, :, 32 * c:32 * c + 32]
                  .rearrange("h r p t -> p h r t"), sld)
        mk["k"] = sld.n
        for hg in range(4):
            P.dma("sync", RR(RV_ + hg * 512, 512),
                  K.vb[4 * hg:4 * hg + 4, :, :, 32 * c:32 * c + 32]
                  .rearrange("h r p t -> p h r t"), sld)
        mk["v"] = sld.n
        P.dma("sync", K.A(AQT, 512),
              K.qb[:, :, 32 * c:32 * c + 32].rearrange("h p t -> p h t"), sld)
        mk["q"] = sld.n
        P.dma("sync", K.A(AG_, 512),
              K.gb[:, :, 32 * c:32 * c + 32].rearrange("h p t -> p h t"), sld)
        mk["g"] = sld.n
        P.dma("sync", K.arena[0:16, ABLD:ABLD + 128], K.betab[c], sld)
        mk["b"] = sld.n
        return mk

    lmark = load_chunk(0)

    for c in range(NBC):
        if c > 0:
            P.barrier()
        P.wait("vector", sld, lmark["g"])
        for h in range(H_):
            OP("vector", lambda e, h=h: e.tensor_tensor_scan(
                K.A(ACUMG + 32 * h, 32), K.A(AG_ + 32 * h, 32),
                K.A(AG_ + 32 * h, 32), 0.0, ALU.add, ALU.bypass), svc)
        OP("gpsimd", lambda e: e.memset(K.A(ABSX, 128), 0.0), sgc)
        P.wait("gpsimd", svc)
        OP("gpsimd", lambda e: e.tensor_copy(
            K.A(ABSX, 128).rearrange("p (h b) -> p h b", b=8)[:, :, 1:8],
            cumg4[:, :, 0:7, 3]), sgc)
        OP("gpsimd", lambda e: e.tensor_copy(
            K.A(ACND, 128).rearrange("p (h b) -> p h b", b=8), cumg4[:, :, :, 3]), sgc)
        P.wait("vector", sgc)
        OP("vector", lambda e: e.tensor_sub(K.A(ABER, 128), K.A(ACND, 128),
                                              K.A(ABSX, 128)), svc)
        P.wait("vector", svc)
        OP("vector", lambda e: e.tensor_add(
            K.A(ABER, 128).rearrange("p (h b) -> p h b", b=8),
            K.A(ABER, 128).rearrange("p (h b) -> p h b", b=8),
            K.A(ALF12, 16).unsqueeze(2).broadcast_to([128, 16, 8])), svc)
        P.wait("scalar", svc)
        OP("scalar", lambda e: e.activation(K.A(AEBER, 128), K.A(ABER, 128), AF.Exp), ssc)
        OP("vector", lambda e: e.tensor_add(K.A(ATMP, 128), K.A(ABSX, 128),
                                              K.A(ABV12, 128)), svc)
        P.wait("scalar", svc)
        OP("scalar", lambda e: e.activation(K.A(AELGB, 128), K.A(ATMP, 128), AF.Exp), ssc)
        P.wait("gpsimd", ssc)
        OP("gpsimd", lambda e: e.tensor_scalar_mul(K.A(ANEGB, 128), K.A(AELGB, 128),
                                                     -1.0), sgc)
        OP("vector", lambda e: e.tensor_add(
            K.A(ATMP + 128, 16), cumg3[:, :, 31], K.A(ALF96, 16)), svc)
        P.wait("scalar", svc)
        OP("scalar", lambda e: e.activation(K.A(AELGL, 16), K.A(ATMP + 128, 16),
                                              AF.Exp), ssc)
        OP("vector", lambda e: e.tensor_sub(
            K.A(ALGL, 128).rearrange("p (h b) -> p h b", b=8),
            cumg3[:, :, 31].unsqueeze(2).broadcast_to([128, 16, 8]),
            K.A(ABSX, 128).rearrange("p (h b) -> p h b", b=8)), svc)
        P.wait("vector", svc)
        OP("vector", lambda e: e.tensor_add(K.A(ALGL, 128), K.A(ALGL, 128),
                                              K.A(ALFLR, 128)), svc)
        OP("vector", lambda e: e.tensor_copy(whtj(RZ_), cumg_rep), svc)
        P.wait("vector", svc)
        OP("vector", lambda e: e.tensor_sub(whbm(RZ_), whbm(RZ_), rep8(ABSX)), svc)
        P.wait("gpsimd", svc)
        OP("gpsimd", lambda e: e.tensor_add(RR(RZ_, 2048), RR(RZ_, 2048),
                                              K.A(AJLF, 2048)), sgc)
        P.wait("scalar", sgc)
        OP("scalar", lambda e: e.activation(RR(RE_, 2048), RR(RZ_, 2048), AF.Exp), ssc)
        P.wait("vector", sld, lmark["k"])
        P.wait("vector", ssc)
        OP("vector", lambda e: e.tensor_mul(whtj(RCSK_), kmic, whtj(RE_)), svc)
        P.wait("gpsimd", sld, lmark["q"])
        P.wait("gpsimd", ssc)
        OP("gpsimd", lambda e: e.tensor_mul(whtj(RCSQ_), qrep, whtj(RE_)), sgc)
        P.wait("scalar", svc)
        P.wait("scalar", sgc)
        OP("scalar", lambda e: e.activation(RR(RE_, 2048), RR(RZ_, 2048), AF.Exp,
                                              bias=K.A(ATMP + 144, 1)), ssc)
        P.wait("vector", ssc)
        OP("vector", lambda e: e.tensor_mul(whtj(RK40_), kmic, whtj(RE_)), svc)
        P.wait("gpsimd", ssc)
        OP("gpsimd", lambda e: e.tensor_mul(whtj(RQ40_), qrep, whtj(RE_)), sgc)
        P.wait("scalar", svc)
        P.wait("scalar", sgc)
        OP("scalar", lambda e: e.activation(RR(RE_, 2048), RR(RZ_, 2048), AF.Exp,
                                              scale=-1.0, bias=K.A(ATMP + 145, 1)), ssc)
        P.wait("vector", ssc)
        OP("vector", lambda e: e.tensor_mul(whtj(RPRK_), kmic, whtj(RE_)), svc)
        P.wait("gpsimd", svc)
        OP("gpsimd", lambda e: e.tensor_sub(whbm(RE_), rep8(ABER), whbm(RZ_)), sgc)
        P.wait("scalar", sgc)
        OP("scalar", lambda e: e.activation(RR(RE_, 2048), RR(RE_, 2048), AF.Exp), ssc)
        P.wait("vector", ssc)
        OP("vector", lambda e: e.tensor_mul(whtj(RPBK_), kmic, whtj(RE_)), svc)
        P.wait("gpsimd", svc)
        OP("gpsimd", lambda e: e.tensor_sub(whbm(RE_), rep8(ALGL), whbm(RZ_)), sgc)
        P.wait("scalar", sgc)
        OP("scalar", lambda e: e.activation(RR(RE_, 2048), RR(RE_, 2048), AF.Exp), ssc)
        P.wait("vector", ssc)
        OP("vector", lambda e: e.tensor_mul(whtj(RBB_), kmic, whtj(RE_)), svc)
        P.wait("gpsimd", svc)
        OP("gpsimd", lambda e: e.tensor_mul(whbm(RAN_), whbm(RCSK_), rep8(ANEGB)), sgc)
        OP("vector", lambda e: e.tensor_mul(whbm(RAQ_), whbm(RCSQ_), rep8(AELGB)), svc)
        P.wait("gpsimd", sld, lmark["v"])
        P.wait("gpsimd", svc)
        OP("gpsimd", lambda e: e.tensor_copy(whtj(RE_), vmicv), sgc)
        P.wait("tensor", sld, lmark["b"])
        OP("tensor", lambda e: e.matmul(K.ps[2][:, 0:16], K.arena[0:16, ABLD:ABLD + 128],
                                          K.cb["ident"][0:16, 0:16],
                                          start=True, stop=True), stc)
        P.wait("vector", stc)
        OP("vector", lambda e: e.tensor_copy(K.A(ABB, 16), K.ps[2][:, 0:16]), svc)
        P.wait("gpsimd", svc)
        OP("gpsimd", lambda e: e.tensor_scalar_mul(K.A(ANBB, 16), K.A(ABB, 16), -1.0), sgc)
        if c + 1 < NBC:
            P.wait("sync", svc)
            P.wait("sync", sgc)
            P.wait("sync", stc)
            lmark = load_chunk(c + 1)

        shipmark = {}
        for h in range(H_):
            if h == 0:
                P.wait("tensor", svc)
                P.wait("tensor", sgc)
            else:
                P.wait("tensor", ssc)
            OP("tensor", lambda e, h=h: e.matmul(
                K.ps[1][:, 0:256], RR(RPRK_ + h * 128, 128),
                RFULL[:, RK40_ // 2048:RK40_ // 2048 + 2, h * 128:(h + 1) * 128],
                start=True, stop=True), stc)
            P.wait("vector", stc)
            OP("vector", lambda e: e.memset(K.A(ABMAT, 128), 0.0), svc)
            for B in range(8):
                if B > 0:
                    P.wait("tensor", svc)
                    P.wait("tensor", sgc)
                    if B == 1 and h > 0:
                        P.wait("tensor", ssc)
                    OP("tensor", lambda e, h=h, B=B: e.matmul(
                        K.ps[0][:, 32 * B:32 * B + 32], K.A(ABMAT, 128),
                        RFULL[:, RCSK_ // 2048:RCSK_ // 2048 + 2,
                              h * 128 + 16 * B:h * 128 + 16 * B + 16],
                        start=True, stop=True), stc)
                if B < 7:
                    if B > 0:
                        P.wait("vector", stc)
                        OP("vector", lambda e, h=h, B=B: e.tensor_scalar(
                            K.A(ABMAT, 128), K.A(ABMAT, 128),
                            K.A(AEBER + h * 8 + B, 1), None, ALU.mult), svc)
                    P.wait("gpsimd", stc)
                    P.wait("gpsimd", svc)
                    OP("gpsimd", lambda e, h=h, B=B: e.tensor_copy(
                        K.A(ABMAT + 16 * B, 16),
                        RR(RPBK_ + h * 128 + 16 * B, 16)), sgc)
            P.wait("scalar", stc)
            P.wait("scalar", svc)
            if h > 0:
                for i4 in range(4):
                    P.wait("scalar", sxs[i4], shipmark[h - 1][i4])
            OP("scalar", lambda e: e.copy(K.A(AGRAM, 256), K.ps[1][:, 0:256]), ssc)
            if h > 1:
                for i4 in range(4):
                    P.wait("scalar", sxs[i4], shipmark[h - 2][i4])
            mrawo = AMRAW + (h % 2) * 128
            OP("scalar", lambda e, mrawo=mrawo: e.copy(
                K.A(mrawo + 16, 112).rearrange("p (b i) -> p b i", i=16),
                K.ps[0][:, 32:256].rearrange("p (b x) -> p b x", x=32)[:, :, 0:16]), ssc)
            OP("scalar", lambda e, h=h: e.copy(
                RR(RMQ_ + h * 128 + 16, 112).rearrange("p (b i) -> p b i", i=16),
                K.ps[0][:, 32:256].rearrange("p (b x) -> p b x", x=32)[:, :, 16:32]), ssc)
            P.wait("vector", ssc)
            OP("vector", lambda e, h=h: e.copy_predicated(
                RR(RMQ_ + h * 128, 128), K.mask8[:], K.A(AGRAM + 128, 128)), svc)
            OP("vector", lambda e, h=h, mrawo=mrawo: e.tensor_scalar(
                RR(RMT_ + h * 128, 128), K.A(mrawo, 128), K.A(ANBB + h, 1),
                None, ALU.mult), svc)
            P.wait("vector", svc)
            OP("vector", lambda e, h=h: e.tensor_scalar(
                RR(RMQ_ + h * 128, 128), RR(RMQ_ + h * 128, 128), K.A(ABB + h, 1),
                None, ALU.mult), svc)
            P.wait("sync", ssc)
            xv = K.xdiag[c, h].rearrange("i (j k) -> i j k", k=32)
            nship = 0
            for B in range(8):
                I, odd = B // 2, B % 2
                P.dma("sync", xv[I, 16 * odd:16 * odd + 16, 16 * odd:16 * odd + 16],
                      K.arena[16 * B:16 * B + 16, AGRAM + 16 * B:AGRAM + 16 * B + 16],
                      sxs[nship % 4])
                nship += 1
            for I in range(4):
                P.dma("sync", xv[I, 0:16, 16:32],
                      K.arena[32 * I:32 * I + 16, mrawo + 32 * I + 16:mrawo + 32 * I + 32],
                      sxs[nship % 4])
                nship += 1
            shipmark[h] = [sx.n for sx in sxs]

        P.wait("sync", stc)
        P.dma("sync", RR(RCSQ_, 2048, 0, 32),
              K.gateb[c // 4][32 * (c % 4):32 * (c % 4) + 32, :], sgt)
        gtmark = sgt.n

        for i4 in range(4):
            P.wait("sync", sxs[i4], shipmark[H_ - 1][i4])
        P.dma("sync", K.arena[0:64, ALB:ALB + 512],
              K.xdiag[c].rearrange("h i m -> (h i) m")[:, 0:512], sxb)
        P.dma("sync", K.arena[0:64, ALB:ALB + 1024]
              .rearrange("p (j k) -> p j k", k=32)[:, 16:32, 16:32],
              K.xdiag[c].rearrange("h i (j k) -> (h i) j k", k=32)[:, 16:32, 16:32], sxb)
        P.dma("sync", K.arena[0:64, ANBB2:ANBB2 + 32],
              K.betab[c].rearrange("h (i j) -> (h i) j", i=4), sxb)
        P.wait("vector", sxb)
        OP("vector", lambda e: e.tensor_scalar_mul(
            K.arena[0:64, ANBB2:ANBB2 + 32], K.arena[0:64, ANBB2:ANBB2 + 32], -1.0), svc)
        for j in range(31):
            OP("vector", lambda e, j=j: e.tensor_scalar(
                K.arena[0:64, ALB + j * 32 + j + 1:ALB + (j + 1) * 32],
                K.arena[0:64, ALB + j * 32 + j + 1:ALB + (j + 1) * 32],
                K.arena[0:64, ANBB2 + j:ANBB2 + j + 1], None, ALU.mult), svc)
        OP("vector", lambda e: e.tensor_copy(
            K.arena[0:64, AXT:AXT + 1024], K.arena[0:64, AID32:AID32 + 1024]), svc)
        xv3 = K.arena[0:64, AXT:AXT + 1024].rearrange("p (j i) -> p j i", i=32)
        for i in range(1, 32):
            for j in range(i):
                OP("vector", lambda e, i=i, j=j: e.scalar_tensor_tensor(
                    xv3[:, :, i], xv3[:, :, j],
                    K.arena[0:64, ALB + j * 32 + i:ALB + j * 32 + i + 1],
                    xv3[:, :, i], ALU.mult, ALU.add))
        OP("vector", lambda e: e.tensor_copy(
            K.arena[0:64, AXT:AXT + 1], K.arena[0:64, AXT:AXT + 1]), svc)
        P.wait("sync", svc)
        P.dma("sync", K.xinv[c].rearrange("h i m -> (h i) m"),
              K.arena[0:64, AXT:AXT + 1024], sxv)
        P.wait("sync", sxv)
        P.dma("sync", K.A(AXTA, 512).rearrange("p (h i) -> p h i", i=32),
              K.xinv[c].rearrange("h i (j k) -> (i j) h k", k=32), sxl)
        xlmark = sxl.n

        P.wait("tensor", sxl, xlmark)
        for h in range(H_):
            if h == 0:
                P.wait("tensor", sgc)
            P.wait("tensor", svc)
            OP("tensor", lambda e, h=h: e.matmul(
                K.ps[3][:, 0:128], RR(RE_ + h * 128, 128), K.cb["ident"][:],
                start=True, stop=False), stc)
            OP("tensor", lambda e, h=h: e.matmul(
                K.ps[3][:, 0:128], RR(RAN_ + h * 128, 128), RR(RS_ + h * 128, 128),
                start=False, stop=True), stc)
            for I in range(4):
                if I > 0:
                    P.wait("tensor", ssc)
                    OP("tensor", lambda e, h=h, I=I: e.matmul(
                        K.ps[4][32 * I:32 * I + 32, 0:128],
                        K.R[0:32 * I, RMT_ + h * 128 + 32 * I:RMT_ + h * 128 + 32 * I + 32],
                        K.arena[0:32 * I, AW_:AW_ + 128],
                        start=True, stop=True, tile_position=(0, 32 * I)), stc)
                    P.wait("vector", stc)
                    OP("vector", lambda e, I=I: e.tensor_copy(
                        K.arena[32 * I:32 * I + 32, ARP:ARP + 128],
                        K.ps[3][32 * I:32 * I + 32, 0:128]), svc)
                    OP("vector", lambda e, I=I: e.tensor_add(
                        K.arena[32 * I:32 * I + 32, ARP:ARP + 128],
                        K.arena[32 * I:32 * I + 32, ARP:ARP + 128],
                        K.ps[4][32 * I:32 * I + 32, 0:128]), svc)
                else:
                    P.wait("vector", stc)
                    OP("vector", lambda e: e.tensor_copy(
                        K.arena[0:32, ARP:ARP + 128], K.ps[3][0:32, 0:128]), svc)
                P.wait("tensor", svc)
                P.wait("tensor", ssc)
                OP("tensor", lambda e, h=h, I=I: e.matmul(
                    K.ps[5][32 * I:32 * I + 32, 0:128],
                    K.arena[32 * I:32 * I + 32,
                            AXTA + h * 32:AXTA + (h + 1) * 32],
                    K.arena[32 * I:32 * I + 32, ARP:ARP + 128],
                    start=True, stop=True, tile_position=(32 * I, 32 * I)), stc)
                P.wait("scalar", stc)
                OP("scalar", lambda e, I=I: e.copy(
                    K.arena[32 * I:32 * I + 32, AW_:AW_ + 128],
                    K.ps[5][32 * I:32 * I + 32, 0:128]), ssc)
            P.wait("tensor", ssc)
            OP("tensor", lambda e, h=h: e.matmul(
                K.ps[6][:, 0:128], RR(RAQ_ + h * 128, 128), RR(RS_ + h * 128, 128),
                start=True, stop=False), stc)
            OP("tensor", lambda e, h=h: e.matmul(
                K.ps[6][:, 0:128], RR(RMQ_ + h * 128, 128), K.A(AW_, 128),
                start=False, stop=True), stc)
            P.wait("scalar", stc)
            OP("scalar", lambda e: e.copy(K.A(AO_, 128), K.ps[6][:, 0:128]), ssc)
            if h < 4:
                slot, scol = 0, h * 128
            elif h < 8:
                slot, scol = 1, (h - 4) * 128
            elif h < 12:
                slot, scol = 2, (h - 8) * 128
            elif h < 15:
                slot, scol = 3, 128 + (h - 12) * 128
            else:
                slot, scol = 7, 384
            P.wait("tensor", ssc)
            OP("tensor", lambda e, h=h, slot=slot, scol=scol: e.matmul(
                K.ps[slot][0:32, scol:scol + 128],
                K.scl[:, 960 + h * 32:960 + (h + 1) * 32], K.A(AO_, 128),
                start=True, stop=True), stc)
            P.wait("tensor", svc)
            OP("tensor", lambda e, h=h: e.matmul(
                K.ps[7][:, 0:128], RR(RBB_ + h * 128, 128), K.cb["ident"][:],
                start=True, stop=True), stc)
            P.wait("vector", stc)
            OP("vector", lambda e, h=h: e.tensor_scalar(
                K.A(ABBAR, 128), K.ps[7][:, 0:128], K.A(ABB + h, 1), None, ALU.mult), svc)
            P.wait("tensor", svc)
            OP("tensor", lambda e: e.matmul(
                K.ps[7][:, 128:256], K.A(ABBAR, 128), K.A(AW_, 128),
                start=True, stop=True), stc)
            P.wait("vector", stc)
            OP("vector", lambda e, h=h: e.tensor_scalar(
                RR(RS_ + h * 128, 128), RR(RS_ + h * 128, 128), K.A(AELGL + h, 1),
                None, ALU.mult), svc)
            P.wait("vector", svc)
            OP("vector", lambda e, h=h: e.tensor_add(
                RR(RS_ + h * 128, 128), RR(RS_ + h * 128, 128), K.ps[7][:, 128:256]), svc)

        P.wait("scalar", stc)
        OP("scalar", lambda e: e.copy(RR(RAN_, 512, 0, 32), K.ps[0][0:32, 0:512]), ssc)
        OP("scalar", lambda e: e.copy(RR(RAN_ + 512, 512, 0, 32), K.ps[1][0:32, 0:512]), ssc)
        OP("scalar", lambda e: e.copy(RR(RAN_ + 1024, 512, 0, 32), K.ps[2][0:32, 0:512]), ssc)
        OP("scalar", lambda e: e.copy(RR(RAN_ + 1536, 384, 0, 32),
                                        K.ps[3][0:32, 128:512]), ssc)
        OP("scalar", lambda e: e.copy(RR(RAN_ + 1920, 128, 0, 32),
                                        K.ps[7][0:32, 384:512]), ssc)
        P.wait("scalar", ssc)
        OP("scalar", lambda e: e.activation(RR(RAQ_, 2048, 0, 32), RR(RAN_, 2048, 0, 32),
                                              AF.Square), ssc)
        P.wait("vector", ssc)
        for h in range(H_):
            OP("vector", lambda e, h=h: e.reduce_sum(
                K.arena[0:32, ATMP + h:ATMP + h + 1], RR(RAQ_ + h * 128, 128, 0, 32),
                mybir.AxisListType.X), svc)
        P.wait("vector", svc)
        OP("vector", lambda e: e.tensor_scalar(
            K.arena[0:32, ATMP + 16:ATMP + 32], K.arena[0:32, ATMP:ATMP + 16], 1.0 / 128, 1e-5,
            ALU.mult, ALU.add), svc)
        P.wait("vector", svc)
        OP("vector", lambda e: e.reciprocal(K.arena[0:32, ATMP + 32:ATMP + 48],
                                              K.arena[0:32, ATMP + 16:ATMP + 32]), svc)
        P.wait("scalar", svc)
        OP("scalar", lambda e: e.activation(K.arena[0:32, ATMP + 48:ATMP + 64],
                                              K.arena[0:32, ATMP + 32:ATMP + 48], AF.Sqrt), ssc)
        P.wait("vector", ssc)
        OP("vector", lambda e: e.tensor_mul(K.arena[0:32, ATMP + 64:ATMP + 80],
                                              K.arena[0:32, ATMP + 48:ATMP + 64],
                                              K.arena[0:32, ATMP + 48:ATMP + 64]), svc)
        P.wait("vector", svc)
        OP("vector", lambda e: e.tensor_mul(K.arena[0:32, ATMP + 64:ATMP + 80],
                                              K.arena[0:32, ATMP + 64:ATMP + 80],
                                              K.arena[0:32, ATMP + 16:ATMP + 32]), svc)
        P.wait("vector", svc)
        OP("vector", lambda e: e.tensor_scalar(K.arena[0:32, ATMP + 64:ATMP + 80],
                                                 K.arena[0:32, ATMP + 64:ATMP + 80],
                                                 -0.5, 1.5, ALU.mult, ALU.add), svc)
        P.wait("vector", svc)
        OP("vector", lambda e: e.tensor_mul(K.arena[0:32, ATMP + 48:ATMP + 64],
                                              K.arena[0:32, ATMP + 48:ATMP + 64],
                                              K.arena[0:32, ATMP + 64:ATMP + 80]), svc)
        P.wait("vector", svc)
        for h in range(H_):
            OP("vector", lambda e, h=h: e.scalar_tensor_tensor(
                RR(RAN_ + h * 128, 128, 0, 32), RR(RAN_ + h * 128, 128, 0, 32),
                K.arena[0:32, ATMP + 48 + h:ATMP + 49 + h], K.scl[0:32, 784:912],
                ALU.mult, ALU.mult), svc)
        P.wait("gpsimd", svc)
        P.wait("gpsimd", sgt, gtmark)
        for h in range(H_):
            OP("gpsimd", lambda e, h=h: e.tensor_mul(
                RR(RAN_ + h * 128, 128, 0, 32), RR(RAN_ + h * 128, 128, 0, 32),
                RR(RCSQ_ + h * 128, 128, 0, 32)), sgc)
        P.wait("tensor", sgc)
        P.wait("tensor", ssc)
        for h in range(H_):
            OP("tensor", lambda e, h=h: e.matmul(
                K.ps[3][:, h * 32:(h + 1) * 32], RR(RAN_ + h * 128, 128, 0, 32),
                K.cb["ident"][0:32, 0:32], start=True, stop=True), stc)
        P.wait("scalar", stc)
        OP("scalar", lambda e: e.copy(K.A(AOBST, 512), K.ps[3][:, 0:512]), ssc)
        P.wait("sync", ssc)
        P.dma("sync", K.ob.rearrange("h p t -> p h t")[:, :, 32 * c:32 * c + 32],
              K.A(AOBST, 512).rearrange("p (h t) -> p h t", t=32), sob)
    P.barrier()


def emit_attnb_proj(K, parts=None):
    if parts is None:
        parts = BPROJ_PARTS
    emit_attnb_prep(K)
    P = K.P
    sbst = P.sem("d_bst")
    stb = P.sem("t_bproj")
    svb = P.sem("v_bproj")
    ssb = P.sem("s_bproj")
    sgb = P.sem("g_bproj")
    Y1 = K.A(12288, 2048)
    acc = K.A(14336, 2048)
    sq = K.A(16384, 2048)
    n2 = K.arena[0:1, 12288 + 6144:12288 + 8192]
    r0 = K.arena[0:1, 16384:18432][0:1, :]
    trow = K.arena[0:1, 14336:16384][0:1, :]
    n2 = K.arena[0:1, 18432:20480]
    r_recip = K.arena[0:1, 16384:18432]
    y_row = K.arena[0:1, 14336 + 0:14336 + 2048]
    y_row = K.arena[0:1, 12288:14336]
    t_row = K.arena[0:1, 12288:14336]

    P.wait("vector", P.sem("d_bprep"))

    def post_conv_norm(oc, conv_col0, dst, norm, qscale, chain_done):
        slot = oc % 2
        st = K.stage[:, slot * T:(slot + 1) * T]
        P.wait("vector", chain_done)
        P.wait("vector", P.sem("ev_done"))
        P.op("vector", lambda e: e.tensor_scalar(
            acc, st, K.scl[:, conv_col0 + 3:conv_col0 + 4], None, ALU.mult), svb)
        for sh in range(1, 4):
            P.wait("vector", svb)
            P.op("vector", lambda e, sh=sh: e.scalar_tensor_tensor(
                acc[:, sh:T], st[:, 0:T - sh],
                K.scl[:, conv_col0 + 3 - sh:conv_col0 + 4 - sh],
                acc[:, sh:T], ALU.mult, ALU.add), svb)
        P.wait("scalar", svb)
        P.op("scalar", lambda e: e.activation(acc, acc, AF.Silu), ssb)
        if norm:
            P.wait("scalar", ssb)
            P.op("scalar", lambda e: e.activation(sq, acc, AF.Square), ssb)
            P.wait("tensor", ssb)
            for qq in range(4):
                P.op("tensor", lambda e, qq=qq: e.matmul(
                    K.ps[4 + qq][0:1, :], K.cb["ones_col"][:],
                    sq[:, qq * 512:(qq + 1) * 512], start=True, stop=True),
                    stb if qq == 3 else None)
            P.wait("vector", stb)
            for qq in range(4):
                P.op("vector", lambda e, qq=qq: e.tensor_copy(
                    n2[:, qq * 512:(qq + 1) * 512], K.ps[4 + qq][0:1, :]), svb)
            P.wait("vector", svb)
            P.op("vector", lambda e: e.tensor_scalar_add(n2, n2, 1e-6), svb)
            P.wait("vector", svb)
            P.op("vector", lambda e: e.reciprocal(r_recip, n2), svb)
            P.wait("scalar", svb)
            P.op("scalar", lambda e: e.activation(y_row, r_recip, AF.Sqrt), ssb)
            P.wait("vector", ssb)
            P.op("vector", lambda e: e.tensor_mul(r_recip, y_row, y_row), svb)
            P.wait("vector", svb)
            P.op("vector", lambda e: e.tensor_mul(r_recip, r_recip, n2), svb)
            P.wait("vector", svb)
            P.op("vector", lambda e: e.tensor_scalar(r_recip, r_recip, -0.5, 1.5,
                                                     ALU.mult, ALU.add), svb)
            P.wait("vector", svb)
            P.op("vector", lambda e: e.tensor_mul(y_row, y_row, r_recip), svb)
            if qscale is not None:
                P.wait("vector", svb)
                P.op("vector", lambda e: e.tensor_scalar_mul(y_row, y_row, qscale), svb)
            P.wait("tensor", svb)
            for qq in range(4):
                P.op("tensor", lambda e, qq=qq: e.matmul(
                    K.ps[4 + qq][:, 0:512], K.cb["ones_row"][0:1, :],
                    y_row[:, qq * 512:(qq + 1) * 512], start=True, stop=True),
                    stb if qq == 3 else None)
            P.wait("vector", stb)
            for qq in range(4):
                P.op("vector", lambda e, qq=qq: e.tensor_mul(
                    acc[:, qq * 512:(qq + 1) * 512], acc[:, qq * 512:(qq + 1) * 512],
                    K.ps[4 + qq][:, 0:512]), svb if qq == 3 else None)
            P.wait("sync", svb)
        else:
            P.wait("sync", ssb)
        P.dma("sync", dst, acc, sbst)

    def mk_evac(kind):
        def evac(oc, tt, bank, mmw, done):
            slot = oc % 2
            st = K.stage[:, slot * T + tt * 512: slot * T + (tt + 1) * 512]
            if tt == 0:
                P.wait("vector", sbst)
            P.wait("vector", *mmw)
            P.op("vector", lambda e, st=st, bank=bank: e.tensor_copy(st, bank[:]), done)
            if tt == 3:
                if kind == "k":
                    post_conv_norm(oc, 272 + oc * 4, K.kb[oc // 4, oc % 4], True, None, sbst)
                elif kind == "v":
                    post_conv_norm(oc, 528 + oc * 4, K.vb[oc // 4, oc % 4], False, None, sbst)
                elif kind == "q":
                    post_conv_norm(oc, 208 + oc * 4, K.qb[oc], True, float(DH) ** -0.5, sbst)
        return evac

    if "kvq" in parts:
        emit_proj(K, K.win["L0_attn_b_k_proj_w"], 64, mk_evac("k"))
        emit_proj(K, K.win["L0_attn_b_v_proj_w"], 64, mk_evac("v"))
        emit_proj(K, K.win["L0_attn_b_q_proj_w"], 16, mk_evac("q"))
    P.barrier()

    def f1_evac(oc, tt, bank, mmw, done):
        P.wait("vector", *mmw)
        P.op("vector", lambda e, tt=tt, bank=bank: e.tensor_copy(
            Y1[:, tt * 512:(tt + 1) * 512], bank[:]), done)
    if "g" in parts:
        emit_proj(K, K.win["L0_attn_b_f1_w"], 1, f1_evac)
    P.barrier()

    def f2_evac(oc, tt, bank, mmw, done):
        slot = oc % 2
        st = K.stage[:, slot * T + tt * 512: slot * T + (tt + 1) * 512]
        if tt == 0:
            P.wait("vector", sbst)
        P.wait("vector", *mmw)
        P.op("vector", lambda e, st=st, bank=bank, oc=oc: e.tensor_scalar(
            st, bank[:], K.scl[:, 128 + oc:129 + oc], None, ALU.add), done)
        if tt == 3:
            stg = K.stage[:, slot * T:(slot + 1) * T]
            P.wait("scalar", sbst)
            P.wait("scalar", P.sem("ev_done"))
            P.op("scalar", lambda e, stg=stg: e.activation(sq, stg, AF.Abs), ssb)
            P.wait("scalar", ssb)
            P.op("scalar", lambda e: e.activation(sq, sq, AF.Exp, scale=-1.0), ssb)
            P.wait("scalar", ssb)
            P.op("scalar", lambda e: e.activation(sq, sq, AF.Ln, bias=1.0), ssb)
            P.wait("vector", ssb)
            P.op("vector", lambda e, stg=stg: e.tensor_scalar_max(acc, stg, 0.0), svb)
            P.wait("vector", svb)
            P.op("vector", lambda e: e.tensor_add(acc, acc, sq), svb)
            P.wait("vector", svb)
            P.op("vector", lambda e, oc=oc: e.tensor_scalar(
                acc, acc, K.scl[:, 160 + oc:161 + oc], None, ALU.mult), svb)
            P.wait("sync", svb)
            P.dma("sync", K.gb[oc], acc, sbst)
    if "g" in parts:
        emit_proj(K, K.win["L0_attn_b_f2_w"], 16, f2_evac, din=128,
                  rhs_fn=lambda kc, tsl: Y1[:, tsl])
    P.barrier()

    if "gate" not in parts and "beta" not in parts:
        return
    if "gate" in parts:
        emit_proj(K, K.win["L0_attn_b_g1_w"], 1, f1_evac)
    P.barrier()
    sld2 = P.sem("d_bld")
    if "gate" not in parts:
        emit_beta_part(K, P, sld2, sbst, stb, svb, ssb, Y1)
        P.barrier()
        return
    P.dma("sync", K.arena[0:1, 16384:18432], K.win["L0_attn_b_g2_b"].unsqueeze(0), sld2)
    P.dma("sync", K.wnat[:, 0:2048],
          K.win["L0_attn_b_g2_w"].rearrange("(c p) n -> p c n", p=128), sld2)
    ld2 = sld2.n
    P.wait("tensor", sld2, ld2)
    for qq in range(4):
        P.op("tensor", lambda e, qq=qq: e.matmul(
            K.ps[4][:, 0:512], K.cb["ones_row"][0:1, :],
            K.arena[0:1, 16384 + qq * 512:16384 + (qq + 1) * 512],
            start=True, stop=True), stb)
        P.wait("vector", stb)
        P.op("vector", lambda e, qq=qq: e.tensor_copy(
            acc[:, qq * 512:(qq + 1) * 512], K.ps[4][:, 0:512]), svb)
        P.wait("tensor", svb)
    for c in range(16):
        bank = K.ps[2 + c % 2]
        P.op("tensor", lambda e, c=c, bank=bank: e.matmul(
            bank[:, 0:128], K.wnat[:, c * 128:(c + 1) * 128], K.cb["ident"][:],
            start=True, stop=True), stb)
        P.wait("vector", stb)
        P.op("vector", lambda e, c=c, bank=bank: e.tensor_copy(
            K.wT[:, c * 128:(c + 1) * 128], bank[:, 0:128]), svb)
        P.wait("tensor", svb)
    svb0 = svb.n
    ssb0 = ssb.n
    for tt in range(16):
        slot = tt % 2
        for oc in range(16):
            gi = tt * 16 + oc
            bank = K.ps[gi % 2]
            if gi >= 2:
                P.wait("tensor", svb, svb0 + gi - 1)
            P.op("tensor", lambda e, tt=tt, oc=oc, bank=bank: e.matmul(
                bank[:, 0:128], Y1[:, tt * 128:(tt + 1) * 128],
                K.wT[:, oc * 128:(oc + 1) * 128], start=True, stop=True), stb)
            st = K.stage[:, slot * 2048 + oc * 128: slot * 2048 + (oc + 1) * 128]
            if oc == 0 and tt >= 2:
                P.wait("vector", sbst)
            P.wait("vector", stb)
            P.op("vector", lambda e, st=st, bank=bank, oc=oc: e.tensor_add(
                st, bank[:, 0:128], acc[:, oc * 128:(oc + 1) * 128]), svb)
            P.wait("scalar", svb, svb0 + gi + 1)
            P.op("scalar", lambda e, st=st: e.activation(st, st, AF.Sigmoid), ssb)
        P.wait("sync", ssb, ssb0 + 16 * (tt + 1))
        P.dma("sync", K.gateb[tt], K.stage[:, slot * 2048:(slot + 1) * 2048], sbst)
    P.barrier()

    if "beta" in parts:
        emit_beta_part(K, P, sld2, sbst, stb, svb, ssb, Y1)
    P.barrier()


def emit_beta_part(K, P, sld2, sbst, stb, svb, ssb, Y1):
    P.dma("sync", K.wnat[0:64, 0:2048], K.win["L0_attn_b_b_proj_w"], sld2)
    for cc4 in range(4):
        P.dma("sync", Y1[32 * cc4:32 * cc4 + 32, 0:512],
              K.cin["expand"].rearrange("j t m -> t j m"), sld2)
    ld3 = sld2.n
    P.wait("tensor", sld2, ld3)
    for kc in range(16):
        bank = K.ps[2 + kc % 2]
        P.op("tensor", lambda e, kc=kc, bank=bank: e.matmul(
            bank[:, 0:64], K.wnat[0:64, kc * 128:(kc + 1) * 128],
            K.cb["ident"][0:64, 0:64], start=True, stop=True), stb)
        P.wait("vector", stb)
        P.op("vector", lambda e, kc=kc, bank=bank: e.tensor_copy(
            K.wT[:, kc * 64:(kc + 1) * 64], bank[:, 0:64]), svb)
        P.wait("tensor", svb)
    btok = K.A(16384, 64)
    svb0 = svb.n
    for tt in range(16):
        bank = K.ps[tt % 2]
        for kc in range(16):
            P.op("tensor", lambda e, tt=tt, kc=kc, bank=bank: e.matmul(
                bank[:, 0:64], K.Rc(kc)[:, tt * 128:(tt + 1) * 128].rearrange("p t -> p t")
                if False else K.Rc(kc)[:, tt * 128:(tt + 1) * 128],
                K.wT[:, kc * 64:(kc + 1) * 64], start=True, stop=True)
                if False else e.matmul(
                bank[:, 0:64], K.Rc(kc)[:, tt * 128:(tt + 1) * 128],
                K.wT[:, kc * 64:(kc + 1) * 64], start=(kc == 0), stop=(kc == 15)),
                stb if kc == 15 else None)
        P.wait("scalar", stb)
        P.op("scalar", lambda e, bank=bank: e.activation(btok, bank[:, 0:64], AF.Sigmoid), ssb)
        P.wait("tensor", ssb)
        for cc in range(4):
            c = tt * 4 + cc
            bank2 = K.ps[4 + cc % 2]
            if c >= 2:
                P.wait("tensor", svb, svb0 + c - 1)
            for j in range(4):
                P.op("tensor", lambda e, cc=cc, j=j, bank2=bank2: e.matmul(
                    bank2[0:16, 0:128],
                    btok[32 * cc:32 * cc + 32, :].rearrange("t (h j) -> t j h", j=4)[:, j, :],
                    Y1[32 * cc:32 * cc + 32, j * 128:(j + 1) * 128],
                    start=(j == 0), stop=(j == 3),
                    tile_position=(32 * cc, 0)),
                    stb if j == 3 else None)
            P.wait("vector", stb)
            st2 = K.stage[:, 2048 + cc * 128:2048 + (cc + 1) * 128]
            P.wait("vector", sbst)
            P.op("vector", lambda e, bank2=bank2, st2=st2: e.tensor_copy(
                st2[0:16, :], bank2[0:16, 0:128]), svb)
            P.wait("sync", svb)
            P.dma("sync", K.betab[c], st2[0:16, :], sbst)
    P.barrier()


PHASES = ["const", "x", "ln1", "aproj", "rope", "acore", "aout",
          "ln2", "ffn1", "ln3", "bproj", "bcore", "bout", "ln4", "ffn2", "store"]


def build(dumps=(), phases=None):
    K = MK(dumps=dumps)
    ph = PHASES if phases is None else phases
    emit_load_const(K)
    if "x" in ph:
        emit_load_x(K)
    K.P.barrier()
    if "ln1" in ph:
        emit_ln(K)
    if "aproj" in ph:
        emit_attna_proj(K)
    if "rope" in ph:
        emit_rope(K)
    if "acore" in ph:
        emit_attna_core(K)
    if "aout" in ph:
        emit_out_proj(K, "L0_attn_a_wo", K.oa, 16)
    if "ln2" in ph:
        emit_ln(K)
    if "ffn1" in ph:
        emit_ffn(K)
    if "ln3" in ph:
        emit_ln(K)
    if "bproj" in ph:
        emit_attnb_proj(K)
    if "bcore" in ph:
        emit_attnb_core(K)
    if "bout" in ph:
        emit_out_proj(K, "L0_attn_b_o_proj_w", K.ob, 16)
    if "ln4" in ph:
        emit_ln(K)
    if "ffn2" in ph:
        emit_ffn(K)
    if "store" in ph:
        emit_final_store(K)
    if dumps:
        emit_dumps(K)
    K.P.build()
    return K


def kernel(**inputs):
    K = build()
    consts = K.consts
    in_maps = []
    x = inputs["L0_attn_a_x"]
    for b in range(NCORES):
        m = {"x": np.ascontiguousarray(x[b])}
        for n, _ in WEIGHT_SPECS:
            m[n] = np.ascontiguousarray(inputs[n].reshape(
                dict(WEIGHT_SPECS)[n]))
        for n, _ in CONST_SPECS:
            m["ci_" + n] = consts[n]
        in_maps.append(m)
    res = run_bass_kernel_spmd(K.nc, in_maps, core_ids=list(range(NCORES)))
    return np.stack([r["out"] for r in res.results], axis=0)


# version 04
# speedup vs baseline: 1.1813x; 1.1813x over previous
import numpy as np
import concourse.bass as bass
import concourse.mybir as mybir
from concourse.bass_utils import run_bass_kernel_spmd

F32 = mybir.dt.float32
U8 = mybir.dt.uint8
AF = mybir.ActivationFunctionType
ALU = mybir.AluOpType

D = 2048
T = 2048
NCH = 16
H, HKV, DH = 16, 4, 128
ACH = 64
NACH = T // ACH
EPS_LN = 1e-5
NCORES = 8
BPROJ_PARTS = ("kvq", "g", "gate", "beta")
PREP_N = 99
PREP_DMAS = None
BCORE_NCH = 64


class Sem:
    def __init__(self, handle):
        self.h = handle
        self.n = 0

    def inc(self, k=1):
        self.n += k
        return self.n


class Prog:
    ENGS = ("sync", "tensor", "vector", "scalar", "gpsimd")

    def __init__(self, nc):
        self.nc = nc
        self.q = {e: [] for e in self.ENGS}
        self._sems = {}

    def sem(self, name):
        if name not in self._sems:
            h = self.nc.semaphore(name).__enter__()
            self._sems[name] = Sem(h)
        return self._sems[name]

    def emit(self, eng, fn):
        self.q[eng].append(fn)

    def wait(self, eng, sem, val=None):
        v = sem.n if val is None else val
        if v <= 0:
            return
        self.emit(eng, lambda e, sh=sem.h, v=v: e.wait_ge(sh, v))

    def dma(self, eng, out, in_, sem, **kw):
        if sem.n > 0:
            self.wait(eng, sem, sem.n)
        tgt = sem.inc(16)
        self.emit(eng, lambda e, out=out, in_=in_, sh=sem.h, kw=kw:
                  e.dma_start(out=out, in_=in_, **kw).then_inc(sh, 16))
        return tgt

    def op(self, eng, fn, sem=None):
        if sem is None:
            self.emit(eng, fn)
            return None
        tgt = sem.inc(1)
        self.emit(eng, lambda e, fn=fn, sh=sem.h: fn(e).then_inc(sh, 1))
        return tgt

    def spc(self, eng, n=2):
        for _ in range(n):
            if eng == "scalar":
                self.emit(eng, lambda e: e.copy(self.spc_dst, self.spc_src))
            else:
                self.emit(eng, lambda e: e.tensor_copy(self.spc_dst, self.spc_src))

    def barrier(self):
        for name, s in list(self._sems.items()):
            if name.startswith("d_"):
                self.wait("sync", s)
        sems = {e: self.sem(f"bar_{e}") for e in self.ENGS}
        for e in self.ENGS:
            s = sems[e]
            s.inc(1)
            if e == "sync":
                self.emit(e, lambda en, sh=s.h: en.sem_inc(sh, 1))
            else:
                def f(en, sh=s.h):
                    en.drain()
                    en.sem_inc(sh, 1)
                self.emit(e, f)
        for e in self.ENGS:
            for o in self.ENGS:
                if o != e:
                    self.wait(e, sems[o])

    def build(self):
        block = self.nc.Block().__enter__()
        for eng in self.ENGS:
            thunks = self.q[eng]
            if not thunks:
                continue

            def body(e, thunks=thunks):
                for f in thunks:
                    f(e)
            getattr(block, eng)(body)
        block.__exit__(None, None, None)


def make_constants():
    c = {}
    c["ident"] = np.eye(128, dtype=np.float32)
    c["ones_col"] = np.ones((128, 1), np.float32)
    c["ones_row"] = np.ones((1, 128), np.float32)
    prot = np.zeros((128, 128), np.float32)
    for p in range(64):
        prot[p, p + 64] = -1.0
        prot[p + 64, p] = 1.0
    c["protT"] = np.ascontiguousarray(prot.T)
    inv = 1.0 / 10000.0 ** (np.arange(0, DH, 2, dtype=np.float32) / DH)
    fr = np.arange(T, dtype=np.float32)[:, None] * inv[None, :]
    c["rope_cos"] = np.ascontiguousarray(
        np.concatenate([np.cos(fr), np.cos(fr)], -1).T, np.float32)
    c["rope_sin"] = np.ascontiguousarray(
        np.concatenate([np.sin(fr), np.sin(fr)], -1).T, np.float32)
    lam = np.exp2(-8.0 * np.arange(1, H + 1, dtype=np.float32) / H)
    r = np.arange(ACH, dtype=np.float32)
    diff = r[:, None] - r[None, :]
    maskT = np.zeros((H, ACH, ACH), np.float32)
    for h in range(H):
        maskT[h] = np.where(diff >= 0, np.exp(-lam[h] * diff), 0.0).T
    c["a_maskT"] = maskT
    c["a_qdec"] = np.ascontiguousarray(
        np.broadcast_to(np.exp(-lam[:, None] * (r[None, :] + 1.0))[:, None, :],
                        (H, 128, ACH)), np.float32)
    c["a_kdec"] = np.ascontiguousarray(
        np.exp(-lam[:, None] * (ACH - 1.0 - r[None, :])).astype(np.float32).T)
    c["a_blk"] = [float(x) for x in np.exp(-lam * ACH)]
    ex = np.zeros((4, 32, 128), np.float32)
    for j in range(4):
        for t in range(32):
            ex[j, t, 4 * t + j] = 1.0
    c["expand"] = ex
    c["jrow"] = np.broadcast_to((np.arange(128) % 4).astype(np.float32)[None, :],
                                (128, 128)).copy()
    c["divsel"] = np.eye(8, dtype=np.float32)[np.arange(128) // 16]
    c["modsel"] = np.eye(16, dtype=np.float32)[np.arange(128) % 16]
    c["id32flat"] = np.broadcast_to(np.eye(32, dtype=np.float32).reshape(1, 1024),
                                    (128, 1024)).copy()
    c["j4"] = np.eye(4, dtype=np.float32)[np.arange(128) % 4].T.copy()
    c["expT"] = np.eye(32, dtype=np.float32)[np.arange(128) // 4].copy()
    m = np.arange(128)
    c["j3row"] = np.broadcast_to((3.0 * ((m // 4) % 4) + (m % 4)).astype(np.float32)[None, :],
                                 (128, 128)).copy()
    bd = np.zeros((128, 128), np.uint8)
    for b in range(8):
        for jl in range(16):
            bd[16*b+jl, 16*b+jl:16*b+16] = 1
    c["bdmask8"] = bd
    c["bval12"] = np.broadcast_to((12.0 * (np.arange(128) % 8)).astype(np.float32)[None, :],
                                  (128, 128)).copy()
    emT = np.zeros((8, 128, 16), np.float32)
    for B in range(8):
        for jl in range(16):
            emT[B, 16 * B + jl, jl] = 1.0
    c["emb16T"] = emT
    emb = np.zeros((8, 16, 128), np.float32)
    for B in range(8):
        for jl in range(16):
            emb[B, jl, 16 * B + jl] = 1.0
    c["emb16"] = emb
    return c


CONST_SPECS = [("ident", [128, 128]), ("ones_col", [128, 1]), ("ones_row", [1, 128]),
               ("protT", [128, 128]), ("rope_cos", [128, T]), ("rope_sin", [128, T]),
               ("a_maskT", [H, ACH, ACH]), ("a_qdec", [H, 128, ACH]),
               ("a_kdec", [ACH, H]),
               ("expand", [4, 32, 128]), ("jrow", [128, 128]),
               ("divsel", [128, 8]), ("modsel", [128, 16]),
               ("id32flat", [128, 1024]), ("emb16", [8, 16, 128]),
               ("j4", [4, 128]), ("expT", [128, 32]),
               ("j3row", [128, 128]), ("bval12", [128, 128]), ("bdmask8", [128, 128]),
               ("emb16T", [8, 128, 16])]

WEIGHT_SPECS = [
    ("L0_attn_a_wq", [D, D]), ("L0_attn_a_wk", [512, D]), ("L0_attn_a_wv", [512, D]),
    ("L0_attn_a_wo", [D, D]),
    ("L0_attn_b_q_proj_w", [D, D]), ("L0_attn_b_k_proj_w", [4 * D, D]),
    ("L0_attn_b_v_proj_w", [4 * D, D]), ("L0_attn_b_q_conv_w", [D, 4]),
    ("L0_attn_b_k_conv_w", [4 * D, 4]), ("L0_attn_b_v_conv_w", [4 * D, 4]),
    ("L0_attn_b_f1_w", [128, D]), ("L0_attn_b_f2_w", [D, 128]),
    ("L0_attn_b_b_proj_w", [64, D]), ("L0_attn_b_micro_logits", [16, 4]),
    ("L0_attn_b_A_log", [16]), ("L0_attn_b_dt_bias", [D]),
    ("L0_attn_b_g1_w", [128, D]), ("L0_attn_b_g2_w", [D, 128]),
    ("L0_attn_b_g2_b", [D]), ("L0_attn_b_o_norm_w", [128]),
    ("L0_attn_b_o_proj_w", [D, D]),
    ("L0_ffn_up_weight", [4 * D, D]), ("L0_ffn_down_weight", [D, 4 * D]),
    ("L0_ffn_up_scale", [4 * D]), ("L0_ffn_down_scale", [D]),
    ("L0_ffn_output_scale", [1]),
]


class MK:
    def __init__(self, dumps=()):
        nc = bass.Bass("TRN2", target_bir_lowering=False, dynamic_dma_scratch_size=4096)
        self.nc = nc
        self.P = Prog(nc)
        self.consts = make_constants()
        self._guards = []

        self.x_in = nc.dram_tensor("x", [T, D], F32, kind="ExternalInput").ap()
        self.out = nc.dram_tensor("out", [T, D], F32, kind="ExternalOutput").ap()
        self.win = {n: nc.dram_tensor(n, s, F32, kind="ExternalInput").ap()
                    for n, s in WEIGHT_SPECS}
        self.cin = {n: nc.dram_tensor("ci_" + n, s,
                                      mybir.dt.uint8 if n == "bdmask8" else F32,
                                      kind="ExternalInput").ap()
                    for n, s in CONST_SPECS}

        def scr(name, shape):
            return nc.dram_tensor(name, shape, F32, kind="Internal").ap()
        self.scr = scr
        self.r_spill = scr("r_spill", [NCH, 128, T])
        self.r_new = scr("r_new", [NCH, 128, T])
        self.qa = scr("qa", [H, 128, T])
        self.ka = scr("ka", [HKV, 128, T])
        self.va = scr("va", [HKV, 128, T])
        self.oa = scr("oa", [H, 128, T])
        self.hid = scr("hid", [64, 128, T])
        self.ffn_part = scr("ffn_part", [NCH, 128, T])
        self.kb = scr("kb", [H, 4, 128, T])
        self.vb = scr("vb", [H, 4, 128, T])
        self.qb = scr("qb", [H, 128, T])
        self.gb = scr("gb", [H, 128, T])
        self.gateb = scr("gateb", [NCH, 128, H * 128])
        self.betab = scr("betab", [T // 32, 16, 128])
        self.xdiag = scr("xdiag", [T // 32, H, 4, 1024])
        self.xinv = scr("xinv", [T // 32, H, 4, 1024])
        self.ob = scr("ob", [H, 128, T])
        self.dumps = {}
        self.sbuf_dumps = []
        for name in dumps:
            if name == "scl":
                self.dumps["scl"] = nc.dram_tensor("dbg_scl", [128, 256], F32,
                                                   kind="ExternalOutput").ap()
                self.sbuf_dumps.append("scl")
                continue
            if name == "scl_early":
                self.scl_early = nc.dram_tensor("dbg_scl_early", [128, 256], F32,
                                                kind="ExternalOutput").ap()
                self.ex_early = nc.dram_tensor("dbg_ex_early", [128, 512], F32,
                                               kind="ExternalOutput").ap()
                continue
            shp = {"r_spill": [NCH, 128, T], "qa": [H, 128, T], "ka": [HKV, 128, T],
                   "va": [HKV, 128, T], "oa": [H, 128, T], "hid": [64, 128, T],
                   "r_new": [NCH, 128, T], "ffn_part": [NCH, 128, T],
                   "kb": [H, 4, 128, T], "vb": [H, 4, 128, T], "qb": [H, 128, T],
                   "gb": [H, 128, T], "gateb": [NCH, 128, H * 128],
                   "betab": [T // 32, 16, 128], "ob": [H, 128, T],
                   }[name]
            self.dumps[name] = nc.dram_tensor("dbg_" + name, shp, F32,
                                              kind="ExternalOutput").ap()

        def sb(name, shape, dtype=F32):
            g = nc.sbuf_tensor(name, shape, dtype)
            t = g.__enter__()
            self._guards.append(g)
            return t
        self.sb = sb
        self.R = sb("Rres", [128, NCH * T])
        self.cb = {}
        for nm, shp in [("ident", [128, 128]), ("ones_col", [128, 1]),
                        ("ones_row", [1, 128]), ("protT", [128, 128])]:
            self.cb[nm] = sb("c_" + nm, shp)
        self.scl = sb("scl", [128, 1536])
        self.mask8 = sb("mask8", [128, 128], mybir.dt.uint8)
        self.P.spc_src = self.scl[:, 1408:1472]
        self.P.spc_dst = self.scl[:, 1472:1536]
        self.arena = sb("arena", [128, 20480])
        self.wnat = self.arena[:, 0:4096]
        self.wT = self.arena[:, 4096:8192]
        self.stage = self.arena[:, 8192:12288]
        self.extra = self.arena[:, 12288:20480]
        self.ps = []
        for i in range(8):
            g = nc.psum_tensor(f"ps{i}", [128, 512], F32)
            self.ps.append(g.__enter__())
            self._guards.append(g)

    def Rc(self, c):
        return self.R[:, c * T:(c + 1) * T]

    def A(self, off, width):
        return self.arena[:, off:off + width]

    def row(self, i):
        return self.arena[0:1, 8192 + i * T:8192 + (i + 1) * T]


def emit_load_const(K):
    P = K.P
    s = P.sem("d_const")
    for name in ("ident", "ones_col", "ones_row", "protT"):
        P.dma("sync", K.cb[name][:], K.cin[name], s)


def emit_load_x(K):
    P = K.P
    P.wait("tensor", P.sem("d_const"))
    P.wait("vector", P.sem("d_const"))
    s = P.sem("d_x")
    stp = P.sem("t_x")
    svv = P.sem("v_x")
    slots = [K.A(0, 2048), K.A(2048, 2048)]
    s0 = s.n
    for tt in range(16):
        buf = slots[tt % 2]
        if tt >= 2:
            P.wait("sync", stp, (tt - 1) * 16)
        P.dma("sync", buf, K.x_in[tt * 128:(tt + 1) * 128, :], s)
        P.wait("tensor", s, s0 + 16 * (tt + 1))
        for c in range(NCH):
            gi = tt * 16 + c
            bank = K.ps[gi % 2]
            if gi >= 2:
                P.wait("tensor", svv, svv.n - 1)
            P.op("tensor", lambda e, buf=buf, c=c, bank=bank: e.transpose(
                bank[:, 0:128], buf[:, c * 128:(c + 1) * 128], K.cb["ident"][:]), stp)
            P.wait("vector", stp)
            P.op("vector", lambda e, c=c, tt=tt, bank=bank: e.tensor_copy(
                K.Rc(c)[:, tt * 128:(tt + 1) * 128], bank[:, 0:128]), svv)


def emit_ln(K):
    P = K.P
    ssq = P.sem("ln_sq")
    stns = P.sem("ln_tns")
    ssp = P.sem("d_rspill")
    svec = P.sem("ln_vec")
    pb = K.ps[0]
    sp0 = ssp.n
    sq_slots = [K.A(0, 2048), K.A(2048, 2048)]

    for c in range(NCH):
        P.dma("sync", K.r_spill[c], K.Rc(c), ssp)

    stns0 = stns.n
    ssq0 = ssq.n
    for c in range(NCH):
        sq = sq_slots[c % 2]
        if c >= 2:
            P.wait("scalar", stns, stns0 + (c - 1))
        P.op("scalar", lambda e, c=c, sq=sq: e.activation(sq, K.Rc(c), AF.Square), ssq)
        P.wait("tensor", ssq, ssq0 + c + 1)
        for qq in range(4):
            sl = slice(qq * 512, (qq + 1) * 512)
            P.op("tensor", lambda e, c=c, sl=sl, qq=qq: e.matmul(
                K.ps[qq][0:1, :], K.cb["ones_col"][:], K.Rc(c)[:, sl],
                start=(c == 0), stop=(c == NCH - 1)))
            P.op("tensor", lambda e, c=c, sl=sl, sq=sq, qq=qq: e.matmul(
                K.ps[4 + qq][0:1, :], K.cb["ones_col"][:], sq[:, sl],
                start=(c == 0), stop=(c == NCH - 1)), stns if qq == 3 else None)
    smm_t = stns.n

    mean, varpe, rstd, tmp = K.row(0), K.row(1), K.row(2), K.row(3)
    P.wait("vector", stns, smm_t)
    for qq in range(4):
        sl = slice(qq * 512, (qq + 1) * 512)
        P.op("vector", lambda e, qq=qq, sl=sl: e.tensor_scalar_mul(
            mean[:, sl], K.ps[qq][0:1, :], 1.0 / D))
        P.op("vector", lambda e, qq=qq, sl=sl: e.tensor_scalar_mul(
            varpe[:, sl], K.ps[4 + qq][0:1, :], 1.0 / D), svec)
    P.wait("vector", svec)
    P.op("vector", lambda e: e.tensor_mul(tmp, mean, mean), svec)
    P.wait("vector", svec)
    P.op("vector", lambda e: e.tensor_sub(varpe, varpe, tmp), svec)
    P.wait("vector", svec)
    P.op("vector", lambda e: e.tensor_scalar_add(varpe, varpe, EPS_LN), svec)
    P.wait("vector", svec)
    P.op("vector", lambda e: e.reciprocal(rstd, varpe), svec)
    P.wait("scalar", svec)
    P.op("scalar", lambda e: e.activation(rstd, rstd, AF.Sqrt), ssq)
    P.wait("vector", ssq)
    P.op("vector", lambda e: e.tensor_mul(tmp, rstd, rstd), svec)
    P.wait("vector", svec)
    P.op("vector", lambda e: e.tensor_mul(tmp, tmp, varpe), svec)
    P.wait("vector", svec)
    P.op("vector", lambda e: e.tensor_scalar(tmp, tmp, -0.5, 1.5, ALU.mult, ALU.add), svec)
    P.wait("vector", svec)
    P.op("vector", lambda e: e.tensor_mul(rstd, rstd, tmp), svec)
    P.wait("vector", svec)
    P.op("vector", lambda e: e.tensor_mul(tmp, mean, rstd), svec)

    rb0, rb1 = K.A(4096, 2048), K.A(6144, 2048)
    P.wait("tensor", svec)
    for which, row in ((0, rstd), (1, tmp)):
        dst = rb0 if which == 0 else rb1
        for qq in range(4):
            sl = slice(qq * 512, (qq + 1) * 512)
            P.op("tensor", lambda e, row=row, sl=sl: e.matmul(
                pb[:, 0:512], K.cb["ones_row"][0:1, :], row[:, sl],
                start=True, stop=True), stns)
            P.wait("vector", stns)
            P.op("vector", lambda e, dst=dst, sl=sl: e.tensor_copy(
                dst[:, sl], pb[:, 0:512]), svec)
            P.wait("tensor", svec)

    for c in range(NCH):
        P.wait("vector", ssp, sp0 + 16 * (c + 1))
        P.op("vector", lambda e, c=c: e.tensor_mul(K.Rc(c), K.Rc(c), rb0), svec)
        P.wait("vector", svec)
        P.op("vector", lambda e, c=c: e.tensor_sub(K.Rc(c), K.Rc(c), rb1), svec)
    P.barrier()


def emit_proj(K, w_ap, n_oc, evac_fn, din=D, rhs_fn=None, sign=False):
    P = K.P
    ndin = din // 128
    dW, tTR, vWT = P.sem("d_w"), P.sem("t_tr"), P.sem("v_wT")
    tMM, eEV, ssg = P.sem("t_mm"), P.sem("ev_done"), P.sem("s_sign")
    ntt = T // 512
    ngr = (ndin + 3) // 4
    dW0, tTR0, vWT0, tMM0, eEV0 = dW.n, tTR.n, vWT.n, tMM.n, eEV.n
    if rhs_fn is None:
        def rhs_fn(kc, tsl):
            return K.Rc(kc)[:, tsl]

    for oc in range(n_oc):
        slot = oc % 2
        wn = K.wnat[:, slot * 2048: slot * 2048 + din]
        wt = K.wT[:, slot * 2048: slot * 2048 + din]
        if oc >= 2:
            P.wait("sync", tTR, tTR0 + (oc - 1) * ngr)
        P.dma("sync", wn, w_ap[oc * 128:(oc + 1) * 128, :], dW)
        if sign:
            P.wait("scalar", dW, dW0 + 16 * (oc + 1))
            if oc >= 2:
                P.wait("scalar", tTR, tTR0 + (oc - 1) * ngr)
            P.op("scalar", lambda e, wn=wn: e.activation(wn, wn, AF.Sign), ssg)
            P.wait("tensor", ssg)
        else:
            P.wait("tensor", dW, dW0 + 16 * (oc + 1))
        for g in range(ngr):
            bank = K.ps[2 + g % 2]
            gi = oc * ngr + g
            if gi >= 2:
                P.wait("tensor", vWT, vWT0 + gi - 1)
            n_in_g = min(4, ndin - g * 4)
            for j in range(n_in_g):
                kc = g * 4 + j
                P.op("tensor", lambda e, wn=wn, kc=kc, j=j, bank=bank: e.transpose(
                    bank[:, j * 128:(j + 1) * 128],
                    wn[:, kc * 128:(kc + 1) * 128], K.cb["ident"][:]),
                    tTR if j == n_in_g - 1 else None)
            P.wait("vector", tTR, tTR0 + gi + 1)
            if oc >= 2 and g == 0:
                P.wait("vector", tMM, tMM0 + (oc - 1) * ntt)
            P.op("vector", lambda e, wt=wt, g=g, bank=bank, n_in_g=n_in_g: e.tensor_copy(
                wt[:, g * 512:g * 512 + n_in_g * 128], bank[:, 0:n_in_g * 128]), vWT)
        for tt in range(ntt):
            gidx = oc * ntt + tt
            bank = K.ps[gidx % 2]
            tsl = slice(tt * 512, (tt + 1) * 512)
            P.wait("tensor", vWT, vWT0 + (oc + 1) * ngr)
            if gidx >= 2:
                P.wait("tensor", eEV, eEV0 + gidx - 1)
            for kc in range(ndin):
                P.op("tensor", lambda e, wt=wt, kc=kc, bank=bank, tsl=tsl: e.matmul(
                    bank[:], wt[:, kc * 128:(kc + 1) * 128], rhs_fn(kc, tsl),
                    start=(kc == 0), stop=(kc == ndin - 1)),
                    tMM if kc == ndin - 1 else None)
            evac_fn(oc, tt, bank, (tMM, tMM0 + gidx + 1), eEV)


def emit_attna_proj(K):
    P = K.P
    sst = P.sem("d_qkv_st")

    def mk_evac(dst):
        def evac(oc, tt, bank, mmw, done):
            slot = oc % 2
            st = K.stage[:, slot * T + tt * 512: slot * T + (tt + 1) * 512]
            if tt == 0:
                P.wait("vector", P.sem("d_qkv_st"))
            P.wait("vector", *mmw)
            P.op("vector", lambda e, st=st, bank=bank: e.tensor_copy(st, bank[:]), done)
            if tt == 3:
                P.wait("sync", done)
                P.dma("sync", dst[oc], K.stage[:, slot * T:(slot + 1) * T], sst)
        return evac
    emit_proj(K, K.win["L0_attn_a_wq"], 16, mk_evac(K.qa))
    emit_proj(K, K.win["L0_attn_a_wk"], 4, mk_evac(K.ka))
    emit_proj(K, K.win["L0_attn_a_wv"], 4, mk_evac(K.va))
    P.barrier()


def emit_rope(K):
    P = K.P
    sld = P.sem("d_rope")
    stp = P.sem("t_rope")
    svv = P.sem("v_rope")
    sst = P.sem("d_ropest")
    tiles = [(K.qa, oc) for oc in range(16)] + [(K.ka, oc) for oc in range(4)]
    buf_slots = [K.A(0, 2048), K.A(2048, 2048)]
    tmp = K.A(4096, 2048)
    cosb, sinb = K.A(6144, 2048), K.A(8192, 2048)
    P.dma("sync", cosb, K.cin["rope_cos"], sld)
    P.dma("sync", sinb, K.cin["rope_sin"], sld)
    ld0 = sld.n
    for i, (scr, oc) in enumerate(tiles):
        buf = buf_slots[i % 2]
        if i >= 2:
            P.wait("sync", sst, 16 * (i - 1))
        P.dma("sync", buf, scr[oc], sld)
        P.wait("tensor", sld, ld0 + 16 * (i + 1))
        for qq in range(4):
            bank = K.ps[qq % 2]
            sl = slice(qq * 512, (qq + 1) * 512)
            if i > 0 or qq >= 2:
                P.wait("tensor", svv, max(0, svv.n - 1))
            P.op("tensor", lambda e, bank=bank, buf=buf, sl=sl: e.matmul(
                bank[:], K.cb["protT"][:], buf[:, sl], start=True, stop=True), stp)
            P.wait("vector", stp)
            P.op("vector", lambda e, bank=bank, sl=sl: e.tensor_mul(
                tmp[:, sl], bank[:], sinb[:, sl]), svv)
        P.wait("vector", svv)
        P.op("vector", lambda e, buf=buf: e.tensor_mul(buf, buf, cosb), svv)
        P.wait("vector", svv)
        P.op("vector", lambda e, buf=buf: e.tensor_add(buf, buf, tmp), svv)
        P.wait("sync", svv)
        P.dma("sync", scr[oc], buf, sst)
    P.barrier()


def emit_attna_core(K):
    P = K.P
    kv = K.A(0, 2048)
    qf_slots = [K.A(2048, 1024), K.A(3072, 1024)]
    kvf_slots = [K.A(4096, 512), K.A(4608, 512)]
    ktok = K.A(5120, 512)
    vtok = K.A(5632, 512)
    ks = K.A(6144, 128)
    qdq = K.A(6272, 64)
    scT = K.A(6336, 64)
    ost_slots = [K.A(6656, 1024), K.A(7680, 1024)]
    amaskT = K.A(8704, 1024)
    aqdec = K.A(9728, 1024)
    akdec = K.A(10752, 16)

    sld = P.sem("d_acore")
    stp = P.sem("t_acore")
    svv = P.sem("v_acore")
    ssc = P.sem("s_acore")
    sst = P.sem("d_aost")
    P.op("vector", lambda e: e.memset(kv[:], 0.0), svv)
    P.dma("sync", amaskT[0:ACH, :], K.cin["a_maskT"].rearrange("h j i -> j h i"), sld)
    P.dma("sync", aqdec, K.cin["a_qdec"].rearrange("h p i -> p h i"), sld)
    P.dma("sync", akdec[0:ACH, :], K.cin["a_kdec"], sld)
    ld0 = sld.n
    P.wait("vector", sld, ld0)
    P.wait("gpsimd", sld, ld0)

    def load_chunk(c):
        tsl = slice(c * ACH, (c + 1) * ACH)
        if c >= 2:
            P.wait("sync", ssc, 16 * (c - 1))
        P.dma("sync", qf_slots[c % 2][:, 0:16 * ACH],
              K.qa.rearrange("h p t -> p h t")[:, :, tsl], sld)
        P.dma("sync", kvf_slots[c % 2][:, 0:4 * ACH],
              K.ka.rearrange("h p t -> p h t")[:, :, tsl], sld)
        P.dma("sync", kvf_slots[c % 2][:, 4 * ACH:8 * ACH],
              K.va.rearrange("h p t -> p h t")[:, :, tsl], sld)

    load_chunk(0)
    for c in range(NACH):
        if c + 1 < NACH:
            load_chunk(c + 1)
        qfc, kvc = qf_slots[c % 2], kvf_slots[c % 2]
        P.wait("tensor", sld, ld0 + 48 * (c + 1))
        if c >= 2:
            P.wait("scalar", sst, 16 * (c - 1))
        for kh in range(HKV):
            P.op("tensor", lambda e, kh=kh, kvc=kvc: e.transpose(
                K.ps[0][0:64, kh * 128:(kh + 1) * 128],
                kvc[:, kh * 64:(kh + 1) * 64], K.cb["ident"][:]))
            P.op("tensor", lambda e, kh=kh, kvc=kvc: e.transpose(
                K.ps[1][0:64, kh * 128:(kh + 1) * 128],
                kvc[:, 4 * 64 + kh * 64:4 * 64 + (kh + 1) * 64], K.cb["ident"][:]),
                stp if kh == HKV - 1 else None)
        P.wait("vector", stp)
        P.wait("vector", svv)
        P.op("vector", lambda e: e.tensor_copy(ktok[0:64, 0:512], K.ps[0][0:64, 0:512]), svv)
        P.wait("vector", svv)
        P.op("vector", lambda e: e.tensor_copy(vtok[0:64, 0:512], K.ps[1][0:64, 0:512]), svv)
        P.wait("tensor", svv)
        for h in range(H):
            kh = h // 4
            kfc = kvc[:, kh * 64:(kh + 1) * 64]
            qfh = qfc[:, h * 64:(h + 1) * 64]
            ktk = ktok[0:64, kh * 128:(kh + 1) * 128]
            vtk = vtok[0:64, kh * 128:(kh + 1) * 128]
            kvh = kv[:, h * 128:(h + 1) * 128]
            b_sc = K.ps[2 + h % 2]
            b_o = K.ps[4 + h % 2]
            b_kv = K.ps[6 + h % 2]
            P.wait("tensor", svv)
            P.op("tensor", lambda e, kfc=kfc, qfh=qfh, b_sc=b_sc: e.matmul(
                b_sc[0:64, 0:64], kfc, qfh, start=True, stop=True), stp)
            P.op("vector", lambda e, ktk=ktk, h=h: e.tensor_scalar(
                ks[0:64, :], ktk, akdec[0:64, h:h + 1], None, ALU.mult), svv)
            P.wait("gpsimd", stp)
            P.op("gpsimd", lambda e, qfh=qfh, h=h: e.tensor_mul(
                qdq, qfh, aqdec[:, h * 64:(h + 1) * 64]), P.sem("g_acore"))
            P.wait("vector", stp)
            P.op("vector", lambda e, b_sc=b_sc, h=h: e.tensor_mul(
                scT[0:64, :], b_sc[0:64, 0:64],
                amaskT[0:64, h * 64:(h + 1) * 64]), svv)
            P.wait("tensor", svv)
            P.wait("tensor", P.sem("g_acore"))
            P.wait("tensor", ssc)
            P.op("tensor", lambda e, vtk=vtk, b_o=b_o: e.matmul(
                b_o[:, 0:64], vtk, scT[0:64, :], start=True, stop=False))
            P.op("tensor", lambda e, kvh=kvh, b_o=b_o: e.matmul(
                b_o[:, 0:64], kvh, qdq, start=False, stop=True), stp)
            P.op("tensor", lambda e, vtk=vtk, b_kv=b_kv: e.matmul(
                b_kv[:, 0:128], ks[0:64, :], vtk, start=True, stop=True), stp)
            P.wait("vector", stp)
            blk = K.consts["a_blk"][h]
            P.op("vector", lambda e, kvh=kvh, b_kv=b_kv, blk=blk: e.scalar_tensor_tensor(
                kvh, kvh, blk, b_kv[:, 0:128], ALU.mult, ALU.add), svv)
            P.wait("scalar", stp)
            P.op("scalar", lambda e, b_o=b_o, h=h, c=c: e.copy(
                ost_slots[c % 2][:, h * 64:(h + 1) * 64], b_o[:, 0:64]), ssc)
        P.wait("sync", ssc)
        P.dma("sync", K.oa.rearrange("h p t -> p h t")[:, :, c * ACH:(c + 1) * ACH],
              ost_slots[c % 2], sst)
    P.barrier()


def emit_out_proj(K, w_name, src_scr, n_in_ch, scale_fn=None):
    P = K.P
    sld = P.sem("d_oproj")
    srn = P.sem("d_rnew")
    srold = P.sem("d_rold")
    svv = P.sem("v_oproj")
    ld0 = sld.n
    for c in range(n_in_ch):
        P.dma("sync", K.Rc(c), src_scr[c], sld)
    P.wait("tensor", sld, ld0 + 16 * n_in_ch)
    P.wait("vector", sld, ld0 + 16 * n_in_ch)
    rold_slots = [K.extra[:, 0:512], K.extra[:, 512:1024]]

    def evac(oc, tt, bank, mmw, done):
        slot = oc % 2
        st = K.stage[:, slot * T + tt * 512: slot * T + (tt + 1) * 512]
        gidx = oc * 4 + tt
        rb = rold_slots[gidx % 2][:, 0:512]
        P.wait("sync", done, max(0, done.n - 1))
        P.dma("sync", rb, K.r_spill[oc][:, tt * 512:(tt + 1) * 512], srold)
        if tt == 0:
            P.wait("vector", P.sem("d_rnew"))
        P.wait("vector", srold)
        P.wait("vector", *mmw)
        if scale_fn is None:
            P.op("vector", lambda e, st=st, bank=bank, rb=rb: e.tensor_add(
                st, bank[:], rb), done)
        else:
            sc = scale_fn(oc)
            P.op("vector", lambda e, st=st, bank=bank, sc=sc: e.tensor_scalar(
                st, bank[:], sc, None, ALU.mult))
            P.op("vector", lambda e, st=st, rb=rb: e.tensor_add(st, st, rb), done)
        if tt == 3:
            P.wait("sync", done)
            P.dma("sync", K.r_new[oc], K.stage[:, slot * T:(slot + 1) * T], srn)

    emit_proj(K, K.win[w_name], NCH, evac, din=n_in_ch * 128)
    P.barrier()
    s2 = P.sem("d_rback")
    for c in range(NCH):
        P.dma("sync", K.Rc(c), K.r_new[c], s2)
    P.barrier()


def emit_final_store(K):
    P = K.P
    stp = P.sem("t_fin")
    svv = P.sem("v_fin")
    sst = P.sem("d_fin")
    for c in range(NCH):
        for tt in range(NCH):
            bank = K.ps[(c * NCH + tt) % 2]
            gi = c * NCH + tt
            if gi >= 2:
                P.wait("tensor", svv, svv.n - 1)
            P.op("tensor", lambda e, c=c, tt=tt, bank=bank: e.transpose(
                bank[:, 0:128], K.Rc(c)[:, tt * 128:(tt + 1) * 128],
                K.cb["ident"][:]), stp)
            P.wait("vector", stp)
            st = K.stage[:, (gi % 8) * 128:(gi % 8 + 1) * 128]
            if gi >= 8:
                P.wait("vector", sst, 16 * (gi - 7))
            P.op("vector", lambda e, st=st, bank=bank: e.tensor_copy(
                st, bank[:, 0:128]), svv)
            P.wait("sync", svv)
            P.dma("sync", K.out[tt * 128:(tt + 1) * 128, c * 128:(c + 1) * 128],
                  st, sst)
    P.wait("sync", sst)


def emit_dumps(K):
    P = K.P
    s = P.sem("d_dump")
    for name, dst in K.dumps.items():
        if name in K.sbuf_dumps:
            P.dma("sync", dst, getattr(K, name)[:], s)
        else:
            P.dma("sync", dst, getattr(K, name), s)
    P.wait("sync", s)


def emit_ffn(K):
    P = K.P
    sld = P.sem("d_ffs")
    stp = P.sem("t_ffs")
    svv = P.sem("v_ffs")
    ld0 = sld.n
    ex = K.extra
    svv0 = P.sem("v_ffs")
    P.op("vector", lambda e: e.memset(ex[:, 0:256], 0.0), svv0)
    P.wait("sync", svv0)
    P.dma("sync", ex[0:64, 0:128], K.win["L0_ffn_up_scale"].rearrange("(a b) -> a b", b=128), sld)
    P.dma("sync", ex[0:16, 128:256], K.win["L0_ffn_down_scale"].rearrange("(a b) -> a b", b=128), sld)
    P.dma("sync", ex[0:1, 384:385], K.win["L0_ffn_output_scale"].unsqueeze(0), sld)
    P.wait("vector", sld, ld0 + 48)
    P.op("vector", lambda e: e.tensor_copy(ex[0:1, 256:257], ex[0:1, 384:385]), svv)
    P.wait("tensor", svv)
    P.op("tensor", lambda e: e.matmul(K.ps[6][:, 0:128], ex[:, 128:256],
                                      K.cb["ident"][:], start=True, stop=True))
    P.op("tensor", lambda e: e.matmul(K.ps[5][:, 0:128], ex[:, 0:128],
                                      K.cb["ident"][:], start=True, stop=True), stp)
    P.op("tensor", lambda e: e.matmul(K.ps[7][:, 0:16], K.cb["ones_row"][0:1, :],
                                      ex[0:1, 256:257].broadcast_to([1, 16]),
                                      start=True, stop=True), stp)
    P.wait("vector", stp)
    P.wait("vector", svv)
    P.op("vector", lambda e: e.tensor_copy(K.scl[:, 96:97], K.ps[7][:, 0:1]), svv)
    P.op("vector", lambda e: e.tensor_copy(K.scl[:, 0:64], K.ps[5][:, 0:64]), svv)
    P.wait("vector", svv)
    P.op("vector", lambda e: e.tensor_scalar(K.scl[:, 64:80], K.ps[6][:, 0:16],
                                             K.scl[:, 96:97], None, ALU.mult), svv)
    P.barrier()
    if hasattr(K, "scl_early"):
        P.dma("sync", K.scl_early, K.scl[:], P.sem("d_dump"))
        P.dma("sync", K.ex_early, ex[:, 0:512], P.sem("d_dump"))
        P.barrier()

    shst = P.sem("d_hidst")

    def up_evac(oc, tt, bank, mmw, done):
        slot = oc % 2
        st = K.stage[:, slot * T + tt * 512: slot * T + (tt + 1) * 512]
        if tt == 0:
            P.wait("vector", P.sem("d_hidst"))
        P.wait("vector", *mmw)
        P.op("vector", lambda e, st=st, bank=bank, oc=oc: e.tensor_scalar(
            st, bank[:], K.scl[:, oc:oc + 1], 0.0, ALU.mult, ALU.max), done)
        if tt == 3:
            P.wait("sync", done)
            P.dma("sync", K.hid[oc], K.stage[:, slot * T:(slot + 1) * T], shst)

    emit_proj(K, K.win["L0_ffn_up_weight"], 64, up_evac, sign=True)
    P.barrier()

    dW, tTR, vWT = P.sem("d_w"), P.sem("t_tr"), P.sem("v_wT")
    tMM, eEV, ssg = P.sem("t_mm"), P.sem("ev_done"), P.sem("s_sign")
    sprt = P.sem("d_part")
    shl = P.sem("d_hidld")
    srn = P.sem("d_rnew2")
    srold = P.sem("d_rold2")
    rold_slots = [K.extra[:, 0:1024], K.extra[:, 1024:2048]]
    part_slots = [K.extra[:, 2048:2560], K.extra[:, 2560:3072]]
    for q in range(4):
        hl0 = shl.n
        tMM0q = tMM.n
        P.barrier()
        for c in range(16):
            P.dma("sync", K.Rc(c), K.hid[q * 16 + c], shl)
        P.wait("tensor", shl, hl0 + 256)
        P.wait("vector", shl, hl0 + 256)
        for oc in range(NCH):
            slot = oc % 2
            wn = K.wnat[:, slot * 2048: slot * 2048 + 2048]
            wt = K.wT[:, slot * 2048: slot * 2048 + 2048]
            if oc >= 2:
                P.wait("sync", tTR, tTR.n - 4)
            P.dma("sync", wn, K.win["L0_ffn_down_weight"][oc * 128:(oc + 1) * 128,
                                                          q * 2048:(q + 1) * 2048], dW)
            P.wait("scalar", dW)
            if oc >= 2:
                P.wait("scalar", tTR, tTR.n - 4)
            P.op("scalar", lambda e, wn=wn: e.activation(wn, wn, AF.Sign), ssg)
            P.wait("tensor", ssg)
            for g in range(4):
                bank = K.ps[2 + g % 2]
                if oc * 4 + g >= 2:
                    P.wait("tensor", vWT, vWT.n - 1)
                for j in range(4):
                    kc = g * 4 + j
                    P.op("tensor", lambda e, wn=wn, kc=kc, j=j, bank=bank: e.transpose(
                        bank[:, j * 128:(j + 1) * 128],
                        wn[:, kc * 128:(kc + 1) * 128], K.cb["ident"][:]),
                        tTR if j == 3 else None)
                P.wait("vector", tTR)
                if g == 0 and oc >= 2:
                    P.wait("vector", tMM, tMM0q + (oc - 1) * 4)
                P.op("vector", lambda e, wt=wt, g=g, bank=bank: e.tensor_copy(
                    wt[:, g * 512:(g + 1) * 512], bank[:, 0:512]), vWT)
            for tt in range(4):
                gidx = oc * 4 + tt
                bank = K.ps[gidx % 2]
                tsl = slice(tt * 512, (tt + 1) * 512)
                P.wait("tensor", vWT)
                if gidx >= 2:
                    P.wait("tensor", eEV, eEV.n - 1)
                for kc in range(16):
                    P.op("tensor", lambda e, wt=wt, kc=kc, bank=bank, tsl=tsl: e.matmul(
                        bank[:], wt[:, kc * 128:(kc + 1) * 128], K.Rc(kc)[:, tsl],
                        start=(kc == 0), stop=(kc == 15)), tMM if kc == 15 else None)
                slot2 = gidx % 2
                st = K.stage[:, (oc % 2) * T + tt * 512: (oc % 2) * T + (tt + 1) * 512]
                if tt == 0:
                    P.wait("vector", sprt)
                    P.wait("vector", srn)
                if q == 0:
                    P.wait("vector", tMM)
                    P.op("vector", lambda e, st=st, bank=bank: e.tensor_copy(st, bank[:]), eEV)
                else:
                    pb = part_slots[slot2][:, 0:512]
                    P.wait("sync", eEV, max(0, eEV.n - 1))
                    P.dma("sync", pb, K.ffn_part[oc][:, tsl], sprt)
                    P.wait("vector", sprt)
                    P.wait("vector", tMM)
                    if q < 3:
                        P.op("vector", lambda e, st=st, bank=bank, pb=pb: e.tensor_add(
                            st, bank[:], pb), eEV)
                    else:
                        rb = rold_slots[slot2][:, 512:1024]
                        P.wait("sync", eEV, max(0, eEV.n - 1))
                        P.dma("sync", rb, K.r_spill[oc][:, tsl], srold)
                        P.wait("vector", srold)
                        P.op("vector", lambda e, st=st, bank=bank, pb=pb: e.tensor_add(
                            st, bank[:], pb), eEV)
                        P.wait("vector", eEV)
                        P.op("vector", lambda e, st=st, oc=oc: e.tensor_scalar(
                            st, st, K.scl[:, 64 + oc:64 + oc + 1], None, ALU.mult), eEV)
                        P.wait("vector", eEV)
                        P.op("vector", lambda e, st=st, rb=rb: e.tensor_add(st, st, rb), eEV)
                if tt == 3:
                    P.wait("sync", eEV)
                    dst = K.ffn_part[oc] if q < 3 else K.r_new[oc]
                    P.dma("sync", dst, K.stage[:, (oc % 2) * T:(oc % 2 + 1) * T],
                          srn if q == 3 else sprt)
    P.barrier()
    s2 = P.sem("d_rback2")
    for c in range(NCH):
        P.dma("sync", K.Rc(c), K.r_new[c], s2)
    P.barrier()




def emit_attnb_prep(K):
    P = K.P
    sld = P.sem("d_bprep")
    stp = P.sem("t_bprep")
    svv = P.sem("v_bprep")
    ssc = P.sem("s_bprep")
    sgp = P.sem("g_bprep")
    ex = K.extra
    P.op("vector", lambda e: e.memset(ex[:, 0:768], 0.0), svv)
    P.wait("sync", svv)
    _dmas = [
        lambda: P.dma("sync", ex[0:16, 0:128],
                      K.win["L0_attn_b_dt_bias"].rearrange("(a b) -> a b", b=128), sld),
        lambda: P.dma("sync", ex[0:1, 256:272], K.win["L0_attn_b_A_log"].unsqueeze(0), sld),
        lambda: P.dma("sync", ex[0:16, 384:388], K.win["L0_attn_b_micro_logits"], sld),
        lambda: P.dma("sync", ex[0:1, 388:516], K.win["L0_attn_b_o_norm_w"].unsqueeze(0), sld),
        lambda: P.dma("sync", K.scl[:, 208:272],
                      K.win["L0_attn_b_q_conv_w"].rearrange("(c p) k -> p c k", p=128), sld),
        lambda: P.dma("sync", K.scl[:, 272:528],
                      K.win["L0_attn_b_k_conv_w"].rearrange("(c p) k -> p c k", p=128), sld),
        lambda: P.dma("sync", K.scl[:, 528:784],
                      K.win["L0_attn_b_v_conv_w"].rearrange("(c p) k -> p c k", p=128), sld),
        lambda: P.dma("sync", ex[0:4, 640:768], K.cin["j4"], sld),
        lambda: P.dma("sync", ex[:, 768:800], K.cin["expT"], sld),
    ]
    for _i, _f in enumerate(_dmas):
        if PREP_DMAS is None or _i in PREP_DMAS:
            _f()
    ld_t = sld.n
    if PREP_N == 1:
        P.barrier(); return

    P.wait("tensor", sld, ld_t)
    P.op("tensor", lambda e: e.matmul(K.ps[0][:, 0:128], ex[:, 0:128], K.cb["ident"][:],
                                      start=True, stop=True), stp)
    P.wait("scalar", sld, ld_t)
    P.op("scalar", lambda e: e.activation(ex[0:1, 272:288], ex[0:1, 256:272], AF.Exp), ssc)
    P.wait("gpsimd", ssc)
    P.op("gpsimd", lambda e: e.tensor_scalar_mul(ex[0:1, 288:304], ex[0:1, 272:288], -1.0), sgp)
    P.wait("tensor", sgp)
    P.op("tensor", lambda e: e.matmul(K.ps[1][:, 0:16], K.cb["ones_row"][0:1, :],
                                      ex[0:1, 288:304], start=True, stop=True))
    P.op("tensor", lambda e: e.matmul(K.ps[2][:, 0:128], K.cb["ones_row"][0:1, :],
                                      ex[0:1, 388:516], start=True, stop=True), stp)
    P.wait("vector", stp)
    P.op("vector", lambda e: e.tensor_copy(K.scl[:, 128:144], K.ps[0][:, 0:16]))
    P.op("vector", lambda e: e.tensor_copy(K.scl[:, 160:176], K.ps[1][:, 0:16]))
    P.op("vector", lambda e: e.tensor_copy(K.scl[:, 784:912], K.ps[2][:, 0:128]), svv)
    if PREP_N == 2:
        P.barrier(); return

    t0 = ex[:, 544:560]
    t1 = ex[:, 560:576]
    P.wait("gpsimd", svv)
    P.op("gpsimd", lambda e: e.tensor_scalar_add(t0, K.scl[:, 128:144], -10000.0), sgp)
    P.wait("scalar", sgp)
    P.op("scalar", lambda e: e.activation(t1, t0, AF.Abs), ssc)
    P.wait("scalar", ssc)
    P.op("scalar", lambda e: e.activation(t1, t1, AF.Exp, scale=-1.0), ssc)
    P.wait("scalar", ssc)
    P.op("scalar", lambda e: e.activation(t1, t1, AF.Ln, bias=1.0), ssc)
    P.wait("gpsimd", ssc)
    P.op("gpsimd", lambda e: e.tensor_scalar_max(t0, t0, 0.0), sgp)
    P.wait("vector", sgp)
    P.op("vector", lambda e: e.tensor_add(t1, t1, t0), svv)
    P.wait("vector", svv)
    P.op("vector", lambda e: e.tensor_mul(K.scl[:, 144:160], t1, K.scl[:, 160:176]), svv)

    if PREP_N == 3:
        P.barrier(); return
    w = ex[0:16, 384:388]
    wm = ex[0:16, 576:580]
    rmax = ex[0:16, 580:581]
    rsum = ex[0:16, 581:582]
    P.op("vector", lambda e: e.reduce_max(rmax, w, mybir.AxisListType.X), svv)
    P.wait("gpsimd", svv)
    P.op("gpsimd", lambda e: e.tensor_scalar_mul(rmax, rmax, -1.0), sgp)
    P.wait("scalar", sgp)
    P.op("scalar", lambda e: e.activation(wm, w, AF.Exp, bias=rmax), ssc)
    if PREP_N == 4:
        P.barrier(); return
    P.wait("vector", ssc)
    P.op("vector", lambda e: e.reduce_sum(rsum, wm, mybir.AxisListType.X), svv)
    P.wait("vector", svv)
    P.op("vector", lambda e: e.reciprocal(rmax, rsum), svv)
    P.wait("vector", svv)
    P.op("vector", lambda e: e.tensor_scalar(wm, wm, rmax, None, ALU.mult), svv)
    P.wait("vector", svv)
    P.op("vector", lambda e: e.tensor_copy(ex[0:16, 592:596], wm), svv)
    if PREP_N == 5:
        P.barrier(); return
    P.wait("tensor", svv)
    P.op("tensor", lambda e: e.matmul(K.ps[3][0:4, 0:128], ex[:, 592:596], K.cb["ident"][:],
                                      start=True, stop=True), stp)
    P.wait("vector", stp)
    P.op("vector", lambda e: e.tensor_copy(ex[0:4, 608:624], K.ps[3][0:4, 0:16]), svv)
    P.wait("tensor", svv)
    P.op("tensor", lambda e: e.matmul(K.ps[4][:, 0:16], ex[0:4, 640:768], ex[0:4, 608:624],
                                      start=True, stop=True), stp)
    P.wait("vector", stp)
    P.op("vector", lambda e: e.tensor_copy(K.scl[:, 932:948], K.ps[4][:, 0:16]), svv)
    P.wait("vector", svv)
    for h in range(H):
        P.op("vector", lambda e, h=h: e.tensor_scalar(
            K.scl[:, 960 + h * 32:960 + (h + 1) * 32], ex[:, 768:800],
            K.scl[:, 932 + h:933 + h], None, ALU.mult), svv)
    P.barrier()


RS_, RV_, RAN_, RAQ_, RBB_ = 0, 2048, 4096, 6144, 8192
RMT_, RMQ_, RK_, RZ_ = 10240, 12288, 14336, 16384
RCSK_, RCSQ_, RK40_, RQ40_ = 18432, 20480, 22528, 24576
RPRK_, RPBK_, RE_ = 26624, 28672, 30720
ACUMG, AG_, AQT = 0, 512, 1024
AJLF, AID32, ALB, AXT = 2048, 4096, 5120, 6144
AXTA = 7168
AJ3, ABV12, ALFLR = 7680, 7808, 7936
ALF12, ALF96 = 8064, 8080
ABSX, ACND, ABER, AEBER, ALGL = 8096, 8224, 8352, 8480, 8608
AELGB, ANEGB, AELGL = 8736, 8864, 8992
ABB, ANBB, ABLD, ANBB2 = 9008, 9024, 9040, 9168
AGRAM, AMRAW, ABMAT = 9200, 9456, 9712
AW_, AO_, ABBAR, ARP = 9840, 9968, 10096, 10224
AOBST, ATMP = 10352, 10864


def emit_attnb_core(K):
    P = K.P
    H_ = 16
    sld = P.sem("d_bcl")
    sgt = P.sem("d_bgt")
    sxs = [P.sem(f"d_bxs{i}") for i in range(4)]
    sxb = P.sem("d_bxb")
    sxv = P.sem("d_bxv")
    sxl = P.sem("d_bxl")
    sob = P.sem("d_bob")
    stc = P.sem("t_bc")
    svc = P.sem("v_bc")
    ssc = P.sem("s_bc")
    sgc = P.sem("g_bc")
    QS = {"tensor": stc, "vector": svc, "scalar": ssc, "gpsimd": sgc}

    def OP(eng, fn, sem=None):
        P.wait(eng, QS[eng])
        P.op(eng, fn, sem if sem is not None else QS[eng])

    def RR(off, w, p0=0, p1=128):
        return K.R[p0:p1, off:off + w]

    RFULL = K.R[:, 0:NCH * T].rearrange("p (a b) -> p a b", b=2048)
    kmic = RR(RK_, 2048).rearrange("p (h r t) -> p h t r", r=4, t=32)
    vmicv = RR(RV_, 2048).rearrange("p (h r t) -> p h t r", r=4, t=32)
    qrep = K.A(AQT, 512).rearrange("p (h t) -> p h t", t=32) \
        .unsqueeze(3).broadcast_to([128, 16, 32, 4])
    cumg3 = K.A(ACUMG, 512).rearrange("p (h t) -> p h t", t=32)
    cumg_rep = cumg3.unsqueeze(3).broadcast_to([128, 16, 32, 4])
    cumg4 = K.A(ACUMG, 512).rearrange("p (h b f) -> p h b f", b=8, f=4)

    def rep8(off):
        return K.A(off, 128).rearrange("p (h b) -> p h b", b=8) \
            .unsqueeze(3).broadcast_to([128, 16, 8, 16])

    def rep16(off, n):
        return K.A(off, 16).unsqueeze(2).broadcast_to([128, 16, n])

    def whtj(off):
        return RR(off, 2048).rearrange("p (h t j) -> p h t j", t=32, j=4)

    def whbm(off):
        return RR(off, 2048).rearrange("p (h b m) -> p h b m", b=8, m=16)

    NBC = BCORE_NCH

    pl0 = sld.n
    P.dma("sync", K.A(AJ3, 128), K.cin["j3row"], sld)
    P.dma("sync", K.A(ABV12, 128), K.cin["bval12"], sld)
    P.dma("sync", K.A(AID32, 1024), K.cin["id32flat"], sld)
    P.dma("sync", K.mask8[:], K.cin["bdmask8"], sld)
    P.wait("gpsimd", sld, pl0 + 64)
    OP("gpsimd", lambda e: e.tensor_scalar_mul(K.A(ALF96, 16), K.scl[:, 144:160], 96.0), sgc)
    OP("gpsimd", lambda e: e.tensor_scalar_mul(K.A(ALF12, 16), K.scl[:, 144:160], 12.0), sgc)
    P.wait("vector", sld, pl0 + 64)
    lfrep128 = K.scl[:, 144:160].unsqueeze(2).broadcast_to([128, 16, 128])
    lfrep8 = K.scl[:, 144:160].unsqueeze(2).broadcast_to([128, 16, 8])
    j3rep = K.A(AJ3, 128).unsqueeze(1).broadcast_to([128, 16, 128])
    OP("vector", lambda e: e.tensor_mul(
        K.A(AJLF, 2048).rearrange("p (h m) -> p h m", m=128), j3rep, lfrep128), svc)
    OP("vector", lambda e: e.tensor_mul(
        K.A(ABV12, 128).rearrange("p (h b) -> p h b", b=8),
        K.A(ABV12, 128).rearrange("p (h b) -> p h b", b=8), lfrep8), svc)
    P.wait("vector", sgc)
    P.wait("vector", svc)
    OP("vector", lambda e: e.tensor_sub(
        K.A(ALFLR, 128).rearrange("p (h b) -> p h b", b=8),
        K.A(ALF96, 16).unsqueeze(2).broadcast_to([128, 16, 8]),
        K.A(ABV12, 128).rearrange("p (h b) -> p h b", b=8)), svc)
    OP("vector", lambda e: e.memset(K.A(ATMP + 144, 1), 40.0), svc)
    OP("vector", lambda e: e.memset(K.A(ATMP + 145, 1), -40.0), svc)
    OP("vector", lambda e: e.memset(RR(RS_, 2048), 0.0), svc)
    OP("vector", lambda e: e.memset(RR(RMQ_, 2048), 0.0), svc)
    OP("vector", lambda e: e.memset(K.A(AMRAW, 256), 0.0), svc)

    def load_chunk(c):
        mk = {}
        for hg in range(4):
            P.dma("sync", RR(RK_ + hg * 512, 512),
                  K.kb[4 * hg:4 * hg + 4, :, :, 32 * c:32 * c + 32]
                  .rearrange("h r p t -> p h r t"), sld)
        mk["k"] = sld.n
        for hg in range(4):
            P.dma("sync", RR(RV_ + hg * 512, 512),
                  K.vb[4 * hg:4 * hg + 4, :, :, 32 * c:32 * c + 32]
                  .rearrange("h r p t -> p h r t"), sld)
        mk["v"] = sld.n
        P.dma("sync", K.A(AQT, 512),
              K.qb[:, :, 32 * c:32 * c + 32].rearrange("h p t -> p h t"), sld)
        mk["q"] = sld.n
        P.dma("sync", K.A(AG_, 512),
              K.gb[:, :, 32 * c:32 * c + 32].rearrange("h p t -> p h t"), sld)
        mk["g"] = sld.n
        P.dma("sync", K.arena[0:16, ABLD:ABLD + 128], K.betab[c], sld)
        mk["b"] = sld.n
        return mk

    lmark = load_chunk(0)

    for c in range(NBC):
        if c > 0:
            P.barrier()
        P.wait("vector", sld, lmark["g"])
        for h in range(H_):
            OP("vector", lambda e, h=h: e.tensor_tensor_scan(
                K.A(ACUMG + 32 * h, 32), K.A(AG_ + 32 * h, 32),
                K.A(AG_ + 32 * h, 32), 0.0, ALU.add, ALU.bypass), svc)
        OP("gpsimd", lambda e: e.memset(K.A(ABSX, 128), 0.0), sgc)
        P.wait("gpsimd", svc)
        OP("gpsimd", lambda e: e.tensor_copy(
            K.A(ABSX, 128).rearrange("p (h b) -> p h b", b=8)[:, :, 1:8],
            cumg4[:, :, 0:7, 3]), sgc)
        OP("gpsimd", lambda e: e.tensor_copy(
            K.A(ACND, 128).rearrange("p (h b) -> p h b", b=8), cumg4[:, :, :, 3]), sgc)
        P.wait("vector", sgc)
        OP("vector", lambda e: e.tensor_sub(K.A(ABER, 128), K.A(ACND, 128),
                                              K.A(ABSX, 128)), svc)
        P.wait("vector", svc)
        OP("vector", lambda e: e.tensor_add(
            K.A(ABER, 128).rearrange("p (h b) -> p h b", b=8),
            K.A(ABER, 128).rearrange("p (h b) -> p h b", b=8),
            K.A(ALF12, 16).unsqueeze(2).broadcast_to([128, 16, 8])), svc)
        P.wait("scalar", svc)
        OP("scalar", lambda e: e.activation(K.A(AEBER, 128), K.A(ABER, 128), AF.Exp), ssc)
        OP("vector", lambda e: e.tensor_add(K.A(ATMP, 128), K.A(ABSX, 128),
                                              K.A(ABV12, 128)), svc)
        P.wait("scalar", svc)
        OP("scalar", lambda e: e.activation(K.A(AELGB, 128), K.A(ATMP, 128), AF.Exp), ssc)
        P.wait("gpsimd", ssc)
        OP("gpsimd", lambda e: e.tensor_scalar_mul(K.A(ANEGB, 128), K.A(AELGB, 128),
                                                     -1.0), sgc)
        OP("vector", lambda e: e.tensor_add(
            K.A(ATMP + 128, 16), cumg3[:, :, 31], K.A(ALF96, 16)), svc)
        P.wait("scalar", svc)
        OP("scalar", lambda e: e.activation(K.A(AELGL, 16), K.A(ATMP + 128, 16),
                                              AF.Exp), ssc)
        OP("vector", lambda e: e.tensor_sub(
            K.A(ALGL, 128).rearrange("p (h b) -> p h b", b=8),
            cumg3[:, :, 31].unsqueeze(2).broadcast_to([128, 16, 8]),
            K.A(ABSX, 128).rearrange("p (h b) -> p h b", b=8)), svc)
        P.wait("vector", svc)
        OP("vector", lambda e: e.tensor_add(K.A(ALGL, 128), K.A(ALGL, 128),
                                              K.A(ALFLR, 128)), svc)
        OP("vector", lambda e: e.tensor_copy(whtj(RZ_), cumg_rep), svc)
        P.wait("vector", svc)
        OP("vector", lambda e: e.tensor_sub(whbm(RZ_), whbm(RZ_), rep8(ABSX)), svc)
        P.wait("gpsimd", svc)
        OP("gpsimd", lambda e: e.tensor_add(RR(RZ_, 2048), RR(RZ_, 2048),
                                              K.A(AJLF, 2048)), sgc)
        P.wait("scalar", sgc)
        OP("scalar", lambda e: e.activation(RR(RE_, 2048), RR(RZ_, 2048), AF.Exp), ssc)
        P.wait("vector", sld, lmark["k"])
        P.wait("vector", ssc)
        OP("vector", lambda e: e.tensor_mul(whtj(RCSK_), kmic, whtj(RE_)), svc)
        P.wait("gpsimd", sld, lmark["q"])
        P.wait("gpsimd", ssc)
        OP("gpsimd", lambda e: e.tensor_mul(whtj(RCSQ_), qrep, whtj(RE_)), sgc)
        P.wait("scalar", svc)
        P.wait("scalar", sgc)
        OP("scalar", lambda e: e.activation(RR(RE_, 2048), RR(RZ_, 2048), AF.Exp,
                                              bias=K.A(ATMP + 144, 1)), ssc)
        P.wait("vector", ssc)
        OP("vector", lambda e: e.tensor_mul(whtj(RK40_), kmic, whtj(RE_)), svc)
        P.wait("gpsimd", ssc)
        OP("gpsimd", lambda e: e.tensor_mul(whtj(RQ40_), qrep, whtj(RE_)), sgc)
        P.wait("scalar", svc)
        P.wait("scalar", sgc)
        OP("scalar", lambda e: e.activation(RR(RE_, 2048), RR(RZ_, 2048), AF.Exp,
                                              scale=-1.0, bias=K.A(ATMP + 145, 1)), ssc)
        P.wait("vector", ssc)
        OP("vector", lambda e: e.tensor_mul(whtj(RPRK_), kmic, whtj(RE_)), svc)
        P.wait("gpsimd", svc)
        OP("gpsimd", lambda e: e.tensor_sub(whbm(RE_), rep8(ABER), whbm(RZ_)), sgc)
        P.wait("scalar", sgc)
        OP("scalar", lambda e: e.activation(RR(RE_, 2048), RR(RE_, 2048), AF.Exp), ssc)
        P.wait("vector", ssc)
        OP("vector", lambda e: e.tensor_mul(whtj(RPBK_), kmic, whtj(RE_)), svc)
        P.wait("gpsimd", svc)
        OP("gpsimd", lambda e: e.tensor_sub(whbm(RE_), rep8(ALGL), whbm(RZ_)), sgc)
        P.wait("scalar", sgc)
        OP("scalar", lambda e: e.activation(RR(RE_, 2048), RR(RE_, 2048), AF.Exp), ssc)
        P.wait("vector", ssc)
        OP("vector", lambda e: e.tensor_mul(whtj(RBB_), kmic, whtj(RE_)), svc)
        P.wait("gpsimd", svc)
        OP("gpsimd", lambda e: e.tensor_mul(whbm(RAN_), whbm(RCSK_), rep8(ANEGB)), sgc)
        OP("vector", lambda e: e.tensor_mul(whbm(RAQ_), whbm(RCSQ_), rep8(AELGB)), svc)
        P.wait("gpsimd", sld, lmark["v"])
        P.wait("gpsimd", svc)
        OP("gpsimd", lambda e: e.tensor_copy(whtj(RE_), vmicv), sgc)
        P.wait("tensor", sld, lmark["b"])
        OP("tensor", lambda e: e.matmul(K.ps[2][:, 0:16], K.arena[0:16, ABLD:ABLD + 128],
                                          K.cb["ident"][0:16, 0:16],
                                          start=True, stop=True), stc)
        P.wait("vector", stc)
        OP("vector", lambda e: e.tensor_copy(K.A(ABB, 16), K.ps[2][:, 0:16]), svc)
        P.wait("gpsimd", svc)
        OP("gpsimd", lambda e: e.tensor_scalar_mul(K.A(ANBB, 16), K.A(ABB, 16), -1.0), sgc)
        if c + 1 < NBC:
            P.wait("sync", svc)
            P.wait("sync", sgc)
            P.wait("sync", stc)
            lmark = load_chunk(c + 1)

        shipmark = {}
        for h in range(H_):
            if h == 0:
                P.wait("tensor", svc)
                P.wait("tensor", sgc)
            else:
                P.wait("tensor", ssc)
            OP("tensor", lambda e, h=h: e.matmul(
                K.ps[1][:, 0:256], RR(RPRK_ + h * 128, 128),
                RFULL[:, RK40_ // 2048:RK40_ // 2048 + 2, h * 128:(h + 1) * 128],
                start=True, stop=True), stc)
            P.wait("vector", stc)
            OP("vector", lambda e: e.memset(K.A(ABMAT, 128), 0.0), svc)
            for B in range(8):
                if B > 0:
                    P.wait("tensor", svc)
                    if B == 1 and h > 0:
                        P.wait("tensor", ssc)
                    OP("tensor", lambda e, h=h, B=B: e.matmul(
                        K.ps[0][:, 32 * B:32 * B + 32], K.A(ABMAT, 128),
                        RFULL[:, RCSK_ // 2048:RCSK_ // 2048 + 2,
                              h * 128 + 16 * B:h * 128 + 16 * B + 16],
                        start=True, stop=True), stc)
                if B < 7:
                    P.wait("vector", stc)
                    if B > 0:
                        OP("vector", lambda e, h=h, B=B: e.tensor_scalar(
                            K.A(ABMAT, 128), K.A(ABMAT, 128),
                            K.A(AEBER + h * 8 + B, 1), None, ALU.mult), svc)
                    OP("vector", lambda e, h=h, B=B: e.tensor_copy(
                        K.A(ABMAT + 16 * B, 16),
                        RR(RPBK_ + h * 128 + 16 * B, 16)), svc)
            P.wait("scalar", stc)
            P.wait("scalar", svc)
            if h > 0:
                for i4 in range(4):
                    P.wait("scalar", sxs[i4], shipmark[h - 1][i4])
            OP("scalar", lambda e: e.copy(K.A(AGRAM, 256), K.ps[1][:, 0:256]), ssc)
            if h > 1:
                for i4 in range(4):
                    P.wait("scalar", sxs[i4], shipmark[h - 2][i4])
            mrawo = AMRAW + (h % 2) * 128
            OP("scalar", lambda e, mrawo=mrawo: e.copy(
                K.A(mrawo + 16, 112).rearrange("p (b i) -> p b i", i=16),
                K.ps[0][:, 32:256].rearrange("p (b x) -> p b x", x=32)[:, :, 0:16]), ssc)
            OP("scalar", lambda e, h=h: e.copy(
                RR(RMQ_ + h * 128 + 16, 112).rearrange("p (b i) -> p b i", i=16),
                K.ps[0][:, 32:256].rearrange("p (b x) -> p b x", x=32)[:, :, 16:32]), ssc)
            P.wait("vector", ssc)
            OP("vector", lambda e, h=h: e.copy_predicated(
                RR(RMQ_ + h * 128, 128), K.mask8[:], K.A(AGRAM + 128, 128)), svc)
            OP("vector", lambda e, h=h, mrawo=mrawo: e.tensor_scalar(
                RR(RMT_ + h * 128, 128), K.A(mrawo, 128), K.A(ANBB + h, 1),
                None, ALU.mult), svc)
            P.wait("vector", svc)
            OP("vector", lambda e, h=h: e.tensor_scalar(
                RR(RMQ_ + h * 128, 128), RR(RMQ_ + h * 128, 128), K.A(ABB + h, 1),
                None, ALU.mult), svc)
            xv = K.xdiag[c, h].rearrange("i (j k) -> i j k", k=32)
            P.wait("sync", ssc)
            nship = 0
            for B in range(8):
                I, odd = B // 2, B % 2
                P.dma("sync", xv[I, 16 * odd:16 * odd + 16, 16 * odd:16 * odd + 16],
                      K.arena[16 * B:16 * B + 16, AGRAM + 16 * B:AGRAM + 16 * B + 16],
                      sxs[nship % 4])
                nship += 1
            for I in range(4):
                P.dma("sync", xv[I, 0:16, 16:32],
                      K.arena[32 * I:32 * I + 16, mrawo + 32 * I + 16:mrawo + 32 * I + 32],
                      sxs[nship % 4])
                nship += 1
            shipmark[h] = [sx.n for sx in sxs]

        P.wait("sync", stc)
        P.dma("sync", RR(RCSQ_, 2048, 0, 32),
              K.gateb[c // 4][32 * (c % 4):32 * (c % 4) + 32, :], sgt)
        gtmark = sgt.n

        for i4 in range(4):
            P.wait("sync", sxs[i4], shipmark[H_ - 1][i4])
        P.dma("sync", K.arena[0:64, ALB:ALB + 512],
              K.xdiag[c].rearrange("h i m -> (h i) m")[:, 0:512], sxb)
        P.dma("sync", K.arena[0:64, ALB:ALB + 1024]
              .rearrange("p (j k) -> p j k", k=32)[:, 16:32, 16:32],
              K.xdiag[c].rearrange("h i (j k) -> (h i) j k", k=32)[:, 16:32, 16:32], sxb)
        P.dma("sync", K.arena[0:64, ANBB2:ANBB2 + 32],
              K.betab[c].rearrange("h (i j) -> (h i) j", i=4), sxb)
        P.wait("vector", sxb)
        OP("vector", lambda e: e.tensor_scalar_mul(
            K.arena[0:64, ANBB2:ANBB2 + 32], K.arena[0:64, ANBB2:ANBB2 + 32], -1.0), svc)
        for j in range(31):
            OP("vector", lambda e, j=j: e.tensor_scalar(
                K.arena[0:64, ALB + j * 32 + j + 1:ALB + (j + 1) * 32],
                K.arena[0:64, ALB + j * 32 + j + 1:ALB + (j + 1) * 32],
                K.arena[0:64, ANBB2 + j:ANBB2 + j + 1], None, ALU.mult), svc)
        OP("vector", lambda e: e.tensor_copy(
            K.arena[0:64, AXT:AXT + 1024], K.arena[0:64, AID32:AID32 + 1024]), svc)
        xv3 = K.arena[0:64, AXT:AXT + 1024].rearrange("p (j i) -> p j i", i=32)
        for i in range(1, 32):
            for j in range(i):
                OP("vector", lambda e, i=i, j=j: e.scalar_tensor_tensor(
                    xv3[:, :, i], xv3[:, :, j],
                    K.arena[0:64, ALB + j * 32 + i:ALB + j * 32 + i + 1],
                    xv3[:, :, i], ALU.mult, ALU.add))
        OP("vector", lambda e: e.tensor_copy(
            K.arena[0:64, AXT:AXT + 1], K.arena[0:64, AXT:AXT + 1]), svc)
        P.wait("sync", svc)
        P.dma("sync", K.xinv[c].rearrange("h i m -> (h i) m"),
              K.arena[0:64, AXT:AXT + 1024], sxv)
        P.wait("sync", sxv)
        P.dma("sync", K.A(AXTA, 512).rearrange("p (h i) -> p h i", i=32),
              K.xinv[c].rearrange("h i (j k) -> (i j) h k", k=32), sxl)
        xlmark = sxl.n

        P.wait("tensor", sxl, xlmark)
        for h in range(H_):
            if h == 0:
                P.wait("tensor", sgc)
            P.wait("tensor", svc)
            OP("tensor", lambda e, h=h: e.matmul(
                K.ps[3][:, 0:128], RR(RE_ + h * 128, 128), K.cb["ident"][:],
                start=True, stop=False), stc)
            OP("tensor", lambda e, h=h: e.matmul(
                K.ps[3][:, 0:128], RR(RAN_ + h * 128, 128), RR(RS_ + h * 128, 128),
                start=False, stop=True), stc)
            for I in range(4):
                if I > 0:
                    P.wait("tensor", svc)
                    OP("tensor", lambda e, h=h, I=I: e.matmul(
                        K.ps[4][32 * I:32 * I + 32, 0:128],
                        K.R[0:32 * I, RMT_ + h * 128 + 32 * I:RMT_ + h * 128 + 32 * I + 32],
                        K.arena[0:32 * I, AW_:AW_ + 128],
                        start=True, stop=True, tile_position=(0, 32 * I)), stc)
                    P.wait("vector", stc)
                    OP("vector", lambda e, I=I: e.tensor_copy(
                        K.arena[32 * I:32 * I + 32, ARP:ARP + 128],
                        K.ps[3][32 * I:32 * I + 32, 0:128]), svc)
                    OP("vector", lambda e, I=I: e.tensor_add(
                        K.arena[32 * I:32 * I + 32, ARP:ARP + 128],
                        K.arena[32 * I:32 * I + 32, ARP:ARP + 128],
                        K.ps[4][32 * I:32 * I + 32, 0:128]), svc)
                else:
                    P.wait("vector", stc)
                    OP("vector", lambda e: e.tensor_copy(
                        K.arena[0:32, ARP:ARP + 128], K.ps[3][0:32, 0:128]), svc)
                P.wait("tensor", svc)
                OP("tensor", lambda e, h=h, I=I: e.matmul(
                    K.ps[5][32 * I:32 * I + 32, 0:128],
                    K.arena[32 * I:32 * I + 32,
                            AXTA + h * 32:AXTA + (h + 1) * 32],
                    K.arena[32 * I:32 * I + 32, ARP:ARP + 128],
                    start=True, stop=True, tile_position=(32 * I, 32 * I)), stc)
                P.wait("vector", stc)
                OP("vector", lambda e, I=I: e.tensor_copy(
                    K.arena[32 * I:32 * I + 32, AW_:AW_ + 128],
                    K.ps[5][32 * I:32 * I + 32, 0:128]), svc)
            P.wait("tensor", svc)
            OP("tensor", lambda e, h=h: e.matmul(
                K.ps[6][:, 0:128], RR(RAQ_ + h * 128, 128), RR(RS_ + h * 128, 128),
                start=True, stop=False), stc)
            OP("tensor", lambda e, h=h: e.matmul(
                K.ps[6][:, 0:128], RR(RMQ_ + h * 128, 128), K.A(AW_, 128),
                start=False, stop=True), stc)
            P.wait("scalar", stc)
            OP("scalar", lambda e: e.copy(K.A(AO_, 128), K.ps[6][:, 0:128]), ssc)
            if h < 4:
                slot, scol = 0, h * 128
            elif h < 8:
                slot, scol = 1, (h - 4) * 128
            elif h < 12:
                slot, scol = 2, (h - 8) * 128
            elif h < 15:
                slot, scol = 3, 128 + (h - 12) * 128
            else:
                slot, scol = 7, 384
            P.wait("tensor", ssc)
            OP("tensor", lambda e, h=h, slot=slot, scol=scol: e.matmul(
                K.ps[slot][0:32, scol:scol + 128],
                K.scl[:, 960 + h * 32:960 + (h + 1) * 32], K.A(AO_, 128),
                start=True, stop=True), stc)
            P.wait("tensor", svc)
            OP("tensor", lambda e, h=h: e.matmul(
                K.ps[7][:, 0:128], RR(RBB_ + h * 128, 128), K.cb["ident"][:],
                start=True, stop=True), stc)
            P.wait("vector", stc)
            OP("vector", lambda e, h=h: e.tensor_scalar(
                K.A(ABBAR, 128), K.ps[7][:, 0:128], K.A(ABB + h, 1), None, ALU.mult), svc)
            P.wait("tensor", svc)
            OP("tensor", lambda e: e.matmul(
                K.ps[7][:, 128:256], K.A(ABBAR, 128), K.A(AW_, 128),
                start=True, stop=True), stc)
            P.wait("vector", stc)
            OP("vector", lambda e, h=h: e.tensor_scalar(
                RR(RS_ + h * 128, 128), RR(RS_ + h * 128, 128), K.A(AELGL + h, 1),
                None, ALU.mult), svc)
            P.wait("vector", svc)
            OP("vector", lambda e, h=h: e.tensor_add(
                RR(RS_ + h * 128, 128), RR(RS_ + h * 128, 128), K.ps[7][:, 128:256]), svc)

        P.wait("scalar", stc)
        OP("scalar", lambda e: e.copy(RR(RAN_, 512, 0, 32), K.ps[0][0:32, 0:512]), ssc)
        OP("scalar", lambda e: e.copy(RR(RAN_ + 512, 512, 0, 32), K.ps[1][0:32, 0:512]), ssc)
        OP("scalar", lambda e: e.copy(RR(RAN_ + 1024, 512, 0, 32), K.ps[2][0:32, 0:512]), ssc)
        OP("scalar", lambda e: e.copy(RR(RAN_ + 1536, 384, 0, 32),
                                        K.ps[3][0:32, 128:512]), ssc)
        OP("scalar", lambda e: e.copy(RR(RAN_ + 1920, 128, 0, 32),
                                        K.ps[7][0:32, 384:512]), ssc)
        P.wait("scalar", ssc)
        OP("scalar", lambda e: e.activation(RR(RAQ_, 2048, 0, 32), RR(RAN_, 2048, 0, 32),
                                              AF.Square), ssc)
        P.wait("vector", ssc)
        for h in range(H_):
            OP("vector", lambda e, h=h: e.reduce_sum(
                K.arena[0:32, ATMP + h:ATMP + h + 1], RR(RAQ_ + h * 128, 128, 0, 32),
                mybir.AxisListType.X), svc)
        P.wait("vector", svc)
        OP("vector", lambda e: e.tensor_scalar(
            K.arena[0:32, ATMP + 16:ATMP + 32], K.arena[0:32, ATMP:ATMP + 16], 1.0 / 128, 1e-5,
            ALU.mult, ALU.add), svc)
        P.wait("scalar", svc)
        OP("scalar", lambda e: e.activation(K.arena[0:32, ATMP + 32:ATMP + 48],
                                              K.arena[0:32, ATMP + 16:ATMP + 32], AF.Ln), ssc)
        P.wait("scalar", ssc)
        OP("scalar", lambda e: e.activation(K.arena[0:32, ATMP + 48:ATMP + 64],
                                              K.arena[0:32, ATMP + 32:ATMP + 48], AF.Exp,
                                              scale=-0.5), ssc)
        P.wait("vector", ssc)
        OP("vector", lambda e: e.tensor_mul(K.arena[0:32, ATMP + 64:ATMP + 80],
                                              K.arena[0:32, ATMP + 48:ATMP + 64],
                                              K.arena[0:32, ATMP + 48:ATMP + 64]), svc)
        P.wait("vector", svc)
        OP("vector", lambda e: e.tensor_mul(K.arena[0:32, ATMP + 64:ATMP + 80],
                                              K.arena[0:32, ATMP + 64:ATMP + 80],
                                              K.arena[0:32, ATMP + 16:ATMP + 32]), svc)
        P.wait("vector", svc)
        OP("vector", lambda e: e.tensor_scalar(K.arena[0:32, ATMP + 64:ATMP + 80],
                                                 K.arena[0:32, ATMP + 64:ATMP + 80],
                                                 -0.5, 1.5, ALU.mult, ALU.add), svc)
        P.wait("vector", svc)
        OP("vector", lambda e: e.tensor_mul(K.arena[0:32, ATMP + 48:ATMP + 64],
                                              K.arena[0:32, ATMP + 48:ATMP + 64],
                                              K.arena[0:32, ATMP + 64:ATMP + 80]), svc)
        P.wait("vector", svc)
        for h in range(H_):
            OP("vector", lambda e, h=h: e.scalar_tensor_tensor(
                RR(RAN_ + h * 128, 128, 0, 32), RR(RAN_ + h * 128, 128, 0, 32),
                K.arena[0:32, ATMP + 48 + h:ATMP + 49 + h], K.scl[0:32, 784:912],
                ALU.mult, ALU.mult), svc)
        P.wait("gpsimd", svc)
        P.wait("gpsimd", sgt, gtmark)
        for h in range(H_):
            OP("gpsimd", lambda e, h=h: e.tensor_mul(
                RR(RAN_ + h * 128, 128, 0, 32), RR(RAN_ + h * 128, 128, 0, 32),
                RR(RCSQ_ + h * 128, 128, 0, 32)), sgc)
        P.wait("tensor", sgc)
        P.wait("tensor", ssc)
        for h in range(H_):
            OP("tensor", lambda e, h=h: e.matmul(
                K.ps[3][:, h * 32:(h + 1) * 32], RR(RAN_ + h * 128, 128, 0, 32),
                K.cb["ident"][0:32, 0:32], start=True, stop=True), stc)
        P.wait("scalar", stc)
        OP("scalar", lambda e: e.copy(K.A(AOBST, 512), K.ps[3][:, 0:512]), ssc)
        P.wait("sync", ssc)
        P.dma("sync", K.ob.rearrange("h p t -> p h t")[:, :, 32 * c:32 * c + 32],
              K.A(AOBST, 512).rearrange("p (h t) -> p h t", t=32), sob)
    P.barrier()


def emit_attnb_proj(K, parts=None):
    if parts is None:
        parts = BPROJ_PARTS
    emit_attnb_prep(K)
    P = K.P
    sbst = P.sem("d_bst")
    stb = P.sem("t_bproj")
    svb = P.sem("v_bproj")
    ssb = P.sem("s_bproj")
    sgb = P.sem("g_bproj")
    Y1 = K.A(12288, 2048)
    acc = K.A(14336, 2048)
    sq = K.A(16384, 2048)
    n2 = K.arena[0:1, 12288 + 6144:12288 + 8192]
    r0 = K.arena[0:1, 16384:18432][0:1, :]
    trow = K.arena[0:1, 14336:16384][0:1, :]
    n2 = K.arena[0:1, 18432:20480]
    r_recip = K.arena[0:1, 16384:18432]
    y_row = K.arena[0:1, 14336 + 0:14336 + 2048]
    y_row = K.arena[0:1, 12288:14336]
    t_row = K.arena[0:1, 12288:14336]

    P.wait("vector", P.sem("d_bprep"))

    def post_conv_norm(oc, conv_col0, dst, norm, qscale, chain_done):
        slot = oc % 2
        st = K.stage[:, slot * T:(slot + 1) * T]
        P.wait("vector", chain_done)
        P.wait("vector", P.sem("ev_done"))
        P.op("vector", lambda e: e.tensor_scalar(
            acc, st, K.scl[:, conv_col0 + 3:conv_col0 + 4], None, ALU.mult), svb)
        for sh in range(1, 4):
            P.wait("vector", svb)
            P.op("vector", lambda e, sh=sh: e.scalar_tensor_tensor(
                acc[:, sh:T], st[:, 0:T - sh],
                K.scl[:, conv_col0 + 3 - sh:conv_col0 + 4 - sh],
                acc[:, sh:T], ALU.mult, ALU.add), svb)
        P.wait("scalar", svb)
        P.op("scalar", lambda e: e.activation(acc, acc, AF.Silu), ssb)
        if norm:
            P.wait("scalar", ssb)
            P.op("scalar", lambda e: e.activation(sq, acc, AF.Square), ssb)
            P.wait("tensor", ssb)
            for qq in range(4):
                P.op("tensor", lambda e, qq=qq: e.matmul(
                    K.ps[4 + qq][0:1, :], K.cb["ones_col"][:],
                    sq[:, qq * 512:(qq + 1) * 512], start=True, stop=True),
                    stb if qq == 3 else None)
            P.wait("vector", stb)
            for qq in range(4):
                P.op("vector", lambda e, qq=qq: e.tensor_copy(
                    n2[:, qq * 512:(qq + 1) * 512], K.ps[4 + qq][0:1, :]), svb)
            P.wait("vector", svb)
            P.op("vector", lambda e: e.tensor_scalar_add(n2, n2, 1e-6), svb)
            P.wait("scalar", svb)
            P.op("scalar", lambda e: e.activation(r_recip, n2, AF.Ln), ssb)
            P.wait("scalar", ssb)
            P.op("scalar", lambda e: e.activation(y_row, r_recip, AF.Exp, scale=-0.5), ssb)
            P.wait("vector", ssb)
            P.op("vector", lambda e: e.tensor_mul(r_recip, y_row, y_row), svb)
            P.wait("vector", svb)
            P.op("vector", lambda e: e.tensor_mul(r_recip, r_recip, n2), svb)
            P.wait("vector", svb)
            P.op("vector", lambda e: e.tensor_scalar(r_recip, r_recip, -0.5, 1.5,
                                                     ALU.mult, ALU.add), svb)
            P.wait("vector", svb)
            P.op("vector", lambda e: e.tensor_mul(y_row, y_row, r_recip), svb)
            if qscale is not None:
                P.wait("vector", svb)
                P.op("vector", lambda e: e.tensor_scalar_mul(y_row, y_row, qscale), svb)
            P.wait("tensor", svb)
            for qq in range(4):
                P.op("tensor", lambda e, qq=qq: e.matmul(
                    K.ps[4 + qq][:, 0:512], K.cb["ones_row"][0:1, :],
                    y_row[:, qq * 512:(qq + 1) * 512], start=True, stop=True),
                    stb if qq == 3 else None)
            P.wait("vector", stb)
            for qq in range(4):
                P.op("vector", lambda e, qq=qq: e.tensor_mul(
                    acc[:, qq * 512:(qq + 1) * 512], acc[:, qq * 512:(qq + 1) * 512],
                    K.ps[4 + qq][:, 0:512]), svb if qq == 3 else None)
            P.wait("sync", svb)
        else:
            P.wait("sync", ssb)
        P.dma("sync", dst, acc, sbst)

    def mk_evac(kind):
        def evac(oc, tt, bank, mmw, done):
            slot = oc % 2
            st = K.stage[:, slot * T + tt * 512: slot * T + (tt + 1) * 512]
            if tt == 0:
                P.wait("vector", sbst)
            P.wait("vector", *mmw)
            P.op("vector", lambda e, st=st, bank=bank: e.tensor_copy(st, bank[:]), done)
            if tt == 3:
                if kind == "k":
                    post_conv_norm(oc, 272 + oc * 4, K.kb[oc // 4, oc % 4], True, None, sbst)
                elif kind == "v":
                    post_conv_norm(oc, 528 + oc * 4, K.vb[oc // 4, oc % 4], False, None, sbst)
                elif kind == "q":
                    post_conv_norm(oc, 208 + oc * 4, K.qb[oc], True, float(DH) ** -0.5, sbst)
        return evac

    if "kvq" in parts:
        emit_proj(K, K.win["L0_attn_b_k_proj_w"], 64, mk_evac("k"))
        emit_proj(K, K.win["L0_attn_b_v_proj_w"], 64, mk_evac("v"))
        emit_proj(K, K.win["L0_attn_b_q_proj_w"], 16, mk_evac("q"))
    P.barrier()

    def f1_evac(oc, tt, bank, mmw, done):
        P.wait("vector", *mmw)
        P.op("vector", lambda e, tt=tt, bank=bank: e.tensor_copy(
            Y1[:, tt * 512:(tt + 1) * 512], bank[:]), done)
    if "g" in parts:
        emit_proj(K, K.win["L0_attn_b_f1_w"], 1, f1_evac)
    P.barrier()

    def f2_evac(oc, tt, bank, mmw, done):
        slot = oc % 2
        st = K.stage[:, slot * T + tt * 512: slot * T + (tt + 1) * 512]
        if tt == 0:
            P.wait("vector", sbst)
        P.wait("vector", *mmw)
        P.op("vector", lambda e, st=st, bank=bank, oc=oc: e.tensor_scalar(
            st, bank[:], K.scl[:, 128 + oc:129 + oc], None, ALU.add), done)
        if tt == 3:
            stg = K.stage[:, slot * T:(slot + 1) * T]
            P.wait("scalar", sbst)
            P.wait("scalar", P.sem("ev_done"))
            P.op("scalar", lambda e, stg=stg: e.activation(sq, stg, AF.Abs), ssb)
            P.wait("scalar", ssb)
            P.op("scalar", lambda e: e.activation(sq, sq, AF.Exp, scale=-1.0), ssb)
            P.wait("scalar", ssb)
            P.op("scalar", lambda e: e.activation(sq, sq, AF.Ln, bias=1.0), ssb)
            P.wait("vector", ssb)
            P.op("vector", lambda e, stg=stg: e.tensor_scalar_max(acc, stg, 0.0), svb)
            P.wait("vector", svb)
            P.op("vector", lambda e: e.tensor_add(acc, acc, sq), svb)
            P.wait("vector", svb)
            P.op("vector", lambda e, oc=oc: e.tensor_scalar(
                acc, acc, K.scl[:, 160 + oc:161 + oc], None, ALU.mult), svb)
            P.wait("sync", svb)
            P.dma("sync", K.gb[oc], acc, sbst)
    if "g" in parts:
        emit_proj(K, K.win["L0_attn_b_f2_w"], 16, f2_evac, din=128,
                  rhs_fn=lambda kc, tsl: Y1[:, tsl])
    P.barrier()

    if "gate" not in parts and "beta" not in parts:
        return
    if "gate" in parts:
        emit_proj(K, K.win["L0_attn_b_g1_w"], 1, f1_evac)
    P.barrier()
    sld2 = P.sem("d_bld")
    if "gate" not in parts:
        emit_beta_part(K, P, sld2, sbst, stb, svb, ssb, Y1)
        P.barrier()
        return
    P.dma("sync", K.arena[0:1, 16384:18432], K.win["L0_attn_b_g2_b"].unsqueeze(0), sld2)
    P.dma("sync", K.wnat[:, 0:2048],
          K.win["L0_attn_b_g2_w"].rearrange("(c p) n -> p c n", p=128), sld2)
    ld2 = sld2.n
    P.wait("tensor", sld2, ld2)
    for qq in range(4):
        P.op("tensor", lambda e, qq=qq: e.matmul(
            K.ps[4][:, 0:512], K.cb["ones_row"][0:1, :],
            K.arena[0:1, 16384 + qq * 512:16384 + (qq + 1) * 512],
            start=True, stop=True), stb)
        P.wait("vector", stb)
        P.op("vector", lambda e, qq=qq: e.tensor_copy(
            acc[:, qq * 512:(qq + 1) * 512], K.ps[4][:, 0:512]), svb)
        P.wait("tensor", svb)
    for c in range(16):
        bank = K.ps[2 + c % 2]
        P.op("tensor", lambda e, c=c, bank=bank: e.matmul(
            bank[:, 0:128], K.wnat[:, c * 128:(c + 1) * 128], K.cb["ident"][:],
            start=True, stop=True), stb)
        P.wait("vector", stb)
        P.op("vector", lambda e, c=c, bank=bank: e.tensor_copy(
            K.wT[:, c * 128:(c + 1) * 128], bank[:, 0:128]), svb)
        P.wait("tensor", svb)
    svb0 = svb.n
    ssb0 = ssb.n
    for tt in range(16):
        slot = tt % 2
        for oc in range(16):
            gi = tt * 16 + oc
            bank = K.ps[gi % 2]
            if gi >= 2:
                P.wait("tensor", svb, svb0 + gi - 1)
            P.op("tensor", lambda e, tt=tt, oc=oc, bank=bank: e.matmul(
                bank[:, 0:128], Y1[:, tt * 128:(tt + 1) * 128],
                K.wT[:, oc * 128:(oc + 1) * 128], start=True, stop=True), stb)
            st = K.stage[:, slot * 2048 + oc * 128: slot * 2048 + (oc + 1) * 128]
            if oc == 0 and tt >= 2:
                P.wait("vector", sbst)
            P.wait("vector", stb)
            P.op("vector", lambda e, st=st, bank=bank, oc=oc: e.tensor_add(
                st, bank[:, 0:128], acc[:, oc * 128:(oc + 1) * 128]), svb)
            P.wait("scalar", svb, svb0 + gi + 1)
            P.op("scalar", lambda e, st=st: e.activation(st, st, AF.Sigmoid), ssb)
        P.wait("sync", ssb, ssb0 + 16 * (tt + 1))
        P.dma("sync", K.gateb[tt], K.stage[:, slot * 2048:(slot + 1) * 2048], sbst)
    P.barrier()

    if "beta" in parts:
        emit_beta_part(K, P, sld2, sbst, stb, svb, ssb, Y1)
    P.barrier()


def emit_beta_part(K, P, sld2, sbst, stb, svb, ssb, Y1):
    P.dma("sync", K.wnat[0:64, 0:2048], K.win["L0_attn_b_b_proj_w"], sld2)
    for cc4 in range(4):
        P.dma("sync", Y1[32 * cc4:32 * cc4 + 32, 0:512],
              K.cin["expand"].rearrange("j t m -> t j m"), sld2)
    ld3 = sld2.n
    P.wait("tensor", sld2, ld3)
    for kc in range(16):
        bank = K.ps[2 + kc % 2]
        P.op("tensor", lambda e, kc=kc, bank=bank: e.matmul(
            bank[:, 0:64], K.wnat[0:64, kc * 128:(kc + 1) * 128],
            K.cb["ident"][0:64, 0:64], start=True, stop=True), stb)
        P.wait("vector", stb)
        P.op("vector", lambda e, kc=kc, bank=bank: e.tensor_copy(
            K.wT[:, kc * 64:(kc + 1) * 64], bank[:, 0:64]), svb)
        P.wait("tensor", svb)
    btok = K.A(16384, 64)
    svb0 = svb.n
    for tt in range(16):
        bank = K.ps[tt % 2]
        for kc in range(16):
            P.op("tensor", lambda e, tt=tt, kc=kc, bank=bank: e.matmul(
                bank[:, 0:64], K.Rc(kc)[:, tt * 128:(tt + 1) * 128].rearrange("p t -> p t")
                if False else K.Rc(kc)[:, tt * 128:(tt + 1) * 128],
                K.wT[:, kc * 64:(kc + 1) * 64], start=True, stop=True)
                if False else e.matmul(
                bank[:, 0:64], K.Rc(kc)[:, tt * 128:(tt + 1) * 128],
                K.wT[:, kc * 64:(kc + 1) * 64], start=(kc == 0), stop=(kc == 15)),
                stb if kc == 15 else None)
        P.wait("scalar", stb)
        P.op("scalar", lambda e, bank=bank: e.activation(btok, bank[:, 0:64], AF.Sigmoid), ssb)
        P.wait("tensor", ssb)
        for cc in range(4):
            c = tt * 4 + cc
            bank2 = K.ps[4 + cc % 2]
            if c >= 2:
                P.wait("tensor", svb, svb0 + c - 1)
            for j in range(4):
                P.op("tensor", lambda e, cc=cc, j=j, bank2=bank2: e.matmul(
                    bank2[0:16, 0:128],
                    btok[32 * cc:32 * cc + 32, :].rearrange("t (h j) -> t j h", j=4)[:, j, :],
                    Y1[32 * cc:32 * cc + 32, j * 128:(j + 1) * 128],
                    start=(j == 0), stop=(j == 3),
                    tile_position=(32 * cc, 0)),
                    stb if j == 3 else None)
            P.wait("vector", stb)
            st2 = K.stage[:, 2048 + cc * 128:2048 + (cc + 1) * 128]
            P.wait("vector", sbst)
            P.op("vector", lambda e, bank2=bank2, st2=st2: e.tensor_copy(
                st2[0:16, :], bank2[0:16, 0:128]), svb)
            P.wait("sync", svb)
            P.dma("sync", K.betab[c], st2[0:16, :], sbst)
    P.barrier()


PHASES = ["const", "x", "ln1", "aproj", "rope", "acore", "aout",
          "ln2", "ffn1", "ln3", "bproj", "bcore", "bout", "ln4", "ffn2", "store"]


def build(dumps=(), phases=None):
    K = MK(dumps=dumps)
    ph = PHASES if phases is None else phases
    emit_load_const(K)
    if "x" in ph:
        emit_load_x(K)
    K.P.barrier()
    if "ln1" in ph:
        emit_ln(K)
    if "aproj" in ph:
        emit_attna_proj(K)
    if "rope" in ph:
        emit_rope(K)
    if "acore" in ph:
        emit_attna_core(K)
    if "aout" in ph:
        emit_out_proj(K, "L0_attn_a_wo", K.oa, 16)
    if "ln2" in ph:
        emit_ln(K)
    if "ffn1" in ph:
        emit_ffn(K)
    if "ln3" in ph:
        emit_ln(K)
    if "bproj" in ph:
        emit_attnb_proj(K)
    if "bcore" in ph:
        emit_attnb_core(K)
    if "bout" in ph:
        emit_out_proj(K, "L0_attn_b_o_proj_w", K.ob, 16)
    if "ln4" in ph:
        emit_ln(K)
    if "ffn2" in ph:
        emit_ffn(K)
    if "store" in ph:
        emit_final_store(K)
    if dumps:
        emit_dumps(K)
    K.P.build()
    return K


def kernel(**inputs):
    K = build()
    consts = K.consts
    in_maps = []
    x = inputs["L0_attn_a_x"]
    for b in range(NCORES):
        m = {"x": np.ascontiguousarray(x[b])}
        for n, _ in WEIGHT_SPECS:
            m[n] = np.ascontiguousarray(inputs[n].reshape(
                dict(WEIGHT_SPECS)[n]))
        for n, _ in CONST_SPECS:
            m["ci_" + n] = consts[n]
        in_maps.append(m)
    res = run_bass_kernel_spmd(K.nc, in_maps, core_ids=list(range(NCORES)))
    return np.stack([r["out"] for r in res.results], axis=0)


# version 05
# speedup vs baseline: 1.4528x; 1.2299x over previous
import numpy as np
import concourse.bass as bass
import concourse.mybir as mybir
from concourse.bass_utils import run_bass_kernel_spmd

F32 = mybir.dt.float32
BF16 = mybir.dt.bfloat16
U8 = mybir.dt.uint8
AF = mybir.ActivationFunctionType
ALU = mybir.AluOpType

D = 2048
T = 2048
NCH = 16
H, HKV, DH = 16, 4, 128
ACH = 64
NACH = T // ACH
EPS_LN = 1e-5
NCORES = 8
BPROJ_PARTS = ("kvq", "g", "gate", "beta")
PREP_N = 99
PREP_DMAS = None
BCORE_NCH = 64


class Sem:
    def __init__(self, handle):
        self.h = handle
        self.n = 0

    def inc(self, k=1):
        self.n += k
        return self.n


class Prog:
    ENGS = ("sync", "tensor", "vector", "scalar", "gpsimd")

    def __init__(self, nc):
        self.nc = nc
        self.q = {e: [] for e in self.ENGS}
        self._sems = {}

    def sem(self, name):
        if name not in self._sems:
            h = self.nc.semaphore(name).__enter__()
            self._sems[name] = Sem(h)
        return self._sems[name]

    def emit(self, eng, fn):
        self.q[eng].append(fn)

    def wait(self, eng, sem, val=None):
        v = sem.n if val is None else val
        if v <= 0:
            return
        self.emit(eng, lambda e, sh=sem.h, v=v: e.wait_ge(sh, v))

    def dma(self, eng, out, in_, sem, **kw):
        if sem.n > 0:
            self.wait(eng, sem, sem.n)
        tgt = sem.inc(16)
        self.emit(eng, lambda e, out=out, in_=in_, sh=sem.h, kw=kw:
                  e.dma_start(out=out, in_=in_, **kw).then_inc(sh, 16))
        return tgt

    def op(self, eng, fn, sem=None):
        if sem is None:
            self.emit(eng, fn)
            return None
        tgt = sem.inc(1)
        self.emit(eng, lambda e, fn=fn, sh=sem.h: fn(e).then_inc(sh, 1))
        return tgt

    def spc(self, eng, n=2):
        for _ in range(n):
            if eng == "scalar":
                self.emit(eng, lambda e: e.copy(self.spc_dst, self.spc_src))
            else:
                self.emit(eng, lambda e: e.tensor_copy(self.spc_dst, self.spc_src))

    def barrier(self):
        for name, s in list(self._sems.items()):
            if name.startswith("d_"):
                self.wait("sync", s)
        sems = {e: self.sem(f"bar_{e}") for e in self.ENGS}
        for e in self.ENGS:
            s = sems[e]
            s.inc(1)
            if e == "sync":
                self.emit(e, lambda en, sh=s.h: en.sem_inc(sh, 1))
            else:
                def f(en, sh=s.h):
                    en.drain()
                    en.sem_inc(sh, 1)
                self.emit(e, f)
        for e in self.ENGS:
            for o in self.ENGS:
                if o != e:
                    self.wait(e, sems[o])

    def build(self):
        block = self.nc.Block().__enter__()
        for eng in self.ENGS:
            thunks = self.q[eng]
            if not thunks:
                continue

            def body(e, thunks=thunks):
                for f in thunks:
                    f(e)
            getattr(block, eng)(body)
        block.__exit__(None, None, None)


def make_constants():
    c = {}
    c["ident"] = np.eye(128, dtype=np.float32)
    c["ones_col"] = np.ones((128, 1), np.float32)
    c["ones_row"] = np.ones((1, 128), np.float32)
    prot = np.zeros((128, 128), np.float32)
    for p in range(64):
        prot[p, p + 64] = -1.0
        prot[p + 64, p] = 1.0
    c["protT"] = np.ascontiguousarray(prot.T)
    inv = 1.0 / 10000.0 ** (np.arange(0, DH, 2, dtype=np.float32) / DH)
    fr = np.arange(T, dtype=np.float32)[:, None] * inv[None, :]
    c["rope_cos"] = np.ascontiguousarray(
        np.concatenate([np.cos(fr), np.cos(fr)], -1).T, np.float32)
    c["rope_sin"] = np.ascontiguousarray(
        np.concatenate([np.sin(fr), np.sin(fr)], -1).T, np.float32)
    lam = np.exp2(-8.0 * np.arange(1, H + 1, dtype=np.float32) / H)
    r = np.arange(ACH, dtype=np.float32)
    diff = r[:, None] - r[None, :]
    maskT = np.zeros((H, ACH, ACH), np.float32)
    for h in range(H):
        maskT[h] = np.where(diff >= 0, np.exp(-lam[h] * diff), 0.0).T
    c["a_maskT"] = maskT
    c["a_qdec"] = np.ascontiguousarray(
        np.broadcast_to(np.exp(-lam[:, None] * (r[None, :] + 1.0))[:, None, :],
                        (H, 128, ACH)), np.float32)
    c["a_kdec"] = np.ascontiguousarray(
        np.exp(-lam[:, None] * (ACH - 1.0 - r[None, :])).astype(np.float32).T)
    c["a_blk"] = [float(x) for x in np.exp(-lam * ACH)]
    ex = np.zeros((4, 32, 128), np.float32)
    for j in range(4):
        for t in range(32):
            ex[j, t, 4 * t + j] = 1.0
    c["expand"] = ex
    c["jrow"] = np.broadcast_to((np.arange(128) % 4).astype(np.float32)[None, :],
                                (128, 128)).copy()
    c["divsel"] = np.eye(8, dtype=np.float32)[np.arange(128) // 16]
    c["modsel"] = np.eye(16, dtype=np.float32)[np.arange(128) % 16]
    c["id32flat"] = np.broadcast_to(np.eye(32, dtype=np.float32).reshape(1, 1024),
                                    (128, 1024)).copy()
    c["j4"] = np.eye(4, dtype=np.float32)[np.arange(128) % 4].T.copy()
    c["expT"] = np.eye(32, dtype=np.float32)[np.arange(128) // 4].copy()
    m = np.arange(128)
    c["j3row"] = np.broadcast_to((3.0 * ((m // 4) % 4) + (m % 4)).astype(np.float32)[None, :],
                                 (128, 128)).copy()
    bd = np.zeros((128, 128), np.uint8)
    for b in range(8):
        for jl in range(16):
            bd[16*b+jl, 16*b+jl:16*b+16] = 1
    c["bdmask8"] = bd
    c["bval12"] = np.broadcast_to((12.0 * (np.arange(128) % 8)).astype(np.float32)[None, :],
                                  (128, 128)).copy()
    emT = np.zeros((8, 128, 16), np.float32)
    for B in range(8):
        for jl in range(16):
            emT[B, 16 * B + jl, jl] = 1.0
    c["emb16T"] = emT
    emb = np.zeros((8, 16, 128), np.float32)
    for B in range(8):
        for jl in range(16):
            emb[B, jl, 16 * B + jl] = 1.0
    c["emb16"] = emb
    return c


CONST_SPECS = [("ident", [128, 128]), ("ones_col", [128, 1]), ("ones_row", [1, 128]),
               ("protT", [128, 128]), ("rope_cos", [128, T]), ("rope_sin", [128, T]),
               ("a_maskT", [H, ACH, ACH]), ("a_qdec", [H, 128, ACH]),
               ("a_kdec", [ACH, H]),
               ("expand", [4, 32, 128]), ("jrow", [128, 128]),
               ("divsel", [128, 8]), ("modsel", [128, 16]),
               ("id32flat", [128, 1024]), ("emb16", [8, 16, 128]),
               ("j4", [4, 128]), ("expT", [128, 32]),
               ("j3row", [128, 128]), ("bval12", [128, 128]), ("bdmask8", [128, 128]),
               ("emb16T", [8, 128, 16])]

WEIGHT_SPECS = [
    ("L0_attn_a_wq", [D, D]), ("L0_attn_a_wk", [512, D]), ("L0_attn_a_wv", [512, D]),
    ("L0_attn_a_wo", [D, D]),
    ("L0_attn_b_q_proj_w", [D, D]), ("L0_attn_b_k_proj_w", [4 * D, D]),
    ("L0_attn_b_v_proj_w", [4 * D, D]), ("L0_attn_b_q_conv_w", [D, 4]),
    ("L0_attn_b_k_conv_w", [4 * D, 4]), ("L0_attn_b_v_conv_w", [4 * D, 4]),
    ("L0_attn_b_f1_w", [128, D]), ("L0_attn_b_f2_w", [D, 128]),
    ("L0_attn_b_b_proj_w", [64, D]), ("L0_attn_b_micro_logits", [16, 4]),
    ("L0_attn_b_A_log", [16]), ("L0_attn_b_dt_bias", [D]),
    ("L0_attn_b_g1_w", [128, D]), ("L0_attn_b_g2_w", [D, 128]),
    ("L0_attn_b_g2_b", [D]), ("L0_attn_b_o_norm_w", [128]),
    ("L0_attn_b_o_proj_w", [D, D]),
    ("L0_ffn_up_weight", [4 * D, D]), ("L0_ffn_down_weight", [D, 4 * D]),
    ("L0_ffn_up_scale", [4 * D]), ("L0_ffn_down_scale", [D]),
    ("L0_ffn_output_scale", [1]),
]


class MK:
    def __init__(self, dumps=()):
        nc = bass.Bass("TRN2", target_bir_lowering=False, dynamic_dma_scratch_size=4096)
        self.nc = nc
        self.P = Prog(nc)
        self.consts = make_constants()
        self._guards = []

        self.x_in = nc.dram_tensor("x", [T, D], F32, kind="ExternalInput").ap()
        self.out = nc.dram_tensor("out", [T, D], F32, kind="ExternalOutput").ap()
        self.win = {n: nc.dram_tensor(n, s, F32, kind="ExternalInput").ap()
                    for n, s in WEIGHT_SPECS}
        self.cin = {n: nc.dram_tensor("ci_" + n, s,
                                      mybir.dt.uint8 if n == "bdmask8" else F32,
                                      kind="ExternalInput").ap()
                    for n, s in CONST_SPECS}

        def scr(name, shape):
            return nc.dram_tensor(name, shape, F32, kind="Internal").ap()
        self.scr = scr
        self.r_spill = scr("r_spill", [NCH, 128, T])
        self.r_new = scr("r_new", [NCH, 128, T])
        self.hidb = nc.dram_tensor("hidb", [64, 128, T], BF16, kind="Internal").ap()
        self.qa = scr("qa", [H, 128, T])
        self.ka = scr("ka", [HKV, 128, T])
        self.va = scr("va", [HKV, 128, T])
        self.oa = scr("oa", [H, 128, T])
        self.hid = scr("hid", [64, 128, T])
        self.ffn_part = scr("ffn_part", [NCH, 128, T])
        self.kb = scr("kb", [H, 4, 128, T])
        self.vb = scr("vb", [H, 4, 128, T])
        self.qb = scr("qb", [H, 128, T])
        self.gb = scr("gb", [H, 128, T])
        self.gateb = scr("gateb", [NCH, 128, H * 128])
        self.betab = scr("betab", [T // 32, 16, 128])
        self.xdiag = scr("xdiag", [T // 32, H, 4, 1024])
        self.xinv = scr("xinv", [T // 32, H, 4, 1024])
        self.ob = scr("ob", [H, 128, T])
        self.dumps = {}
        self.sbuf_dumps = []
        for name in dumps:
            if name == "scl":
                self.dumps["scl"] = nc.dram_tensor("dbg_scl", [128, 256], F32,
                                                   kind="ExternalOutput").ap()
                self.sbuf_dumps.append("scl")
                continue
            if name == "scl_early":
                self.scl_early = nc.dram_tensor("dbg_scl_early", [128, 256], F32,
                                                kind="ExternalOutput").ap()
                self.ex_early = nc.dram_tensor("dbg_ex_early", [128, 512], F32,
                                               kind="ExternalOutput").ap()
                continue
            shp = {"r_spill": [NCH, 128, T], "qa": [H, 128, T], "ka": [HKV, 128, T],
                   "va": [HKV, 128, T], "oa": [H, 128, T], "hid": [64, 128, T],
                   "r_new": [NCH, 128, T], "ffn_part": [NCH, 128, T],
                   "kb": [H, 4, 128, T], "vb": [H, 4, 128, T], "qb": [H, 128, T],
                   "gb": [H, 128, T], "gateb": [NCH, 128, H * 128],
                   "betab": [T // 32, 16, 128], "ob": [H, 128, T],
                   }[name]
            self.dumps[name] = nc.dram_tensor("dbg_" + name, shp, F32,
                                              kind="ExternalOutput").ap()

        def sb(name, shape, dtype=F32):
            g = nc.sbuf_tensor(name, shape, dtype)
            t = g.__enter__()
            self._guards.append(g)
            return t
        self.sb = sb
        self.R = sb("Rres", [128, NCH * T])
        self.cb = {}
        for nm, shp in [("ident", [128, 128]), ("ones_col", [128, 1]),
                        ("ones_row", [1, 128]), ("protT", [128, 128])]:
            self.cb[nm] = sb("c_" + nm, shp)
        self.scl = sb("scl", [128, 1536])
        self.mask8 = sb("mask8", [128, 128], mybir.dt.uint8)
        self.P.spc_src = self.scl[:, 1408:1472]
        self.P.spc_dst = self.scl[:, 1472:1536]
        self.arena = sb("arena", [128, 20480])
        self.Rb = self.R.bitcast(BF16)
        self.arenab = self.arena.bitcast(BF16)
        self.wnat = self.arena[:, 0:4096]
        self.wT = self.arena[:, 4096:8192]
        self.stage = self.arena[:, 8192:12288]
        self.extra = self.arena[:, 12288:20480]
        self.ps = []
        for i in range(8):
            g = nc.psum_tensor(f"ps{i}", [128, 512], F32)
            self.ps.append(g.__enter__())
            self._guards.append(g)

    def Rc(self, c):
        return self.R[:, c * T:(c + 1) * T]

    def Rbc(self, c):
        return self.Rb[:, c * T:(c + 1) * T]

    def A(self, off, width):
        return self.arena[:, off:off + width]

    def row(self, i):
        return self.arena[0:1, 8192 + i * T:8192 + (i + 1) * T]


def emit_load_const(K):
    P = K.P
    s = P.sem("d_const")
    for name in ("ident", "ones_col", "ones_row", "protT"):
        P.dma("sync", K.cb[name][:], K.cin[name], s)


def emit_load_x(K):
    P = K.P
    P.wait("tensor", P.sem("d_const"))
    P.wait("vector", P.sem("d_const"))
    s = P.sem("d_x")
    stp = P.sem("t_x")
    svv = P.sem("v_x")
    slots = [K.A(0, 2048), K.A(2048, 2048)]
    s0 = s.n
    for tt in range(16):
        buf = slots[tt % 2]
        if tt >= 2:
            P.wait("sync", stp, (tt - 1) * 16)
        P.dma("sync", buf, K.x_in[tt * 128:(tt + 1) * 128, :], s)
        P.wait("tensor", s, s0 + 16 * (tt + 1))
        for c in range(NCH):
            gi = tt * 16 + c
            bank = K.ps[gi % 2]
            if gi >= 2:
                P.wait("tensor", svv, svv.n - 1)
            P.op("tensor", lambda e, buf=buf, c=c, bank=bank: e.transpose(
                bank[:, 0:128], buf[:, c * 128:(c + 1) * 128], K.cb["ident"][:]), stp)
            P.wait("vector", stp)
            P.op("vector", lambda e, c=c, tt=tt, bank=bank: e.tensor_copy(
                K.Rc(c)[:, tt * 128:(tt + 1) * 128], bank[:, 0:128]), svv)


def emit_ln(K):
    P = K.P
    ssq = P.sem("ln_sq")
    stns = P.sem("ln_tns")
    ssp = P.sem("d_rspill")
    svec = P.sem("ln_vec")
    pb = K.ps[0]
    sp0 = ssp.n
    sq_slots = [K.A(0, 2048), K.A(2048, 2048)]

    for c in range(NCH):
        P.dma("sync", K.r_spill[c], K.Rc(c), ssp)

    stns0 = stns.n
    ssq0 = ssq.n
    for c in range(NCH):
        sq = sq_slots[c % 2]
        if c >= 2:
            P.wait("scalar", stns, stns0 + (c - 1))
        P.op("scalar", lambda e, c=c, sq=sq: e.activation(sq, K.Rc(c), AF.Square), ssq)
        P.wait("tensor", ssq, ssq0 + c + 1)
        for qq in range(4):
            sl = slice(qq * 512, (qq + 1) * 512)
            P.op("tensor", lambda e, c=c, sl=sl, qq=qq: e.matmul(
                K.ps[qq][0:1, :], K.cb["ones_col"][:], K.Rc(c)[:, sl],
                start=(c == 0), stop=(c == NCH - 1)))
            P.op("tensor", lambda e, c=c, sl=sl, sq=sq, qq=qq: e.matmul(
                K.ps[4 + qq][0:1, :], K.cb["ones_col"][:], sq[:, sl],
                start=(c == 0), stop=(c == NCH - 1)), stns if qq == 3 else None)
    smm_t = stns.n

    mean, varpe, rstd, tmp = K.row(0), K.row(1), K.row(2), K.row(3)
    P.wait("vector", stns, smm_t)
    for qq in range(4):
        sl = slice(qq * 512, (qq + 1) * 512)
        P.op("vector", lambda e, qq=qq, sl=sl: e.tensor_scalar_mul(
            mean[:, sl], K.ps[qq][0:1, :], 1.0 / D))
        P.op("vector", lambda e, qq=qq, sl=sl: e.tensor_scalar_mul(
            varpe[:, sl], K.ps[4 + qq][0:1, :], 1.0 / D), svec)
    P.wait("vector", svec)
    P.op("vector", lambda e: e.tensor_mul(tmp, mean, mean), svec)
    P.wait("vector", svec)
    P.op("vector", lambda e: e.tensor_sub(varpe, varpe, tmp), svec)
    P.wait("vector", svec)
    P.op("vector", lambda e: e.tensor_scalar_add(varpe, varpe, EPS_LN), svec)
    P.wait("vector", svec)
    P.op("vector", lambda e: e.reciprocal(rstd, varpe), svec)
    P.wait("scalar", svec)
    P.op("scalar", lambda e: e.activation(rstd, rstd, AF.Sqrt), ssq)
    P.wait("vector", ssq)
    P.op("vector", lambda e: e.tensor_mul(tmp, rstd, rstd), svec)
    P.wait("vector", svec)
    P.op("vector", lambda e: e.tensor_mul(tmp, tmp, varpe), svec)
    P.wait("vector", svec)
    P.op("vector", lambda e: e.tensor_scalar(tmp, tmp, -0.5, 1.5, ALU.mult, ALU.add), svec)
    P.wait("vector", svec)
    P.op("vector", lambda e: e.tensor_mul(rstd, rstd, tmp), svec)
    P.wait("vector", svec)
    P.op("vector", lambda e: e.tensor_mul(tmp, mean, rstd), svec)

    rb0, rb1 = K.A(4096, 2048), K.A(6144, 2048)
    P.wait("tensor", svec)
    for which, row in ((0, rstd), (1, tmp)):
        dst = rb0 if which == 0 else rb1
        for qq in range(4):
            sl = slice(qq * 512, (qq + 1) * 512)
            P.op("tensor", lambda e, row=row, sl=sl: e.matmul(
                pb[:, 0:512], K.cb["ones_row"][0:1, :], row[:, sl],
                start=True, stop=True), stns)
            P.wait("vector", stns)
            P.op("vector", lambda e, dst=dst, sl=sl: e.tensor_copy(
                dst[:, sl], pb[:, 0:512]), svec)
            P.wait("tensor", svec)

    for c in range(NCH):
        P.wait("vector", ssp, sp0 + 16 * (c + 1))
        P.op("vector", lambda e, c=c: e.tensor_mul(K.Rc(c), K.Rc(c), rb0), svec)
        P.wait("vector", svec)
        P.op("vector", lambda e, c=c: e.tensor_sub(K.Rc(c), K.Rc(c), rb1), svec)
    P.barrier()
    scast = P.sem("ln_cast")
    tmpb = K.arenab[:, 16384:18432]
    P.op("vector", lambda e: e.tensor_copy(tmpb, K.Rc(0)), scast)
    for c in range(1, NCH):
        P.op("vector", lambda e, c=c: e.tensor_copy(K.Rbc(c), K.Rc(c)), scast)
    P.wait("vector", scast)
    P.op("vector", lambda e: e.tensor_copy(K.Rbc(0), tmpb), scast)
    P.barrier()


def emit_proj(K, w_ap, n_oc, evac_fn, din=D, rhs_fn=None, sign=False, bf16=True):
    P = K.P
    ndin = din // 128
    dW, tTR, vWT = P.sem("d_w"), P.sem("t_tr"), P.sem("v_wT")
    tMM, eEV, ssg = P.sem("t_mm"), P.sem("ev_done"), P.sem("s_sign")
    ntt = T // 512
    ngr = (ndin + 3) // 4
    dW0, tTR0, vWT0, tMM0, eEV0 = dW.n, tTR.n, vWT.n, tMM.n, eEV.n
    if rhs_fn is None:
        if bf16:
            def rhs_fn(kc, tsl):
                return K.Rbc(kc)[:, tsl]
        else:
            def rhs_fn(kc, tsl):
                return K.Rc(kc)[:, tsl]

    for oc in range(n_oc):
        slot = oc % 2
        wn = K.wnat[:, slot * 2048: slot * 2048 + din]
        wt = (K.arenab[:, 8192 + slot * 2048: 8192 + slot * 2048 + din] if bf16
              else K.wT[:, slot * 2048: slot * 2048 + din])
        if oc >= 2:
            P.wait("sync", tTR, tTR0 + (oc - 1) * ngr)
        P.dma("sync", wn, w_ap[oc * 128:(oc + 1) * 128, :], dW)
        if sign:
            P.wait("scalar", dW, dW0 + 16 * (oc + 1))
            if oc >= 2:
                P.wait("scalar", tTR, tTR0 + (oc - 1) * ngr)
            P.op("scalar", lambda e, wn=wn: e.activation(wn, wn, AF.Sign), ssg)
            P.wait("tensor", ssg)
        else:
            P.wait("tensor", dW, dW0 + 16 * (oc + 1))
        for g in range(ngr):
            bank = K.ps[2 + g % 2]
            gi = oc * ngr + g
            if gi >= 2:
                P.wait("tensor", vWT, vWT0 + gi - 1)
            n_in_g = min(4, ndin - g * 4)
            for j in range(n_in_g):
                kc = g * 4 + j
                P.op("tensor", lambda e, wn=wn, kc=kc, j=j, bank=bank: e.transpose(
                    bank[:, j * 128:(j + 1) * 128],
                    wn[:, kc * 128:(kc + 1) * 128], K.cb["ident"][:]),
                    tTR if j == n_in_g - 1 else None)
            P.wait("vector", tTR, tTR0 + gi + 1)
            if oc >= 2 and g == 0:
                P.wait("vector", tMM, tMM0 + (oc - 1) * ntt)
            P.op("vector", lambda e, wt=wt, g=g, bank=bank, n_in_g=n_in_g: e.tensor_copy(
                wt[:, g * 512:g * 512 + n_in_g * 128], bank[:, 0:n_in_g * 128]), vWT)
        for tt in range(ntt):
            gidx = oc * ntt + tt
            bank = K.ps[gidx % 2]
            tsl = slice(tt * 512, (tt + 1) * 512)
            P.wait("tensor", vWT, vWT0 + (oc + 1) * ngr)
            if gidx >= 2:
                P.wait("tensor", eEV, eEV0 + gidx - 1)
            for kc in range(ndin):
                P.op("tensor", lambda e, wt=wt, kc=kc, bank=bank, tsl=tsl: e.matmul(
                    bank[:], wt[:, kc * 128:(kc + 1) * 128], rhs_fn(kc, tsl),
                    start=(kc == 0), stop=(kc == ndin - 1)),
                    tMM if kc == ndin - 1 else None)
            evac_fn(oc, tt, bank, (tMM, tMM0 + gidx + 1), eEV)


def emit_attna_proj(K):
    P = K.P
    sst = P.sem("d_qkv_st")

    def mk_evac(dst):
        def evac(oc, tt, bank, mmw, done):
            slot = oc % 2
            st = K.stage[:, slot * T + tt * 512: slot * T + (tt + 1) * 512]
            if tt == 0:
                P.wait("vector", P.sem("d_qkv_st"))
            P.wait("vector", *mmw)
            P.op("vector", lambda e, st=st, bank=bank: e.tensor_copy(st, bank[:]), done)
            if tt == 3:
                P.wait("sync", done)
                P.dma("sync", dst[oc], K.stage[:, slot * T:(slot + 1) * T], sst)
        return evac
    emit_proj(K, K.win["L0_attn_a_wq"], 16, mk_evac(K.qa))
    emit_proj(K, K.win["L0_attn_a_wk"], 4, mk_evac(K.ka))
    emit_proj(K, K.win["L0_attn_a_wv"], 4, mk_evac(K.va))
    P.barrier()


def emit_rope(K):
    P = K.P
    sld = P.sem("d_rope")
    stp = P.sem("t_rope")
    svv = P.sem("v_rope")
    sst = P.sem("d_ropest")
    tiles = [(K.qa, oc) for oc in range(16)] + [(K.ka, oc) for oc in range(4)]
    buf_slots = [K.A(0, 2048), K.A(2048, 2048)]
    tmp = K.A(4096, 2048)
    cosb, sinb = K.A(6144, 2048), K.A(8192, 2048)
    P.dma("sync", cosb, K.cin["rope_cos"], sld)
    P.dma("sync", sinb, K.cin["rope_sin"], sld)
    ld0 = sld.n
    for i, (scr, oc) in enumerate(tiles):
        buf = buf_slots[i % 2]
        if i >= 2:
            P.wait("sync", sst, 16 * (i - 1))
        P.dma("sync", buf, scr[oc], sld)
        P.wait("tensor", sld, ld0 + 16 * (i + 1))
        for qq in range(4):
            bank = K.ps[qq % 2]
            sl = slice(qq * 512, (qq + 1) * 512)
            if i > 0 or qq >= 2:
                P.wait("tensor", svv, max(0, svv.n - 1))
            P.op("tensor", lambda e, bank=bank, buf=buf, sl=sl: e.matmul(
                bank[:], K.cb["protT"][:], buf[:, sl], start=True, stop=True), stp)
            P.wait("vector", stp)
            P.op("vector", lambda e, bank=bank, sl=sl: e.tensor_mul(
                tmp[:, sl], bank[:], sinb[:, sl]), svv)
        P.wait("vector", svv)
        P.op("vector", lambda e, buf=buf: e.tensor_mul(buf, buf, cosb), svv)
        P.wait("vector", svv)
        P.op("vector", lambda e, buf=buf: e.tensor_add(buf, buf, tmp), svv)
        P.wait("sync", svv)
        P.dma("sync", scr[oc], buf, sst)
    P.barrier()


def emit_attna_core(K):
    P = K.P
    kv = K.A(0, 2048)
    qf_slots = [K.A(2048, 1024), K.A(3072, 1024)]
    kvf_slots = [K.A(4096, 512), K.A(4608, 512)]
    ktok = K.A(5120, 512)
    vtok = K.A(5632, 512)
    ks = K.A(6144, 128)
    qdq = K.A(6272, 64)
    scT = K.A(6336, 64)
    ost_slots = [K.A(6656, 1024), K.A(7680, 1024)]
    amaskT = K.A(8704, 1024)
    aqdec = K.A(9728, 1024)
    akdec = K.A(10752, 16)

    sld = P.sem("d_acore")
    stp = P.sem("t_acore")
    svv = P.sem("v_acore")
    ssc = P.sem("s_acore")
    sst = P.sem("d_aost")
    P.op("vector", lambda e: e.memset(kv[:], 0.0), svv)
    P.dma("sync", amaskT[0:ACH, :], K.cin["a_maskT"].rearrange("h j i -> j h i"), sld)
    P.dma("sync", aqdec, K.cin["a_qdec"].rearrange("h p i -> p h i"), sld)
    P.dma("sync", akdec[0:ACH, :], K.cin["a_kdec"], sld)
    ld0 = sld.n
    P.wait("vector", sld, ld0)
    P.wait("gpsimd", sld, ld0)

    def load_chunk(c):
        tsl = slice(c * ACH, (c + 1) * ACH)
        if c >= 2:
            P.wait("sync", ssc, 16 * (c - 1))
        P.dma("sync", qf_slots[c % 2][:, 0:16 * ACH],
              K.qa.rearrange("h p t -> p h t")[:, :, tsl], sld)
        P.dma("sync", kvf_slots[c % 2][:, 0:4 * ACH],
              K.ka.rearrange("h p t -> p h t")[:, :, tsl], sld)
        P.dma("sync", kvf_slots[c % 2][:, 4 * ACH:8 * ACH],
              K.va.rearrange("h p t -> p h t")[:, :, tsl], sld)

    load_chunk(0)
    for c in range(NACH):
        if c + 1 < NACH:
            load_chunk(c + 1)
        qfc, kvc = qf_slots[c % 2], kvf_slots[c % 2]
        P.wait("tensor", sld, ld0 + 48 * (c + 1))
        if c >= 2:
            P.wait("scalar", sst, 16 * (c - 1))
        for kh in range(HKV):
            P.op("tensor", lambda e, kh=kh, kvc=kvc: e.transpose(
                K.ps[0][0:64, kh * 128:(kh + 1) * 128],
                kvc[:, kh * 64:(kh + 1) * 64], K.cb["ident"][:]))
            P.op("tensor", lambda e, kh=kh, kvc=kvc: e.transpose(
                K.ps[1][0:64, kh * 128:(kh + 1) * 128],
                kvc[:, 4 * 64 + kh * 64:4 * 64 + (kh + 1) * 64], K.cb["ident"][:]),
                stp if kh == HKV - 1 else None)
        P.wait("vector", stp)
        P.wait("vector", svv)
        P.op("vector", lambda e: e.tensor_copy(ktok[0:64, 0:512], K.ps[0][0:64, 0:512]), svv)
        P.wait("vector", svv)
        P.op("vector", lambda e: e.tensor_copy(vtok[0:64, 0:512], K.ps[1][0:64, 0:512]), svv)
        P.wait("tensor", svv)
        for h in range(H):
            kh = h // 4
            kfc = kvc[:, kh * 64:(kh + 1) * 64]
            qfh = qfc[:, h * 64:(h + 1) * 64]
            ktk = ktok[0:64, kh * 128:(kh + 1) * 128]
            vtk = vtok[0:64, kh * 128:(kh + 1) * 128]
            kvh = kv[:, h * 128:(h + 1) * 128]
            b_sc = K.ps[2 + h % 2]
            b_o = K.ps[4 + h % 2]
            b_kv = K.ps[6 + h % 2]
            P.wait("tensor", svv)
            P.op("tensor", lambda e, kfc=kfc, qfh=qfh, b_sc=b_sc: e.matmul(
                b_sc[0:64, 0:64], kfc, qfh, start=True, stop=True), stp)
            P.op("vector", lambda e, ktk=ktk, h=h: e.tensor_scalar(
                ks[0:64, :], ktk, akdec[0:64, h:h + 1], None, ALU.mult), svv)
            P.wait("gpsimd", stp)
            P.op("gpsimd", lambda e, qfh=qfh, h=h: e.tensor_mul(
                qdq, qfh, aqdec[:, h * 64:(h + 1) * 64]), P.sem("g_acore"))
            P.wait("vector", stp)
            P.op("vector", lambda e, b_sc=b_sc, h=h: e.tensor_mul(
                scT[0:64, :], b_sc[0:64, 0:64],
                amaskT[0:64, h * 64:(h + 1) * 64]), svv)
            P.wait("tensor", svv)
            P.wait("tensor", P.sem("g_acore"))
            P.wait("tensor", ssc)
            P.op("tensor", lambda e, vtk=vtk, b_o=b_o: e.matmul(
                b_o[:, 0:64], vtk, scT[0:64, :], start=True, stop=False))
            P.op("tensor", lambda e, kvh=kvh, b_o=b_o: e.matmul(
                b_o[:, 0:64], kvh, qdq, start=False, stop=True), stp)
            P.op("tensor", lambda e, vtk=vtk, b_kv=b_kv: e.matmul(
                b_kv[:, 0:128], ks[0:64, :], vtk, start=True, stop=True), stp)
            P.wait("vector", stp)
            blk = K.consts["a_blk"][h]
            P.op("vector", lambda e, kvh=kvh, b_kv=b_kv, blk=blk: e.scalar_tensor_tensor(
                kvh, kvh, blk, b_kv[:, 0:128], ALU.mult, ALU.add), svv)
            P.wait("scalar", stp)
            P.op("scalar", lambda e, b_o=b_o, h=h, c=c: e.copy(
                ost_slots[c % 2][:, h * 64:(h + 1) * 64], b_o[:, 0:64]), ssc)
        P.wait("sync", ssc)
        P.dma("sync", K.oa.rearrange("h p t -> p h t")[:, :, c * ACH:(c + 1) * ACH],
              ost_slots[c % 2], sst)
    P.barrier()


def emit_out_proj(K, w_name, src_scr, n_in_ch, scale_fn=None):
    P = K.P
    sld = P.sem("d_oproj")
    srn = P.sem("d_rnew")
    srold = P.sem("d_rold")
    svv = P.sem("v_oproj")
    ld0 = sld.n
    for c in range(n_in_ch):
        P.dma("sync", K.Rc(c), src_scr[c], sld)
    P.wait("tensor", sld, ld0 + 16 * n_in_ch)
    P.wait("vector", sld, ld0 + 16 * n_in_ch)
    rold_slots = [K.extra[:, 0:512], K.extra[:, 512:1024]]

    def evac(oc, tt, bank, mmw, done):
        slot = oc % 2
        st = K.stage[:, slot * T + tt * 512: slot * T + (tt + 1) * 512]
        gidx = oc * 4 + tt
        rb = rold_slots[gidx % 2][:, 0:512]
        P.wait("sync", done, max(0, done.n - 1))
        P.dma("sync", rb, K.r_spill[oc][:, tt * 512:(tt + 1) * 512], srold)
        if tt == 0:
            P.wait("vector", P.sem("d_rnew"))
        P.wait("vector", srold)
        P.wait("vector", *mmw)
        if scale_fn is None:
            P.op("vector", lambda e, st=st, bank=bank, rb=rb: e.tensor_add(
                st, bank[:], rb), done)
        else:
            sc = scale_fn(oc)
            P.op("vector", lambda e, st=st, bank=bank, sc=sc: e.tensor_scalar(
                st, bank[:], sc, None, ALU.mult))
            P.op("vector", lambda e, st=st, rb=rb: e.tensor_add(st, st, rb), done)
        if tt == 3:
            P.wait("sync", done)
            P.dma("sync", K.r_new[oc], K.stage[:, slot * T:(slot + 1) * T], srn)

    emit_proj(K, K.win[w_name], NCH, evac, din=n_in_ch * 128, bf16=False)
    P.barrier()
    s2 = P.sem("d_rback")
    for c in range(NCH):
        P.dma("sync", K.Rc(c), K.r_new[c], s2)
    P.barrier()


def emit_final_store(K):
    P = K.P
    stp = P.sem("t_fin")
    svv = P.sem("v_fin")
    sst = P.sem("d_fin")
    for c in range(NCH):
        for tt in range(NCH):
            bank = K.ps[(c * NCH + tt) % 2]
            gi = c * NCH + tt
            if gi >= 2:
                P.wait("tensor", svv, svv.n - 1)
            P.op("tensor", lambda e, c=c, tt=tt, bank=bank: e.transpose(
                bank[:, 0:128], K.Rc(c)[:, tt * 128:(tt + 1) * 128],
                K.cb["ident"][:]), stp)
            P.wait("vector", stp)
            st = K.stage[:, (gi % 8) * 128:(gi % 8 + 1) * 128]
            if gi >= 8:
                P.wait("vector", sst, 16 * (gi - 7))
            P.op("vector", lambda e, st=st, bank=bank: e.tensor_copy(
                st, bank[:, 0:128]), svv)
            P.wait("sync", svv)
            P.dma("sync", K.out[tt * 128:(tt + 1) * 128, c * 128:(c + 1) * 128],
                  st, sst)
    P.wait("sync", sst)


def emit_dumps(K):
    P = K.P
    s = P.sem("d_dump")
    for name, dst in K.dumps.items():
        if name in K.sbuf_dumps:
            P.dma("sync", dst, getattr(K, name)[:], s)
        else:
            P.dma("sync", dst, getattr(K, name), s)
    P.wait("sync", s)


def emit_ffn(K):
    P = K.P
    sld = P.sem("d_ffs")
    stp = P.sem("t_ffs")
    svv = P.sem("v_ffs")
    ld0 = sld.n
    ex = K.extra
    svv0 = P.sem("v_ffs")
    P.op("vector", lambda e: e.memset(ex[:, 0:256], 0.0), svv0)
    P.wait("sync", svv0)
    P.dma("sync", ex[0:64, 0:128], K.win["L0_ffn_up_scale"].rearrange("(a b) -> a b", b=128), sld)
    P.dma("sync", ex[0:16, 128:256], K.win["L0_ffn_down_scale"].rearrange("(a b) -> a b", b=128), sld)
    P.dma("sync", ex[0:1, 384:385], K.win["L0_ffn_output_scale"].unsqueeze(0), sld)
    P.wait("vector", sld, ld0 + 48)
    P.op("vector", lambda e: e.tensor_copy(ex[0:1, 256:257], ex[0:1, 384:385]), svv)
    P.wait("tensor", svv)
    P.op("tensor", lambda e: e.matmul(K.ps[6][:, 0:128], ex[:, 128:256],
                                      K.cb["ident"][:], start=True, stop=True))
    P.op("tensor", lambda e: e.matmul(K.ps[5][:, 0:128], ex[:, 0:128],
                                      K.cb["ident"][:], start=True, stop=True), stp)
    P.op("tensor", lambda e: e.matmul(K.ps[7][:, 0:16], K.cb["ones_row"][0:1, :],
                                      ex[0:1, 256:257].broadcast_to([1, 16]),
                                      start=True, stop=True), stp)
    P.wait("vector", stp)
    P.wait("vector", svv)
    P.op("vector", lambda e: e.tensor_copy(K.scl[:, 96:97], K.ps[7][:, 0:1]), svv)
    P.op("vector", lambda e: e.tensor_copy(K.scl[:, 0:64], K.ps[5][:, 0:64]), svv)
    P.wait("vector", svv)
    P.op("vector", lambda e: e.tensor_scalar(K.scl[:, 64:80], K.ps[6][:, 0:16],
                                             K.scl[:, 96:97], None, ALU.mult), svv)
    P.barrier()
    if hasattr(K, "scl_early"):
        P.dma("sync", K.scl_early, K.scl[:], P.sem("d_dump"))
        P.dma("sync", K.ex_early, ex[:, 0:512], P.sem("d_dump"))
        P.barrier()

    shst = P.sem("d_hidst")

    def up_evac(oc, tt, bank, mmw, done):
        slot = oc % 2
        st = K.arenab[:, 16384 + slot * T + tt * 512:
                      16384 + slot * T + (tt + 1) * 512]
        if tt == 0:
            P.wait("vector", P.sem("d_hidst"))
        P.wait("vector", *mmw)
        P.op("vector", lambda e, st=st, bank=bank, oc=oc: e.tensor_scalar(
            st, bank[:], K.scl[:, oc:oc + 1], 0.0, ALU.mult, ALU.max), done)
        if tt == 3:
            P.wait("sync", done)
            P.dma("sync", K.hidb[oc],
                  K.arenab[:, 16384 + slot * T:16384 + (slot + 1) * T], shst)

    emit_proj(K, K.win["L0_ffn_up_weight"], 64, up_evac, sign=True)
    P.barrier()

    dW, tTR, vWT = P.sem("d_w"), P.sem("t_tr"), P.sem("v_wT")
    tMM, eEV, ssg = P.sem("t_mm"), P.sem("ev_done"), P.sem("s_sign")
    sprt = P.sem("d_part")
    shl = P.sem("d_hidld")
    srn = P.sem("d_rnew2")
    srold = P.sem("d_rold2")
    rold_slots = [K.extra[:, 0:1024], K.extra[:, 1024:2048]]
    part_slots = [K.extra[:, 2048:2560], K.extra[:, 2560:3072]]
    for q in range(4):
        hl0 = shl.n
        tMM0q = tMM.n
        P.barrier()
        for c in range(16):
            P.dma("sync", K.Rbc(c), K.hidb[q * 16 + c], shl)
        P.wait("tensor", shl, hl0 + 256)
        P.wait("vector", shl, hl0 + 256)
        for oc in range(NCH):
            slot = oc % 2
            wn = K.wnat[:, slot * 2048: slot * 2048 + 2048]
            wt = K.arenab[:, 8192 + slot * 2048: 8192 + slot * 2048 + 2048]
            if oc >= 2:
                P.wait("sync", tTR, tTR.n - 4)
            P.dma("sync", wn, K.win["L0_ffn_down_weight"][oc * 128:(oc + 1) * 128,
                                                          q * 2048:(q + 1) * 2048], dW)
            P.wait("scalar", dW)
            if oc >= 2:
                P.wait("scalar", tTR, tTR.n - 4)
            P.op("scalar", lambda e, wn=wn: e.activation(wn, wn, AF.Sign), ssg)
            P.wait("tensor", ssg)
            for g in range(4):
                bank = K.ps[2 + g % 2]
                if oc * 4 + g >= 2:
                    P.wait("tensor", vWT, vWT.n - 1)
                for j in range(4):
                    kc = g * 4 + j
                    P.op("tensor", lambda e, wn=wn, kc=kc, j=j, bank=bank: e.transpose(
                        bank[:, j * 128:(j + 1) * 128],
                        wn[:, kc * 128:(kc + 1) * 128], K.cb["ident"][:]),
                        tTR if j == 3 else None)
                P.wait("vector", tTR)
                if g == 0 and oc >= 2:
                    P.wait("vector", tMM, tMM0q + (oc - 1) * 4)
                P.op("vector", lambda e, wt=wt, g=g, bank=bank: e.tensor_copy(
                    wt[:, g * 512:(g + 1) * 512], bank[:, 0:512]), vWT)
            for tt in range(4):
                gidx = oc * 4 + tt
                bank = K.ps[gidx % 2]
                tsl = slice(tt * 512, (tt + 1) * 512)
                P.wait("tensor", vWT)
                if gidx >= 2:
                    P.wait("tensor", eEV, eEV.n - 1)
                for kc in range(16):
                    P.op("tensor", lambda e, wt=wt, kc=kc, bank=bank, tsl=tsl: e.matmul(
                        bank[:], wt[:, kc * 128:(kc + 1) * 128], K.Rbc(kc)[:, tsl],
                        start=(kc == 0), stop=(kc == 15)), tMM if kc == 15 else None)
                slot2 = gidx % 2
                st = K.stage[:, (oc % 2) * T + tt * 512: (oc % 2) * T + (tt + 1) * 512]
                if tt == 0:
                    P.wait("vector", sprt)
                    P.wait("vector", srn)
                if q == 0:
                    P.wait("vector", tMM)
                    P.op("vector", lambda e, st=st, bank=bank: e.tensor_copy(st, bank[:]), eEV)
                else:
                    pb = part_slots[slot2][:, 0:512]
                    P.wait("sync", eEV, max(0, eEV.n - 1))
                    P.dma("sync", pb, K.ffn_part[oc][:, tsl], sprt)
                    P.wait("vector", sprt)
                    P.wait("vector", tMM)
                    if q < 3:
                        P.op("vector", lambda e, st=st, bank=bank, pb=pb: e.tensor_add(
                            st, bank[:], pb), eEV)
                    else:
                        rb = rold_slots[slot2][:, 512:1024]
                        P.wait("sync", eEV, max(0, eEV.n - 1))
                        P.dma("sync", rb, K.r_spill[oc][:, tsl], srold)
                        P.wait("vector", srold)
                        P.op("vector", lambda e, st=st, bank=bank, pb=pb: e.tensor_add(
                            st, bank[:], pb), eEV)
                        P.wait("vector", eEV)
                        P.op("vector", lambda e, st=st, oc=oc: e.tensor_scalar(
                            st, st, K.scl[:, 64 + oc:64 + oc + 1], None, ALU.mult), eEV)
                        P.wait("vector", eEV)
                        P.op("vector", lambda e, st=st, rb=rb: e.tensor_add(st, st, rb), eEV)
                if tt == 3:
                    P.wait("sync", eEV)
                    dst = K.ffn_part[oc] if q < 3 else K.r_new[oc]
                    P.dma("sync", dst, K.stage[:, (oc % 2) * T:(oc % 2 + 1) * T],
                          srn if q == 3 else sprt)
    P.barrier()
    s2 = P.sem("d_rback2")
    for c in range(NCH):
        P.dma("sync", K.Rc(c), K.r_new[c], s2)
    P.barrier()




def emit_attnb_prep(K):
    P = K.P
    sld = P.sem("d_bprep")
    stp = P.sem("t_bprep")
    svv = P.sem("v_bprep")
    ssc = P.sem("s_bprep")
    sgp = P.sem("g_bprep")
    ex = K.extra
    P.op("vector", lambda e: e.memset(ex[:, 0:768], 0.0), svv)
    P.wait("sync", svv)
    _dmas = [
        lambda: P.dma("sync", ex[0:16, 0:128],
                      K.win["L0_attn_b_dt_bias"].rearrange("(a b) -> a b", b=128), sld),
        lambda: P.dma("sync", ex[0:1, 256:272], K.win["L0_attn_b_A_log"].unsqueeze(0), sld),
        lambda: P.dma("sync", ex[0:16, 384:388], K.win["L0_attn_b_micro_logits"], sld),
        lambda: P.dma("sync", ex[0:1, 388:516], K.win["L0_attn_b_o_norm_w"].unsqueeze(0), sld),
        lambda: P.dma("sync", K.scl[:, 208:272],
                      K.win["L0_attn_b_q_conv_w"].rearrange("(c p) k -> p c k", p=128), sld),
        lambda: P.dma("sync", K.scl[:, 272:528],
                      K.win["L0_attn_b_k_conv_w"].rearrange("(c p) k -> p c k", p=128), sld),
        lambda: P.dma("sync", K.scl[:, 528:784],
                      K.win["L0_attn_b_v_conv_w"].rearrange("(c p) k -> p c k", p=128), sld),
        lambda: P.dma("sync", ex[0:4, 640:768], K.cin["j4"], sld),
        lambda: P.dma("sync", ex[:, 768:800], K.cin["expT"], sld),
    ]
    for _i, _f in enumerate(_dmas):
        if PREP_DMAS is None or _i in PREP_DMAS:
            _f()
    ld_t = sld.n
    if PREP_N == 1:
        P.barrier(); return

    P.wait("tensor", sld, ld_t)
    P.op("tensor", lambda e: e.matmul(K.ps[0][:, 0:128], ex[:, 0:128], K.cb["ident"][:],
                                      start=True, stop=True), stp)
    P.wait("scalar", sld, ld_t)
    P.op("scalar", lambda e: e.activation(ex[0:1, 272:288], ex[0:1, 256:272], AF.Exp), ssc)
    P.wait("gpsimd", ssc)
    P.op("gpsimd", lambda e: e.tensor_scalar_mul(ex[0:1, 288:304], ex[0:1, 272:288], -1.0), sgp)
    P.wait("tensor", sgp)
    P.op("tensor", lambda e: e.matmul(K.ps[1][:, 0:16], K.cb["ones_row"][0:1, :],
                                      ex[0:1, 288:304], start=True, stop=True))
    P.op("tensor", lambda e: e.matmul(K.ps[2][:, 0:128], K.cb["ones_row"][0:1, :],
                                      ex[0:1, 388:516], start=True, stop=True), stp)
    P.wait("vector", stp)
    P.op("vector", lambda e: e.tensor_copy(K.scl[:, 128:144], K.ps[0][:, 0:16]))
    P.op("vector", lambda e: e.tensor_copy(K.scl[:, 160:176], K.ps[1][:, 0:16]))
    P.op("vector", lambda e: e.tensor_copy(K.scl[:, 784:912], K.ps[2][:, 0:128]), svv)
    if PREP_N == 2:
        P.barrier(); return

    t0 = ex[:, 544:560]
    t1 = ex[:, 560:576]
    P.wait("gpsimd", svv)
    P.op("gpsimd", lambda e: e.tensor_scalar_add(t0, K.scl[:, 128:144], -10000.0), sgp)
    P.wait("scalar", sgp)
    P.op("scalar", lambda e: e.activation(t1, t0, AF.Abs), ssc)
    P.wait("scalar", ssc)
    P.op("scalar", lambda e: e.activation(t1, t1, AF.Exp, scale=-1.0), ssc)
    P.wait("scalar", ssc)
    P.op("scalar", lambda e: e.activation(t1, t1, AF.Ln, bias=1.0), ssc)
    P.wait("gpsimd", ssc)
    P.op("gpsimd", lambda e: e.tensor_scalar_max(t0, t0, 0.0), sgp)
    P.wait("vector", sgp)
    P.op("vector", lambda e: e.tensor_add(t1, t1, t0), svv)
    P.wait("vector", svv)
    P.op("vector", lambda e: e.tensor_mul(K.scl[:, 144:160], t1, K.scl[:, 160:176]), svv)

    if PREP_N == 3:
        P.barrier(); return
    w = ex[0:16, 384:388]
    wm = ex[0:16, 576:580]
    rmax = ex[0:16, 580:581]
    rsum = ex[0:16, 581:582]
    P.op("vector", lambda e: e.reduce_max(rmax, w, mybir.AxisListType.X), svv)
    P.wait("gpsimd", svv)
    P.op("gpsimd", lambda e: e.tensor_scalar_mul(rmax, rmax, -1.0), sgp)
    P.wait("scalar", sgp)
    P.op("scalar", lambda e: e.activation(wm, w, AF.Exp, bias=rmax), ssc)
    if PREP_N == 4:
        P.barrier(); return
    P.wait("vector", ssc)
    P.op("vector", lambda e: e.reduce_sum(rsum, wm, mybir.AxisListType.X), svv)
    P.wait("vector", svv)
    P.op("vector", lambda e: e.reciprocal(rmax, rsum), svv)
    P.wait("vector", svv)
    P.op("vector", lambda e: e.tensor_scalar(wm, wm, rmax, None, ALU.mult), svv)
    P.wait("vector", svv)
    P.op("vector", lambda e: e.tensor_copy(ex[0:16, 592:596], wm), svv)
    if PREP_N == 5:
        P.barrier(); return
    P.wait("tensor", svv)
    P.op("tensor", lambda e: e.matmul(K.ps[3][0:4, 0:128], ex[:, 592:596], K.cb["ident"][:],
                                      start=True, stop=True), stp)
    P.wait("vector", stp)
    P.op("vector", lambda e: e.tensor_copy(ex[0:4, 608:624], K.ps[3][0:4, 0:16]), svv)
    P.wait("tensor", svv)
    P.op("tensor", lambda e: e.matmul(K.ps[4][:, 0:16], ex[0:4, 640:768], ex[0:4, 608:624],
                                      start=True, stop=True), stp)
    P.wait("vector", stp)
    P.op("vector", lambda e: e.tensor_copy(K.scl[:, 932:948], K.ps[4][:, 0:16]), svv)
    P.wait("vector", svv)
    for h in range(H):
        P.op("vector", lambda e, h=h: e.tensor_scalar(
            K.scl[:, 960 + h * 32:960 + (h + 1) * 32], ex[:, 768:800],
            K.scl[:, 932 + h:933 + h], None, ALU.mult), svv)
    P.barrier()


RS_, RV_, RAN_, RAQ_, RBB_ = 0, 2048, 4096, 6144, 8192
RMT_, RMQ_, RK_, RZ_ = 10240, 12288, 14336, 16384
RCSK_, RCSQ_, RK40_, RQ40_ = 18432, 20480, 22528, 24576
RPRK_, RPBK_, RE_ = 26624, 28672, 30720
ACUMG, AG_, AQT = 0, 512, 1024
AJLF, AID32, ALB, AXT = 2048, 4096, 5120, 6144
AXTA = 7168
AJ3, ABV12, ALFLR = 7680, 7808, 7936
ALF12, ALF96 = 8064, 8080
ABSX, ACND, ABER, AEBER, ALGL = 8096, 8224, 8352, 8480, 8608
AELGB, ANEGB, AELGL = 8736, 8864, 8992
ABB, ANBB, ABLD, ANBB2 = 9008, 9024, 9040, 9168
AGRAM, AMRAW, ABMAT = 9200, 9456, 9712
AW_, AO_, ABBAR, ARP = 9840, 9968, 10096, 10224
AOBST, ATMP = 10352, 10864


def emit_attnb_core(K):
    P = K.P
    H_ = 16
    sld = P.sem("d_bcl")
    sgt = P.sem("d_bgt")
    sxs = [P.sem(f"d_bxs{i}") for i in range(4)]
    sxb = P.sem("d_bxb")
    sxv = P.sem("d_bxv")
    sxl = P.sem("d_bxl")
    sob = P.sem("d_bob")
    stc = P.sem("t_bc")
    svc = P.sem("v_bc")
    ssc = P.sem("s_bc")
    sgc = P.sem("g_bc")
    QS = {"tensor": stc, "vector": svc, "scalar": ssc, "gpsimd": sgc}

    def OP(eng, fn, sem=None):
        P.wait(eng, QS[eng])
        P.op(eng, fn, sem if sem is not None else QS[eng])

    def RR(off, w, p0=0, p1=128):
        return K.R[p0:p1, off:off + w]

    RFULL = K.R[:, 0:NCH * T].rearrange("p (a b) -> p a b", b=2048)
    kmic = RR(RK_, 2048).rearrange("p (h r t) -> p h t r", r=4, t=32)
    vmicv = RR(RV_, 2048).rearrange("p (h r t) -> p h t r", r=4, t=32)
    qrep = K.A(AQT, 512).rearrange("p (h t) -> p h t", t=32) \
        .unsqueeze(3).broadcast_to([128, 16, 32, 4])
    cumg3 = K.A(ACUMG, 512).rearrange("p (h t) -> p h t", t=32)
    cumg_rep = cumg3.unsqueeze(3).broadcast_to([128, 16, 32, 4])
    cumg4 = K.A(ACUMG, 512).rearrange("p (h b f) -> p h b f", b=8, f=4)

    def rep8(off):
        return K.A(off, 128).rearrange("p (h b) -> p h b", b=8) \
            .unsqueeze(3).broadcast_to([128, 16, 8, 16])

    def rep16(off, n):
        return K.A(off, 16).unsqueeze(2).broadcast_to([128, 16, n])

    def whtj(off):
        return RR(off, 2048).rearrange("p (h t j) -> p h t j", t=32, j=4)

    def whbm(off):
        return RR(off, 2048).rearrange("p (h b m) -> p h b m", b=8, m=16)

    NBC = BCORE_NCH

    pl0 = sld.n
    P.dma("sync", K.A(AJ3, 128), K.cin["j3row"], sld)
    P.dma("sync", K.A(ABV12, 128), K.cin["bval12"], sld)
    P.dma("sync", K.A(AID32, 1024), K.cin["id32flat"], sld)
    P.dma("sync", K.mask8[:], K.cin["bdmask8"], sld)
    P.wait("gpsimd", sld, pl0 + 64)
    OP("gpsimd", lambda e: e.tensor_scalar_mul(K.A(ALF96, 16), K.scl[:, 144:160], 96.0), sgc)
    OP("gpsimd", lambda e: e.tensor_scalar_mul(K.A(ALF12, 16), K.scl[:, 144:160], 12.0), sgc)
    P.wait("vector", sld, pl0 + 64)
    lfrep128 = K.scl[:, 144:160].unsqueeze(2).broadcast_to([128, 16, 128])
    lfrep8 = K.scl[:, 144:160].unsqueeze(2).broadcast_to([128, 16, 8])
    j3rep = K.A(AJ3, 128).unsqueeze(1).broadcast_to([128, 16, 128])
    OP("vector", lambda e: e.tensor_mul(
        K.A(AJLF, 2048).rearrange("p (h m) -> p h m", m=128), j3rep, lfrep128), svc)
    OP("vector", lambda e: e.tensor_mul(
        K.A(ABV12, 128).rearrange("p (h b) -> p h b", b=8),
        K.A(ABV12, 128).rearrange("p (h b) -> p h b", b=8), lfrep8), svc)
    P.wait("vector", sgc)
    P.wait("vector", svc)
    OP("vector", lambda e: e.tensor_sub(
        K.A(ALFLR, 128).rearrange("p (h b) -> p h b", b=8),
        K.A(ALF96, 16).unsqueeze(2).broadcast_to([128, 16, 8]),
        K.A(ABV12, 128).rearrange("p (h b) -> p h b", b=8)), svc)
    OP("vector", lambda e: e.memset(K.A(ATMP + 144, 1), 40.0), svc)
    OP("vector", lambda e: e.memset(K.A(ATMP + 145, 1), -40.0), svc)
    OP("vector", lambda e: e.memset(RR(RS_, 2048), 0.0), svc)
    OP("vector", lambda e: e.memset(RR(RMQ_, 2048), 0.0), svc)
    OP("vector", lambda e: e.memset(K.A(AMRAW, 256), 0.0), svc)

    def load_chunk(c):
        mk = {}
        for hg in range(4):
            P.dma("sync", RR(RK_ + hg * 512, 512),
                  K.kb[4 * hg:4 * hg + 4, :, :, 32 * c:32 * c + 32]
                  .rearrange("h r p t -> p h r t"), sld)
        mk["k"] = sld.n
        for hg in range(4):
            P.dma("sync", RR(RV_ + hg * 512, 512),
                  K.vb[4 * hg:4 * hg + 4, :, :, 32 * c:32 * c + 32]
                  .rearrange("h r p t -> p h r t"), sld)
        mk["v"] = sld.n
        P.dma("sync", K.A(AQT, 512),
              K.qb[:, :, 32 * c:32 * c + 32].rearrange("h p t -> p h t"), sld)
        mk["q"] = sld.n
        P.dma("sync", K.A(AG_, 512),
              K.gb[:, :, 32 * c:32 * c + 32].rearrange("h p t -> p h t"), sld)
        mk["g"] = sld.n
        P.dma("sync", K.arena[0:16, ABLD:ABLD + 128], K.betab[c], sld)
        mk["b"] = sld.n
        return mk

    lmark = load_chunk(0)

    for c in range(NBC):
        if c > 0:
            P.barrier()
        P.wait("vector", sld, lmark["g"])
        for h in range(H_):
            OP("vector", lambda e, h=h: e.tensor_tensor_scan(
                K.A(ACUMG + 32 * h, 32), K.A(AG_ + 32 * h, 32),
                K.A(AG_ + 32 * h, 32), 0.0, ALU.add, ALU.bypass), svc)
        OP("gpsimd", lambda e: e.memset(K.A(ABSX, 128), 0.0), sgc)
        P.wait("gpsimd", svc)
        OP("gpsimd", lambda e: e.tensor_copy(
            K.A(ABSX, 128).rearrange("p (h b) -> p h b", b=8)[:, :, 1:8],
            cumg4[:, :, 0:7, 3]), sgc)
        OP("gpsimd", lambda e: e.tensor_copy(
            K.A(ACND, 128).rearrange("p (h b) -> p h b", b=8), cumg4[:, :, :, 3]), sgc)
        P.wait("vector", sgc)
        OP("vector", lambda e: e.tensor_sub(K.A(ABER, 128), K.A(ACND, 128),
                                              K.A(ABSX, 128)), svc)
        P.wait("vector", svc)
        OP("vector", lambda e: e.tensor_add(
            K.A(ABER, 128).rearrange("p (h b) -> p h b", b=8),
            K.A(ABER, 128).rearrange("p (h b) -> p h b", b=8),
            K.A(ALF12, 16).unsqueeze(2).broadcast_to([128, 16, 8])), svc)
        P.wait("scalar", svc)
        OP("scalar", lambda e: e.activation(K.A(AEBER, 128), K.A(ABER, 128), AF.Exp), ssc)
        OP("vector", lambda e: e.tensor_add(K.A(ATMP, 128), K.A(ABSX, 128),
                                              K.A(ABV12, 128)), svc)
        P.wait("scalar", svc)
        OP("scalar", lambda e: e.activation(K.A(AELGB, 128), K.A(ATMP, 128), AF.Exp), ssc)
        P.wait("gpsimd", ssc)
        OP("gpsimd", lambda e: e.tensor_scalar_mul(K.A(ANEGB, 128), K.A(AELGB, 128),
                                                     -1.0), sgc)
        OP("vector", lambda e: e.tensor_add(
            K.A(ATMP + 128, 16), cumg3[:, :, 31], K.A(ALF96, 16)), svc)
        P.wait("scalar", svc)
        OP("scalar", lambda e: e.activation(K.A(AELGL, 16), K.A(ATMP + 128, 16),
                                              AF.Exp), ssc)
        OP("vector", lambda e: e.tensor_sub(
            K.A(ALGL, 128).rearrange("p (h b) -> p h b", b=8),
            cumg3[:, :, 31].unsqueeze(2).broadcast_to([128, 16, 8]),
            K.A(ABSX, 128).rearrange("p (h b) -> p h b", b=8)), svc)
        P.wait("vector", svc)
        OP("vector", lambda e: e.tensor_add(K.A(ALGL, 128), K.A(ALGL, 128),
                                              K.A(ALFLR, 128)), svc)
        OP("vector", lambda e: e.tensor_copy(whtj(RZ_), cumg_rep), svc)
        P.wait("vector", svc)
        OP("vector", lambda e: e.tensor_sub(whbm(RZ_), whbm(RZ_), rep8(ABSX)), svc)
        P.wait("gpsimd", svc)
        OP("gpsimd", lambda e: e.tensor_add(RR(RZ_, 2048), RR(RZ_, 2048),
                                              K.A(AJLF, 2048)), sgc)
        P.wait("scalar", sgc)
        OP("scalar", lambda e: e.activation(RR(RE_, 2048), RR(RZ_, 2048), AF.Exp), ssc)
        P.wait("vector", sld, lmark["k"])
        P.wait("vector", ssc)
        OP("vector", lambda e: e.tensor_mul(whtj(RCSK_), kmic, whtj(RE_)), svc)
        P.wait("gpsimd", sld, lmark["q"])
        P.wait("gpsimd", ssc)
        OP("gpsimd", lambda e: e.tensor_mul(whtj(RCSQ_), qrep, whtj(RE_)), sgc)
        P.wait("scalar", svc)
        P.wait("scalar", sgc)
        OP("scalar", lambda e: e.activation(RR(RE_, 2048), RR(RZ_, 2048), AF.Exp,
                                              bias=K.A(ATMP + 144, 1)), ssc)
        P.wait("vector", ssc)
        OP("vector", lambda e: e.tensor_mul(whtj(RK40_), kmic, whtj(RE_)), svc)
        P.wait("gpsimd", ssc)
        OP("gpsimd", lambda e: e.tensor_mul(whtj(RQ40_), qrep, whtj(RE_)), sgc)
        P.wait("scalar", svc)
        P.wait("scalar", sgc)
        OP("scalar", lambda e: e.activation(RR(RE_, 2048), RR(RZ_, 2048), AF.Exp,
                                              scale=-1.0, bias=K.A(ATMP + 145, 1)), ssc)
        P.wait("vector", ssc)
        OP("vector", lambda e: e.tensor_mul(whtj(RPRK_), kmic, whtj(RE_)), svc)
        P.wait("gpsimd", svc)
        OP("gpsimd", lambda e: e.tensor_sub(whbm(RE_), rep8(ABER), whbm(RZ_)), sgc)
        P.wait("scalar", sgc)
        OP("scalar", lambda e: e.activation(RR(RE_, 2048), RR(RE_, 2048), AF.Exp), ssc)
        P.wait("vector", ssc)
        OP("vector", lambda e: e.tensor_mul(whtj(RPBK_), kmic, whtj(RE_)), svc)
        P.wait("gpsimd", svc)
        OP("gpsimd", lambda e: e.tensor_sub(whbm(RE_), rep8(ALGL), whbm(RZ_)), sgc)
        P.wait("scalar", sgc)
        OP("scalar", lambda e: e.activation(RR(RE_, 2048), RR(RE_, 2048), AF.Exp), ssc)
        P.wait("vector", ssc)
        OP("vector", lambda e: e.tensor_mul(whtj(RBB_), kmic, whtj(RE_)), svc)
        P.wait("gpsimd", svc)
        OP("gpsimd", lambda e: e.tensor_mul(whbm(RAN_), whbm(RCSK_), rep8(ANEGB)), sgc)
        OP("vector", lambda e: e.tensor_mul(whbm(RAQ_), whbm(RCSQ_), rep8(AELGB)), svc)
        P.wait("gpsimd", sld, lmark["v"])
        P.wait("gpsimd", svc)
        OP("gpsimd", lambda e: e.tensor_copy(whtj(RE_), vmicv), sgc)
        P.wait("tensor", sld, lmark["b"])
        OP("tensor", lambda e: e.matmul(K.ps[2][:, 0:16], K.arena[0:16, ABLD:ABLD + 128],
                                          K.cb["ident"][0:16, 0:16],
                                          start=True, stop=True), stc)
        P.wait("vector", stc)
        OP("vector", lambda e: e.tensor_copy(K.A(ABB, 16), K.ps[2][:, 0:16]), svc)
        P.wait("gpsimd", svc)
        OP("gpsimd", lambda e: e.tensor_scalar_mul(K.A(ANBB, 16), K.A(ABB, 16), -1.0), sgc)
        if c + 1 < NBC:
            P.wait("sync", svc)
            P.wait("sync", sgc)
            P.wait("sync", stc)
            lmark = load_chunk(c + 1)

        shipmark = {}
        for h in range(H_):
            if h == 0:
                P.wait("tensor", svc)
                P.wait("tensor", sgc)
            else:
                P.wait("tensor", ssc)
            OP("tensor", lambda e, h=h: e.matmul(
                K.ps[1][:, 0:256], RR(RPRK_ + h * 128, 128),
                RFULL[:, RK40_ // 2048:RK40_ // 2048 + 2, h * 128:(h + 1) * 128],
                start=True, stop=True), stc)
            P.wait("vector", stc)
            OP("vector", lambda e: e.memset(K.A(ABMAT, 128), 0.0), svc)
            for B in range(8):
                if B > 0:
                    P.wait("tensor", svc)
                    if B == 1 and h > 0:
                        P.wait("tensor", ssc)
                    OP("tensor", lambda e, h=h, B=B: e.matmul(
                        K.ps[0][:, 32 * B:32 * B + 32], K.A(ABMAT, 128),
                        RFULL[:, RCSK_ // 2048:RCSK_ // 2048 + 2,
                              h * 128 + 16 * B:h * 128 + 16 * B + 16],
                        start=True, stop=True), stc)
                if B < 7:
                    P.wait("vector", stc)
                    if B > 0:
                        OP("vector", lambda e, h=h, B=B: e.tensor_scalar(
                            K.A(ABMAT, 128), K.A(ABMAT, 128),
                            K.A(AEBER + h * 8 + B, 1), None, ALU.mult), svc)
                    OP("vector", lambda e, h=h, B=B: e.tensor_copy(
                        K.A(ABMAT + 16 * B, 16),
                        RR(RPBK_ + h * 128 + 16 * B, 16)), svc)
            P.wait("scalar", stc)
            P.wait("scalar", svc)
            if h > 0:
                for i4 in range(4):
                    P.wait("scalar", sxs[i4], shipmark[h - 1][i4])
            OP("scalar", lambda e: e.copy(K.A(AGRAM, 256), K.ps[1][:, 0:256]), ssc)
            if h > 1:
                for i4 in range(4):
                    P.wait("scalar", sxs[i4], shipmark[h - 2][i4])
            mrawo = AMRAW + (h % 2) * 128
            OP("scalar", lambda e, mrawo=mrawo: e.copy(
                K.A(mrawo + 16, 112).rearrange("p (b i) -> p b i", i=16),
                K.ps[0][:, 32:256].rearrange("p (b x) -> p b x", x=32)[:, :, 0:16]), ssc)
            OP("scalar", lambda e, h=h: e.copy(
                RR(RMQ_ + h * 128 + 16, 112).rearrange("p (b i) -> p b i", i=16),
                K.ps[0][:, 32:256].rearrange("p (b x) -> p b x", x=32)[:, :, 16:32]), ssc)
            P.wait("vector", ssc)
            OP("vector", lambda e, h=h: e.copy_predicated(
                RR(RMQ_ + h * 128, 128), K.mask8[:], K.A(AGRAM + 128, 128)), svc)
            OP("vector", lambda e, h=h, mrawo=mrawo: e.tensor_scalar(
                RR(RMT_ + h * 128, 128), K.A(mrawo, 128), K.A(ANBB + h, 1),
                None, ALU.mult), svc)
            P.wait("vector", svc)
            OP("vector", lambda e, h=h: e.tensor_scalar(
                RR(RMQ_ + h * 128, 128), RR(RMQ_ + h * 128, 128), K.A(ABB + h, 1),
                None, ALU.mult), svc)
            xv = K.xdiag[c, h].rearrange("i (j k) -> i j k", k=32)
            P.wait("sync", ssc)
            nship = 0
            for B in range(8):
                I, odd = B // 2, B % 2
                P.dma("sync", xv[I, 16 * odd:16 * odd + 16, 16 * odd:16 * odd + 16],
                      K.arena[16 * B:16 * B + 16, AGRAM + 16 * B:AGRAM + 16 * B + 16],
                      sxs[nship % 4])
                nship += 1
            for I in range(4):
                P.dma("sync", xv[I, 0:16, 16:32],
                      K.arena[32 * I:32 * I + 16, mrawo + 32 * I + 16:mrawo + 32 * I + 32],
                      sxs[nship % 4])
                nship += 1
            shipmark[h] = [sx.n for sx in sxs]

        P.wait("sync", stc)
        P.dma("sync", RR(RCSQ_, 2048, 0, 32),
              K.gateb[c // 4][32 * (c % 4):32 * (c % 4) + 32, :], sgt)
        gtmark = sgt.n

        for i4 in range(4):
            P.wait("sync", sxs[i4], shipmark[H_ - 1][i4])
        P.dma("sync", K.arena[0:64, ALB:ALB + 512],
              K.xdiag[c].rearrange("h i m -> (h i) m")[:, 0:512], sxb)
        P.dma("sync", K.arena[0:64, ALB:ALB + 1024]
              .rearrange("p (j k) -> p j k", k=32)[:, 16:32, 16:32],
              K.xdiag[c].rearrange("h i (j k) -> (h i) j k", k=32)[:, 16:32, 16:32], sxb)
        P.dma("sync", K.arena[0:64, ANBB2:ANBB2 + 32],
              K.betab[c].rearrange("h (i j) -> (h i) j", i=4), sxb)
        P.wait("vector", sxb)
        OP("vector", lambda e: e.tensor_scalar_mul(
            K.arena[0:64, ANBB2:ANBB2 + 32], K.arena[0:64, ANBB2:ANBB2 + 32], -1.0), svc)
        for j in range(31):
            OP("vector", lambda e, j=j: e.tensor_scalar(
                K.arena[0:64, ALB + j * 32 + j + 1:ALB + (j + 1) * 32],
                K.arena[0:64, ALB + j * 32 + j + 1:ALB + (j + 1) * 32],
                K.arena[0:64, ANBB2 + j:ANBB2 + j + 1], None, ALU.mult), svc)
        OP("vector", lambda e: e.tensor_copy(
            K.arena[0:64, AXT:AXT + 1024], K.arena[0:64, AID32:AID32 + 1024]), svc)
        xv3 = K.arena[0:64, AXT:AXT + 1024].rearrange("p (j i) -> p j i", i=32)
        for i in range(1, 32):
            for j in range(i):
                OP("vector", lambda e, i=i, j=j: e.scalar_tensor_tensor(
                    xv3[:, :, i], xv3[:, :, j],
                    K.arena[0:64, ALB + j * 32 + i:ALB + j * 32 + i + 1],
                    xv3[:, :, i], ALU.mult, ALU.add))
        OP("vector", lambda e: e.tensor_copy(
            K.arena[0:64, AXT:AXT + 1], K.arena[0:64, AXT:AXT + 1]), svc)
        P.wait("sync", svc)
        P.dma("sync", K.xinv[c].rearrange("h i m -> (h i) m"),
              K.arena[0:64, AXT:AXT + 1024], sxv)
        P.wait("sync", sxv)
        P.dma("sync", K.A(AXTA, 512).rearrange("p (h i) -> p h i", i=32),
              K.xinv[c].rearrange("h i (j k) -> (i j) h k", k=32), sxl)
        xlmark = sxl.n

        P.wait("tensor", sxl, xlmark)
        for h in range(H_):
            if h == 0:
                P.wait("tensor", sgc)
            P.wait("tensor", svc)
            OP("tensor", lambda e, h=h: e.matmul(
                K.ps[3][:, 0:128], RR(RE_ + h * 128, 128), K.cb["ident"][:],
                start=True, stop=False), stc)
            OP("tensor", lambda e, h=h: e.matmul(
                K.ps[3][:, 0:128], RR(RAN_ + h * 128, 128), RR(RS_ + h * 128, 128),
                start=False, stop=True), stc)
            for I in range(4):
                if I > 0:
                    P.wait("tensor", svc)
                    OP("tensor", lambda e, h=h, I=I: e.matmul(
                        K.ps[4][32 * I:32 * I + 32, 0:128],
                        K.R[0:32 * I, RMT_ + h * 128 + 32 * I:RMT_ + h * 128 + 32 * I + 32],
                        K.arena[0:32 * I, AW_:AW_ + 128],
                        start=True, stop=True, tile_position=(0, 32 * I)), stc)
                    P.wait("vector", stc)
                    OP("vector", lambda e, I=I: e.tensor_copy(
                        K.arena[32 * I:32 * I + 32, ARP:ARP + 128],
                        K.ps[3][32 * I:32 * I + 32, 0:128]), svc)
                    OP("vector", lambda e, I=I: e.tensor_add(
                        K.arena[32 * I:32 * I + 32, ARP:ARP + 128],
                        K.arena[32 * I:32 * I + 32, ARP:ARP + 128],
                        K.ps[4][32 * I:32 * I + 32, 0:128]), svc)
                else:
                    P.wait("vector", stc)
                    OP("vector", lambda e: e.tensor_copy(
                        K.arena[0:32, ARP:ARP + 128], K.ps[3][0:32, 0:128]), svc)
                P.wait("tensor", svc)
                OP("tensor", lambda e, h=h, I=I: e.matmul(
                    K.ps[5][32 * I:32 * I + 32, 0:128],
                    K.arena[32 * I:32 * I + 32,
                            AXTA + h * 32:AXTA + (h + 1) * 32],
                    K.arena[32 * I:32 * I + 32, ARP:ARP + 128],
                    start=True, stop=True, tile_position=(32 * I, 32 * I)), stc)
                P.wait("vector", stc)
                OP("vector", lambda e, I=I: e.tensor_copy(
                    K.arena[32 * I:32 * I + 32, AW_:AW_ + 128],
                    K.ps[5][32 * I:32 * I + 32, 0:128]), svc)
            P.wait("tensor", svc)
            OP("tensor", lambda e, h=h: e.matmul(
                K.ps[6][:, 0:128], RR(RAQ_ + h * 128, 128), RR(RS_ + h * 128, 128),
                start=True, stop=False), stc)
            OP("tensor", lambda e, h=h: e.matmul(
                K.ps[6][:, 0:128], RR(RMQ_ + h * 128, 128), K.A(AW_, 128),
                start=False, stop=True), stc)
            P.wait("scalar", stc)
            OP("scalar", lambda e: e.copy(K.A(AO_, 128), K.ps[6][:, 0:128]), ssc)
            if h < 4:
                slot, scol = 0, h * 128
            elif h < 8:
                slot, scol = 1, (h - 4) * 128
            elif h < 12:
                slot, scol = 2, (h - 8) * 128
            elif h < 15:
                slot, scol = 3, 128 + (h - 12) * 128
            else:
                slot, scol = 7, 384
            P.wait("tensor", ssc)
            OP("tensor", lambda e, h=h, slot=slot, scol=scol: e.matmul(
                K.ps[slot][0:32, scol:scol + 128],
                K.scl[:, 960 + h * 32:960 + (h + 1) * 32], K.A(AO_, 128),
                start=True, stop=True), stc)
            P.wait("tensor", svc)
            OP("tensor", lambda e, h=h: e.matmul(
                K.ps[7][:, 0:128], RR(RBB_ + h * 128, 128), K.cb["ident"][:],
                start=True, stop=True), stc)
            P.wait("vector", stc)
            OP("vector", lambda e, h=h: e.tensor_scalar(
                K.A(ABBAR, 128), K.ps[7][:, 0:128], K.A(ABB + h, 1), None, ALU.mult), svc)
            P.wait("tensor", svc)
            OP("tensor", lambda e: e.matmul(
                K.ps[7][:, 128:256], K.A(ABBAR, 128), K.A(AW_, 128),
                start=True, stop=True), stc)
            P.wait("vector", stc)
            OP("vector", lambda e, h=h: e.tensor_scalar(
                RR(RS_ + h * 128, 128), RR(RS_ + h * 128, 128), K.A(AELGL + h, 1),
                None, ALU.mult), svc)
            P.wait("vector", svc)
            OP("vector", lambda e, h=h: e.tensor_add(
                RR(RS_ + h * 128, 128), RR(RS_ + h * 128, 128), K.ps[7][:, 128:256]), svc)

        P.wait("scalar", stc)
        OP("scalar", lambda e: e.copy(RR(RAN_, 512, 0, 32), K.ps[0][0:32, 0:512]), ssc)
        OP("scalar", lambda e: e.copy(RR(RAN_ + 512, 512, 0, 32), K.ps[1][0:32, 0:512]), ssc)
        OP("scalar", lambda e: e.copy(RR(RAN_ + 1024, 512, 0, 32), K.ps[2][0:32, 0:512]), ssc)
        OP("scalar", lambda e: e.copy(RR(RAN_ + 1536, 384, 0, 32),
                                        K.ps[3][0:32, 128:512]), ssc)
        OP("scalar", lambda e: e.copy(RR(RAN_ + 1920, 128, 0, 32),
                                        K.ps[7][0:32, 384:512]), ssc)
        P.wait("scalar", ssc)
        OP("scalar", lambda e: e.activation(RR(RAQ_, 2048, 0, 32), RR(RAN_, 2048, 0, 32),
                                              AF.Square), ssc)
        P.wait("vector", ssc)
        for h in range(H_):
            OP("vector", lambda e, h=h: e.reduce_sum(
                K.arena[0:32, ATMP + h:ATMP + h + 1], RR(RAQ_ + h * 128, 128, 0, 32),
                mybir.AxisListType.X), svc)
        P.wait("vector", svc)
        OP("vector", lambda e: e.tensor_scalar(
            K.arena[0:32, ATMP + 16:ATMP + 32], K.arena[0:32, ATMP:ATMP + 16], 1.0 / 128, 1e-5,
            ALU.mult, ALU.add), svc)
        P.wait("scalar", svc)
        OP("scalar", lambda e: e.activation(K.arena[0:32, ATMP + 32:ATMP + 48],
                                              K.arena[0:32, ATMP + 16:ATMP + 32], AF.Ln), ssc)
        P.wait("scalar", ssc)
        OP("scalar", lambda e: e.activation(K.arena[0:32, ATMP + 48:ATMP + 64],
                                              K.arena[0:32, ATMP + 32:ATMP + 48], AF.Exp,
                                              scale=-0.5), ssc)
        P.wait("vector", ssc)
        OP("vector", lambda e: e.tensor_mul(K.arena[0:32, ATMP + 64:ATMP + 80],
                                              K.arena[0:32, ATMP + 48:ATMP + 64],
                                              K.arena[0:32, ATMP + 48:ATMP + 64]), svc)
        P.wait("vector", svc)
        OP("vector", lambda e: e.tensor_mul(K.arena[0:32, ATMP + 64:ATMP + 80],
                                              K.arena[0:32, ATMP + 64:ATMP + 80],
                                              K.arena[0:32, ATMP + 16:ATMP + 32]), svc)
        P.wait("vector", svc)
        OP("vector", lambda e: e.tensor_scalar(K.arena[0:32, ATMP + 64:ATMP + 80],
                                                 K.arena[0:32, ATMP + 64:ATMP + 80],
                                                 -0.5, 1.5, ALU.mult, ALU.add), svc)
        P.wait("vector", svc)
        OP("vector", lambda e: e.tensor_mul(K.arena[0:32, ATMP + 48:ATMP + 64],
                                              K.arena[0:32, ATMP + 48:ATMP + 64],
                                              K.arena[0:32, ATMP + 64:ATMP + 80]), svc)
        P.wait("vector", svc)
        for h in range(H_):
            OP("vector", lambda e, h=h: e.scalar_tensor_tensor(
                RR(RAN_ + h * 128, 128, 0, 32), RR(RAN_ + h * 128, 128, 0, 32),
                K.arena[0:32, ATMP + 48 + h:ATMP + 49 + h], K.scl[0:32, 784:912],
                ALU.mult, ALU.mult), svc)
        P.wait("gpsimd", svc)
        P.wait("gpsimd", sgt, gtmark)
        for h in range(H_):
            OP("gpsimd", lambda e, h=h: e.tensor_mul(
                RR(RAN_ + h * 128, 128, 0, 32), RR(RAN_ + h * 128, 128, 0, 32),
                RR(RCSQ_ + h * 128, 128, 0, 32)), sgc)
        P.wait("tensor", sgc)
        P.wait("tensor", ssc)
        for h in range(H_):
            OP("tensor", lambda e, h=h: e.matmul(
                K.ps[3][:, h * 32:(h + 1) * 32], RR(RAN_ + h * 128, 128, 0, 32),
                K.cb["ident"][0:32, 0:32], start=True, stop=True), stc)
        P.wait("scalar", stc)
        OP("scalar", lambda e: e.copy(K.A(AOBST, 512), K.ps[3][:, 0:512]), ssc)
        P.wait("sync", ssc)
        P.dma("sync", K.ob.rearrange("h p t -> p h t")[:, :, 32 * c:32 * c + 32],
              K.A(AOBST, 512).rearrange("p (h t) -> p h t", t=32), sob)
    P.barrier()


def emit_attnb_proj(K, parts=None):
    if parts is None:
        parts = BPROJ_PARTS
    emit_attnb_prep(K)
    P = K.P
    sbst = P.sem("d_bst")
    stb = P.sem("t_bproj")
    svb = P.sem("v_bproj")
    ssb = P.sem("s_bproj")
    sgb = P.sem("g_bproj")
    Y1 = K.A(12288, 2048)
    acc = K.A(14336, 2048)
    sq = K.A(16384, 2048)
    n2 = K.arena[0:1, 12288 + 6144:12288 + 8192]
    r0 = K.arena[0:1, 16384:18432][0:1, :]
    trow = K.arena[0:1, 14336:16384][0:1, :]
    n2 = K.arena[0:1, 18432:20480]
    r_recip = K.arena[0:1, 16384:18432]
    y_row = K.arena[0:1, 14336 + 0:14336 + 2048]
    y_row = K.arena[0:1, 12288:14336]
    t_row = K.arena[0:1, 12288:14336]

    P.wait("vector", P.sem("d_bprep"))

    def post_conv_norm(oc, conv_col0, dst, norm, qscale, chain_done):
        slot = oc % 2
        st = K.stage[:, slot * T:(slot + 1) * T]
        P.wait("vector", chain_done)
        P.wait("vector", P.sem("ev_done"))
        P.op("vector", lambda e: e.tensor_scalar(
            acc, st, K.scl[:, conv_col0 + 3:conv_col0 + 4], None, ALU.mult), svb)
        for sh in range(1, 4):
            P.wait("vector", svb)
            P.op("vector", lambda e, sh=sh: e.scalar_tensor_tensor(
                acc[:, sh:T], st[:, 0:T - sh],
                K.scl[:, conv_col0 + 3 - sh:conv_col0 + 4 - sh],
                acc[:, sh:T], ALU.mult, ALU.add), svb)
        P.wait("scalar", svb)
        P.op("scalar", lambda e: e.activation(acc, acc, AF.Silu), ssb)
        if norm:
            P.wait("scalar", ssb)
            P.op("scalar", lambda e: e.activation(sq, acc, AF.Square), ssb)
            P.wait("tensor", ssb)
            for qq in range(4):
                P.op("tensor", lambda e, qq=qq: e.matmul(
                    K.ps[4 + qq][0:1, :], K.cb["ones_col"][:],
                    sq[:, qq * 512:(qq + 1) * 512], start=True, stop=True),
                    stb if qq == 3 else None)
            P.wait("vector", stb)
            for qq in range(4):
                P.op("vector", lambda e, qq=qq: e.tensor_copy(
                    n2[:, qq * 512:(qq + 1) * 512], K.ps[4 + qq][0:1, :]), svb)
            P.wait("vector", svb)
            P.op("vector", lambda e: e.tensor_scalar_add(n2, n2, 1e-6), svb)
            P.wait("scalar", svb)
            P.op("scalar", lambda e: e.activation(r_recip, n2, AF.Ln), ssb)
            P.wait("scalar", ssb)
            P.op("scalar", lambda e: e.activation(y_row, r_recip, AF.Exp, scale=-0.5), ssb)
            P.wait("vector", ssb)
            P.op("vector", lambda e: e.tensor_mul(r_recip, y_row, y_row), svb)
            P.wait("vector", svb)
            P.op("vector", lambda e: e.tensor_mul(r_recip, r_recip, n2), svb)
            P.wait("vector", svb)
            P.op("vector", lambda e: e.tensor_scalar(r_recip, r_recip, -0.5, 1.5,
                                                     ALU.mult, ALU.add), svb)
            P.wait("vector", svb)
            P.op("vector", lambda e: e.tensor_mul(y_row, y_row, r_recip), svb)
            if qscale is not None:
                P.wait("vector", svb)
                P.op("vector", lambda e: e.tensor_scalar_mul(y_row, y_row, qscale), svb)
            P.wait("tensor", svb)
            for qq in range(4):
                P.op("tensor", lambda e, qq=qq: e.matmul(
                    K.ps[4 + qq][:, 0:512], K.cb["ones_row"][0:1, :],
                    y_row[:, qq * 512:(qq + 1) * 512], start=True, stop=True),
                    stb if qq == 3 else None)
            P.wait("vector", stb)
            for qq in range(4):
                P.op("vector", lambda e, qq=qq: e.tensor_mul(
                    acc[:, qq * 512:(qq + 1) * 512], acc[:, qq * 512:(qq + 1) * 512],
                    K.ps[4 + qq][:, 0:512]), svb if qq == 3 else None)
            P.wait("sync", svb)
        else:
            P.wait("sync", ssb)
        P.dma("sync", dst, acc, sbst)

    def mk_evac(kind):
        def evac(oc, tt, bank, mmw, done):
            slot = oc % 2
            st = K.stage[:, slot * T + tt * 512: slot * T + (tt + 1) * 512]
            if tt == 0:
                P.wait("vector", sbst)
            P.wait("vector", *mmw)
            P.op("vector", lambda e, st=st, bank=bank: e.tensor_copy(st, bank[:]), done)
            if tt == 3:
                if kind == "k":
                    post_conv_norm(oc, 272 + oc * 4, K.kb[oc // 4, oc % 4], True, None, sbst)
                elif kind == "v":
                    post_conv_norm(oc, 528 + oc * 4, K.vb[oc // 4, oc % 4], False, None, sbst)
                elif kind == "q":
                    post_conv_norm(oc, 208 + oc * 4, K.qb[oc], True, float(DH) ** -0.5, sbst)
        return evac

    if "kvq" in parts:
        emit_proj(K, K.win["L0_attn_b_k_proj_w"], 64, mk_evac("k"))
        emit_proj(K, K.win["L0_attn_b_v_proj_w"], 64, mk_evac("v"))
        emit_proj(K, K.win["L0_attn_b_q_proj_w"], 16, mk_evac("q"))
    P.barrier()

    def f1_evac(oc, tt, bank, mmw, done):
        P.wait("vector", *mmw)
        P.op("vector", lambda e, tt=tt, bank=bank: e.tensor_copy(
            Y1[:, tt * 512:(tt + 1) * 512], bank[:]), done)
    if "g" in parts:
        emit_proj(K, K.win["L0_attn_b_f1_w"], 1, f1_evac)
    P.barrier()

    def f2_evac(oc, tt, bank, mmw, done):
        slot = oc % 2
        st = K.stage[:, slot * T + tt * 512: slot * T + (tt + 1) * 512]
        if tt == 0:
            P.wait("vector", sbst)
        P.wait("vector", *mmw)
        P.op("vector", lambda e, st=st, bank=bank, oc=oc: e.tensor_scalar(
            st, bank[:], K.scl[:, 128 + oc:129 + oc], None, ALU.add), done)
        if tt == 3:
            stg = K.stage[:, slot * T:(slot + 1) * T]
            P.wait("scalar", sbst)
            P.wait("scalar", P.sem("ev_done"))
            P.op("scalar", lambda e, stg=stg: e.activation(sq, stg, AF.Abs), ssb)
            P.wait("scalar", ssb)
            P.op("scalar", lambda e: e.activation(sq, sq, AF.Exp, scale=-1.0), ssb)
            P.wait("scalar", ssb)
            P.op("scalar", lambda e: e.activation(sq, sq, AF.Ln, bias=1.0), ssb)
            P.wait("vector", ssb)
            P.op("vector", lambda e, stg=stg: e.tensor_scalar_max(acc, stg, 0.0), svb)
            P.wait("vector", svb)
            P.op("vector", lambda e: e.tensor_add(acc, acc, sq), svb)
            P.wait("vector", svb)
            P.op("vector", lambda e, oc=oc: e.tensor_scalar(
                acc, acc, K.scl[:, 160 + oc:161 + oc], None, ALU.mult), svb)
            P.wait("sync", svb)
            P.dma("sync", K.gb[oc], acc, sbst)
    if "g" in parts:
        emit_proj(K, K.win["L0_attn_b_f2_w"], 16, f2_evac, din=128,
                  rhs_fn=lambda kc, tsl: Y1[:, tsl], bf16=False)
    P.barrier()

    if "gate" not in parts and "beta" not in parts:
        return
    if "gate" in parts:
        emit_proj(K, K.win["L0_attn_b_g1_w"], 1, f1_evac)
    P.barrier()
    sld2 = P.sem("d_bld")
    if "gate" not in parts:
        emit_beta_part(K, P, sld2, sbst, stb, svb, ssb, Y1)
        P.barrier()
        return
    P.dma("sync", K.arena[0:1, 16384:18432], K.win["L0_attn_b_g2_b"].unsqueeze(0), sld2)
    P.dma("sync", K.wnat[:, 0:2048],
          K.win["L0_attn_b_g2_w"].rearrange("(c p) n -> p c n", p=128), sld2)
    ld2 = sld2.n
    P.wait("tensor", sld2, ld2)
    for qq in range(4):
        P.op("tensor", lambda e, qq=qq: e.matmul(
            K.ps[4][:, 0:512], K.cb["ones_row"][0:1, :],
            K.arena[0:1, 16384 + qq * 512:16384 + (qq + 1) * 512],
            start=True, stop=True), stb)
        P.wait("vector", stb)
        P.op("vector", lambda e, qq=qq: e.tensor_copy(
            acc[:, qq * 512:(qq + 1) * 512], K.ps[4][:, 0:512]), svb)
        P.wait("tensor", svb)
    for c in range(16):
        bank = K.ps[2 + c % 2]
        P.op("tensor", lambda e, c=c, bank=bank: e.matmul(
            bank[:, 0:128], K.wnat[:, c * 128:(c + 1) * 128], K.cb["ident"][:],
            start=True, stop=True), stb)
        P.wait("vector", stb)
        P.op("vector", lambda e, c=c, bank=bank: e.tensor_copy(
            K.wT[:, c * 128:(c + 1) * 128], bank[:, 0:128]), svb)
        P.wait("tensor", svb)
    svb0 = svb.n
    ssb0 = ssb.n
    for tt in range(16):
        slot = tt % 2
        for oc in range(16):
            gi = tt * 16 + oc
            bank = K.ps[gi % 2]
            if gi >= 2:
                P.wait("tensor", svb, svb0 + gi - 1)
            P.op("tensor", lambda e, tt=tt, oc=oc, bank=bank: e.matmul(
                bank[:, 0:128], Y1[:, tt * 128:(tt + 1) * 128],
                K.wT[:, oc * 128:(oc + 1) * 128], start=True, stop=True), stb)
            st = K.stage[:, slot * 2048 + oc * 128: slot * 2048 + (oc + 1) * 128]
            if oc == 0 and tt >= 2:
                P.wait("vector", sbst)
            P.wait("vector", stb)
            P.op("vector", lambda e, st=st, bank=bank, oc=oc: e.tensor_add(
                st, bank[:, 0:128], acc[:, oc * 128:(oc + 1) * 128]), svb)
            P.wait("scalar", svb, svb0 + gi + 1)
            P.op("scalar", lambda e, st=st: e.activation(st, st, AF.Sigmoid), ssb)
        P.wait("sync", ssb, ssb0 + 16 * (tt + 1))
        P.dma("sync", K.gateb[tt], K.stage[:, slot * 2048:(slot + 1) * 2048], sbst)
    P.barrier()

    if "beta" in parts:
        emit_beta_part(K, P, sld2, sbst, stb, svb, ssb, Y1)
    P.barrier()


def emit_beta_part(K, P, sld2, sbst, stb, svb, ssb, Y1):
    P.dma("sync", K.wnat[0:64, 0:2048], K.win["L0_attn_b_b_proj_w"], sld2)
    for cc4 in range(4):
        P.dma("sync", Y1[32 * cc4:32 * cc4 + 32, 0:512],
              K.cin["expand"].rearrange("j t m -> t j m"), sld2)
    ld3 = sld2.n
    P.wait("tensor", sld2, ld3)
    for kc in range(16):
        bank = K.ps[2 + kc % 2]
        P.op("tensor", lambda e, kc=kc, bank=bank: e.matmul(
            bank[:, 0:64], K.wnat[0:64, kc * 128:(kc + 1) * 128],
            K.cb["ident"][0:64, 0:64], start=True, stop=True), stb)
        P.wait("vector", stb)
        P.op("vector", lambda e, kc=kc, bank=bank: e.tensor_copy(
            K.arenab[:, 8192 + kc * 64:8192 + (kc + 1) * 64], bank[:, 0:64]), svb)
        P.wait("tensor", svb)
    btok = K.A(16384, 64)
    svb0 = svb.n
    for tt in range(16):
        bank = K.ps[tt % 2]
        for kc in range(16):
            P.op("tensor", lambda e, tt=tt, kc=kc, bank=bank: e.matmul(
                bank[:, 0:64], K.Rbc(kc)[:, tt * 128:(tt + 1) * 128],
                K.arenab[:, 8192 + kc * 64:8192 + (kc + 1) * 64],
                start=(kc == 0), stop=(kc == 15)),
                stb if kc == 15 else None)
        P.wait("scalar", stb)
        P.op("scalar", lambda e, bank=bank: e.activation(btok, bank[:, 0:64], AF.Sigmoid), ssb)
        P.wait("tensor", ssb)
        for cc in range(4):
            c = tt * 4 + cc
            bank2 = K.ps[4 + cc % 2]
            if c >= 2:
                P.wait("tensor", svb, svb0 + c - 1)
            for j in range(4):
                P.op("tensor", lambda e, cc=cc, j=j, bank2=bank2: e.matmul(
                    bank2[0:16, 0:128],
                    btok[32 * cc:32 * cc + 32, :].rearrange("t (h j) -> t j h", j=4)[:, j, :],
                    Y1[32 * cc:32 * cc + 32, j * 128:(j + 1) * 128],
                    start=(j == 0), stop=(j == 3),
                    tile_position=(32 * cc, 0)),
                    stb if j == 3 else None)
            P.wait("vector", stb)
            st2 = K.stage[:, 2048 + cc * 128:2048 + (cc + 1) * 128]
            P.wait("vector", sbst)
            P.op("vector", lambda e, bank2=bank2, st2=st2: e.tensor_copy(
                st2[0:16, :], bank2[0:16, 0:128]), svb)
            P.wait("sync", svb)
            P.dma("sync", K.betab[c], st2[0:16, :], sbst)
    P.barrier()


PHASES = ["const", "x", "ln1", "aproj", "rope", "acore", "aout",
          "ln2", "ffn1", "ln3", "bproj", "bcore", "bout", "ln4", "ffn2", "store"]


def build(dumps=(), phases=None):
    K = MK(dumps=dumps)
    ph = PHASES if phases is None else phases
    emit_load_const(K)
    if "x" in ph:
        emit_load_x(K)
    K.P.barrier()
    if "ln1" in ph:
        emit_ln(K)
    if "aproj" in ph:
        emit_attna_proj(K)
    if "rope" in ph:
        emit_rope(K)
    if "acore" in ph:
        emit_attna_core(K)
    if "aout" in ph:
        emit_out_proj(K, "L0_attn_a_wo", K.oa, 16)
    if "ln2" in ph:
        emit_ln(K)
    if "ffn1" in ph:
        emit_ffn(K)
    if "ln3" in ph:
        emit_ln(K)
    if "bproj" in ph:
        emit_attnb_proj(K)
    if "bcore" in ph:
        emit_attnb_core(K)
    if "bout" in ph:
        emit_out_proj(K, "L0_attn_b_o_proj_w", K.ob, 16)
    if "ln4" in ph:
        emit_ln(K)
    if "ffn2" in ph:
        emit_ffn(K)
    if "store" in ph:
        emit_final_store(K)
    if dumps:
        emit_dumps(K)
    K.P.build()
    return K


def kernel(**inputs):
    K = build()
    consts = K.consts
    in_maps = []
    x = inputs["L0_attn_a_x"]
    for b in range(NCORES):
        m = {"x": np.ascontiguousarray(x[b])}
        for n, _ in WEIGHT_SPECS:
            m[n] = np.ascontiguousarray(inputs[n].reshape(
                dict(WEIGHT_SPECS)[n]))
        for n, _ in CONST_SPECS:
            m["ci_" + n] = consts[n]
        in_maps.append(m)
    res = run_bass_kernel_spmd(K.nc, in_maps, core_ids=list(range(NCORES)))
    return np.stack([r["out"] for r in res.results], axis=0)
